# version 41
# baseline (speedup 1.0000x reference)
import numpy as np
from contextlib import ExitStack

import concourse.bass as bass
import concourse.mybir as mybir
from concourse.bass_utils import run_bass_kernel_spmd

F32 = mybir.dt.float32
BF16 = mybir.dt.bfloat16
I32 = mybir.dt.int32
F8 = mybir.dt.float8e4
AF = mybir.ActivationFunctionType
ALU = mybir.AluOpType
AX = mybir.AxisListType

S_SEQ = 4096
D = 1024
NRT = S_SEQ // 128

ENGS = ("tensor", "vector", "scalar", "sync", "gpsimd")


class Sched:

    def __init__(self, nc, serial=False):
        self.nc = nc
        self.serial = serial
        self.comp_sem = {e: "c_" + e for e in ENGS}
        self.dma_sem = {e: "d_" + e for e in ENGS}
        self.cnt = {}
        self.q = {e: [] for e in ENGS}
        self.res = {}
        self.barrier_toks = []

    def _bump(self, sem, inc):
        self.cnt[sem] = self.cnt.get(sem, 0) + inc
        return self.cnt[sem]

    def _emit(self, eng, fn, reads, writes, sem, inc, is_dma):
        toks = []

        def need(tok):
            if tok is None:
                return
            if tok not in toks:
                toks.append(tok)

        for k in reads:
            d = self.res.get(k)
            if d:
                need(d[0])
        for k in writes:
            d = self.res.get(k)
            if d:
                need(d[0])
                for r in d[1]:
                    need(r)
        v = self._bump(sem, inc)
        qidx = len(self.q[eng])
        tok = (eng, qidx, sem, v, is_dma)
        self.q[eng].append([toks, fn, sem, inc])
        for k in writes:
            self.res[k] = [tok, []]
        for k in reads:
            d = self.res.setdefault(k, [None, []])
            d[1].append(tok)

    def op(self, eng, fn, reads=(), writes=()):
        if self.serial:
            reads = tuple(reads) + ("__SER__",)
            writes = tuple(writes) + ("__SER__",)
        self._emit(eng, fn, reads, writes, self.comp_sem[eng], 1, False)

    def dma(self, queue, fn, reads=(), writes=()):
        if self.serial:
            reads = tuple(reads) + ("__SER__",)
            writes = tuple(writes) + ("__SER__",)
        key = writes[0] if writes else "r"
        sem = "dq_%s_%s" % (queue, key)
        self._emit(queue, fn, reads, writes, sem, 16, True)

    def pe_fence(self, key):
        pass

    def barrier_on(self, eng, key):
        self.op(eng, lambda e: e.nop(), reads=(key,))

    def stage_barrier(self):
        for e in ENGS:
            self._emit(e, lambda e2: e2.drain(), ("__BARRIER__",),
                       ("__bar_%s" % e,), self.comp_sem[e], 1, False)
        toks = [self.res["__bar_%s" % e][0] for e in ENGS]
        dma_toks = [("dma", -1, s, v, True) for s, v in self.cnt.items()
                    if s.startswith("dq_") and v > 0]
        for e in ENGS:
            waits = [t for t in toks if t[0] != e] + dma_toks
            v = self._bump(self.comp_sem[e], 1)
            self.q[e].append([waits, lambda e2: e2.nop(), self.comp_sem[e], 1])
            self.res["__bar2_%s" % e] = [(e, len(self.q[e]) - 1,
                                          self.comp_sem[e], v, False), []]

    use_drains = False

    def finalize(self, ctx, block):
        marked = {e: set() for e in ENGS}
        if self.use_drains:
            for eng in ENGS:
                for entry in self.q[eng]:
                    for tok in entry[0]:
                        peng, qidx, sem, v, is_dma = tok
                        if (not is_dma) and peng != eng and qidx >= 0:
                            marked[peng].add(qidx)
        dnum = {}
        for e in ENGS:
            cnt = 0
            for qidx in range(len(self.q[e])):
                if qidx in marked[e]:
                    cnt += 1
                dnum[(e, qidx)] = cnt
        dr_sem = {e: "dr_" + e for e in ENGS}
        sems = {}
        names = set(self.cnt)
        for e in ENGS:
            if marked[e]:
                names.add(dr_sem[e])
        for name in sorted(names):
            sems[name] = ctx.enter_context(self.nc.semaphore(name))

        def make_body(engname):
            def body(eng):
                waited = {}
                for qidx, (toks, fn, sem, inc) in enumerate(self.q[engname]):
                    for tok in toks:
                        peng, pqidx, psem, pv, is_dma = tok
                        if (not is_dma) and peng != engname and pqidx >= 0 \
                                and pqidx in marked[peng]:
                            s, v = dr_sem[peng], dnum[(peng, pqidx)]
                        elif (not is_dma) and peng == engname:
                            if engname == "tensor":
                                continue
                            s, v = psem, pv
                        else:
                            s, v = psem, pv
                        if waited.get(s, -1) < v:
                            eng.wait_ge(sems[s], v)
                            waited[s] = v
                    fn(eng).then_inc(sems[sem], inc)
                    if qidx in marked[engname]:
                        eng.drain().then_inc(sems[dr_sem[engname]], 1)
            return body

        for e in ENGS:
            if self.q[e]:
                getattr(block, e)(make_body(e))



def emit_ln(S, stat, xt_key, xt_ap, out_ap, out_key, out_extra=(), par=0):
    bn6a, mva, sda, rsa, nb, eps = stat
    bn6 = bn6a[:, par * 12:(par + 1) * 12]
    mv = mva[:, par * 2:(par + 1) * 2]
    sd = sda[:, par:par + 1]
    rs = rsa[:, par:par + 1]
    sx = "%d" % par
    S.op("vector", (lambda a=bn6, b=xt_ap: lambda e: e.bn_stats(
        out=a[:, 0:6], in_=b[:, 0:512]))(),
        reads=(xt_key,), writes=("ln_bn6" + sx,))
    S.op("vector", (lambda a=bn6, b=xt_ap: lambda e: e.bn_stats(
        out=a[:, 6:12], in_=b[:, 512:1024]))(),
        reads=(xt_key,), writes=("ln_bn6b" + sx,))
    S.op("vector", (lambda a=mv, b=bn6: lambda e: e.bn_aggr(out=a, in_=b))(),
         reads=("ln_bn6" + sx, "ln_bn6b" + sx), writes=("ln_mv" + sx,))
    S.op("scalar", (lambda a=sd, b=mv: lambda e: e.activation(
        out=a, in_=b[:, 1:2], func=AF.Sqrt, bias=eps, scale=1.0))(),
        reads=("ln_mv" + sx,), writes=("ln_sd" + sx,))
    S.op("vector", (lambda a=rs, b=sd: lambda e: e.reciprocal(out=a, in_=b))(),
         reads=("ln_sd" + sx,), writes=("ln_rs" + sx,))
    S.op("vector", (lambda a=out_ap, b=xt_ap, c=mv, d=rs: lambda e: e.tensor_scalar(
        out=a, in0=b, scalar1=c[:, 0:1], scalar2=d, op0=ALU.subtract, op1=ALU.mult))(),
        reads=(xt_key, "ln_mv" + sx, "ln_rs" + sx), writes=(out_key,) + tuple(out_extra))


def emit_transpose_tile(S, in_ap, in_keys, ps_ap, ps_key, out_ap, out_key, ident,
                        nblk=8, evac_eng="vector"):
    for k in range(nblk):
        S.op("tensor",
             (lambda kk: lambda e: e.transpose(
                 ps_ap[:, kk * 128:(kk + 1) * 128],
                 in_ap[:, kk * 128:(kk + 1) * 128], ident))(k),
             reads=tuple(in_keys) + ("ident",), writes=(ps_key,))
    S.pe_fence(ps_key)
    S.op(evac_eng, (lambda a=out_ap, b=ps_ap: lambda e: e.tensor_copy(out=a, in_=b))(),
         reads=(ps_key,), writes=(out_key,))


def ln_stat_bufs(sb):
    return (sb("ln_bn6", [128, 24], F32), sb("ln_mv", [128, 4], F32),
            sb("ln_sd", [128, 2], F32), sb("ln_rs", [128, 2], F32),
            sb("ln_nb", [128, 1], F32), sb("ln_eps", [128, 1], F32))



def emit_l1ffn_prep(S, sb, psb, wd, w8Td, cst):
    ident = cst["ident"]
    wstage = sb("wstage", [128, 2 * D], F32)
    wclip = sb("wclip", [128, D], F32)
    w8 = sb("w8", [128, D], F8)
    w8b = sb("w8b", [128, D], BF16)
    w8T = sb("w8T", [128, 8 * D], BF16)
    ps_tr = psb("ps_tr", [128, D], BF16)
    for j in range(8):
        p = j % 2
        ws = wstage[:, p * D:(p + 1) * D]
        S.dma("sync", (lambda jj, ws=ws: lambda e: e.dma_start(
            out=ws, in_=wd[jj * 128:(jj + 1) * 128, :]))(j),
            writes=("wstage%d" % p,))
        S.op("vector", (lambda ws=ws: lambda e: e.tensor_scalar(
            out=wclip, in0=ws, scalar1=240.0, scalar2=-240.0,
            op0=ALU.min, op1=ALU.max))(),
            reads=("wstage%d" % p,), writes=("wclip",))
        S.op("vector", lambda e: e.tensor_copy(out=w8, in_=wclip),
             reads=("wclip",), writes=("w8",))
        S.op("vector", lambda e: e.tensor_copy(out=w8b, in_=w8),
             reads=("w8",), writes=("w8b",))
        for k in range(8):
            S.op("tensor", (lambda kk: lambda e: e.transpose(
                ps_tr[:, kk * 128:(kk + 1) * 128],
                w8b[:, kk * 128:(kk + 1) * 128], ident))(k),
                reads=("w8b", "ident"), writes=("ps_tr0",))
        S.pe_fence("ps_tr0")
        S.op("vector", (lambda jj: lambda e: e.tensor_copy(
            out=w8T.rearrange("p (k c) -> p k c", k=8)[:, :, jj * 128:(jj + 1) * 128],
            in_=ps_tr[:, 0:D].rearrange("p (k c) -> p k c", k=8)))(j),
            reads=("ps_tr0",), writes=("w8T",))
    S.dma("sync", lambda e: e.dma_start(out=w8Td, in_=w8T),
          reads=("w8T",), writes=("w8Td",))


def emit_l1ffn_main(S, sb, psb, xd, biasd, w8Td, yd, cst):
    ident = cst["ident"]
    ones_row = cst["ones_row"]
    w8T = sb("w8T", [128, 8 * D], BF16)
    brep05 = sb("brep05", [128, D], F32)
    bias_row = sb("bias_row", [1, D], F32)
    xt = sb("xt", [128, 6 * D], F32)
    xln = sb("xln", [128, 6 * D], F32)
    xq8 = sb("xq8", [128, 6 * D], F8)
    xqb = sb("xqb", [128, 6 * D], BF16)
    xqT = sb("xqT", [128, 6 * D], BF16)
    yt = sb("yt", [128, 6 * D], F32)
    ubuf = sb("ubuf", [128, 6 * D], F32)
    stat2 = (sb("ln2_bn6", [128, 3 * 24], F32), sb("ln2_mv", [128, 3 * 4], F32),
             sb("ln2_sd", [128, 3 * 2], F32), sb("ln2_rs", [128, 3 * 2], F32),
             sb("ln2_eps", [128, 1], F32))
    ps_tr = psb("ps_tr", [128, 2 * D], BF16)
    ps_mm = psb("ps_mm", [128, 2 * 512], F32)
    ps_bc = psb("ps_bc", [128, 512], F32)

    S.op("vector", (lambda a=stat2[4]: lambda e: e.memset(a, 1e-5))(), writes=("ln_eps",))
    S.dma("sync", lambda e: e.dma_start(out=w8T, in_=w8Td),
          reads=("w8Td",), writes=("w8T",))
    S.dma("sync", lambda e: e.dma_start(
        out=bias_row, in_=biasd.rearrange("(one d) -> one d", one=1)),
        writes=("bias_row",))
    for half in range(2):
        S.op("tensor", (lambda hh: lambda e: e.matmul(
            ps_bc, ones_row, bias_row[:, hh * 512:(hh + 1) * 512],
            start=True, stop=True))(half),
            reads=("ones", "bias_row", "brep05"), writes=("ps_bc",))
        S.pe_fence("ps_bc")
        S.op("vector", (lambda hh: lambda e: e.tensor_scalar(
            out=brep05[:, hh * 512:(hh + 1) * 512], in0=ps_bc,
            scalar1=0.5, scalar2=0.0, op0=ALU.mult, op1=ALU.add))(half),
            reads=("ps_bc",), writes=("brep05",))

    NP2 = NRT // 2

    def quantc(pr):
        p = pr % 3
        bn6, mv, sd, rs, eps = (stat2[0][:, p * 24:(p + 1) * 24],
                                stat2[1][:, p * 4:(p + 1) * 4],
                                stat2[2][:, p * 2:(p + 1) * 2],
                                stat2[3][:, p * 2:(p + 1) * 2], stat2[4])
        xtp = xt[:, p * 2 * D:(p + 1) * 2 * D]
        xlnp = xln[:, p * 2 * D:(p + 1) * 2 * D]
        xq8p = xq8[:, p * 2 * D:(p + 1) * 2 * D]
        xqbp = xqb[:, p * 2 * D:(p + 1) * 2 * D]
        kx = "xt%d" % p
        for s_ in range(2):
            S.dma("sync", (lambda rr=2 * pr + s_, s_=s_, xtp=xtp: lambda e: e.dma_start(
                out=xtp[:, s_ * D:(s_ + 1) * D],
                in_=xd[rr * 128:(rr + 1) * 128, :]))(),
                writes=(kx,) if s_ == 1 else (kx + "a",))
        for s_ in range(2):
            for g in range(2):
                S.op("vector", (lambda a=bn6, b=xtp, s_=s_, g=g: lambda e: e.bn_stats(
                    out=a[:, s_ * 12 + g * 6: s_ * 12 + (g + 1) * 6],
                    in_=b[:, s_ * D + g * 512: s_ * D + (g + 1) * 512]))(),
                    reads=(kx, kx + "a"), writes=("ln_bn6_%d_%d" % (p, s_),))
            S.op("vector", (lambda a=mv, b=bn6, s_=s_: lambda e: e.bn_aggr(
                out=a[:, s_ * 2:(s_ + 1) * 2], in_=b[:, s_ * 12:(s_ + 1) * 12]))(),
                reads=("ln_bn6_%d_%d" % (p, s_),), writes=("ln_mv%d" % p,))
        S.op("scalar", (lambda a=sd, b=mv: lambda e: e.activation(
            out=a, in_=b.rearrange("p (s two) -> p s two", two=2)[:, :, 1],
            func=AF.Sqrt, bias=eps, scale=1.0))(),
            reads=("ln_mv%d" % p, "ln_eps"), writes=("ln_sd%d" % p,))
        S.op("vector", (lambda a=rs, b=sd: lambda e: e.reciprocal(out=a, in_=b))(),
             reads=("ln_sd%d" % p,), writes=("ln_rs%d" % p,))
        for s_ in range(2):
            S.op("vector", (lambda a=xlnp, b=xtp, c=mv, d=rs, s_=s_:
                            lambda e: e.tensor_scalar(
                out=a[:, s_ * D:(s_ + 1) * D], in0=b[:, s_ * D:(s_ + 1) * D],
                scalar1=c[:, s_ * 2: s_ * 2 + 1], scalar2=d[:, s_:s_ + 1],
                op0=ALU.subtract, op1=ALU.mult))(),
                reads=(kx, kx + "a", "ln_mv%d" % p, "ln_rs%d" % p),
                writes=("xln%d" % p,))
        S.op("vector", (lambda xq8p=xq8p, xlnp=xlnp: lambda e: e.tensor_copy(
            out=xq8p, in_=xlnp))(),
            reads=("xln%d" % p,), writes=("xq8%d" % p,))
        S.op("vector", (lambda xqbp=xqbp, xq8p=xq8p: lambda e: e.tensor_copy(
            out=xqbp, in_=xq8p))(),
            reads=("xq8%d" % p,), writes=("xqb%d" % p,))

    def consumec(pr):
        p = pr % 3
        xtp = xt[:, p * 2 * D:(p + 1) * 2 * D]
        xqbp = xqb[:, p * 2 * D:(p + 1) * 2 * D]
        xqTp = xqT[:, p * 2 * D:(p + 1) * 2 * D]
        ytp = yt[:, p * 2 * D:(p + 1) * 2 * D]
        ubp = ubuf[:, p * 2 * D:(p + 1) * 2 * D]
        for s_ in range(2):
            pstr = ps_tr[:, s_ * D:(s_ + 1) * D]
            for k in range(8):
                S.op("tensor", (lambda kk=k, s_=s_, pstr=pstr, xqbp=xqbp:
                                lambda e: e.transpose(
                    pstr[:, kk * 128:(kk + 1) * 128],
                    xqbp[:, s_ * D + kk * 128: s_ * D + (kk + 1) * 128], ident))(),
                    reads=("xqb%d" % p, "ident"), writes=("ps_trm%d" % s_,))
            S.op("scalar", (lambda s_=s_, pstr=pstr, xqTp=xqTp: lambda e: e.activation(
                out=xqTp[:, s_ * D:(s_ + 1) * D], in_=pstr,
                func=AF.Copy, scale=1.0))(),
                reads=("ps_trm%d" % s_,), writes=("xqT%d" % p,))
        for s_ in range(2):
            for oh in range(2):
                pm = (2 * s_ + oh) % 2
                psmm = ps_mm[:, pm * 512:(pm + 1) * 512]
                for k in range(8):
                    S.op("tensor", (lambda kk=k, ohh=oh, s_=s_, xqTp=xqTp, psmm=psmm:
                                    lambda e: e.matmul(
                        psmm, xqTp[:, s_ * D + kk * 128: s_ * D + (kk + 1) * 128],
                        w8T[:, kk * D + ohh * 512: kk * D + ohh * 512 + 512],
                        start=(kk == 0), stop=(kk == 7)))(),
                        reads=("xqT%d" % p, "w8T"), writes=("psmm%d" % pm,))
                S.op("vector", (lambda ohh=oh, s_=s_, psmm=psmm, ubp=ubp:
                                lambda e: e.scalar_tensor_tensor(
                    out=ubp[:, s_ * D + ohh * 512: s_ * D + (ohh + 1) * 512],
                    in0=psmm, scalar=0.5,
                    in1=brep05[:, ohh * 512:(ohh + 1) * 512],
                    op0=ALU.mult, op1=ALU.add))(),
                    reads=("psmm%d" % pm, "brep05"), writes=("ubuf%d" % p,))
        S.op("vector", (lambda ytp=ytp, xtp=xtp, ubp=ubp: lambda e: e.tensor_tensor(
            out=ytp, in0=ubp, in1=xtp, op=ALU.add))(),
            reads=("ubuf%d" % p, "xt%d" % p, "xt%da" % p), writes=("yt%d" % p,))
        for s_ in range(2):
            S.dma("scalar", (lambda rr=2 * pr + s_, s_=s_, ytp=ytp:
                             lambda e: e.dma_start(
                out=yd[rr * 128:(rr + 1) * 128, :],
                in_=ytp[:, s_ * D:(s_ + 1) * D]))(),
                reads=("yt%d" % p,), writes=("ydma",))

    quantc(0)
    quantc(1)
    for pr in range(NP2):
        consumec(pr)
        if pr + 2 < NP2:
            quantc(pr + 2)
def _mk_scoped(nc, ctx, prefix):
    sb = lambda n, s, d: ctx.enter_context(nc.sbuf_tensor(prefix + n, s, d)).ap()
    psb = lambda n, s, d: ctx.enter_context(nc.psum_tensor(prefix + n, s, d)).ap()
    return sb, psb


def _emit_consts(S, nc, octx, onesd, identd):
    cst = {}
    cst["ones_row"] = octx.enter_context(nc.sbuf_tensor("cst_ones", [1, 128], F32)).ap()
    cst["ident"] = octx.enter_context(nc.sbuf_tensor("cst_ident", [128, 128], BF16)).ap()
    cst["identf"] = octx.enter_context(nc.sbuf_tensor("cst_identf", [128, 128], F32)).ap()
    S.dma("sync", (lambda a=cst["ones_row"]: lambda e: e.dma_start(out=a, in_=onesd))(),
          writes=("ones",))
    S.dma("gpsimd", (lambda a=cst["ident"]: lambda e: e.dma_start(out=a, in_=identd))(),
          writes=("ident",))
    S.dma("sync", (lambda a=cst["identf"]: lambda e: e.dma_start(out=a, in_=identd))(),
          writes=("identf",))
    return cst


def build_l1ffn():
    nc = bass.Bass("TRN2", target_bir_lowering=False, debug=False)
    xd = nc.dram_tensor("x", [S_SEQ, D], F32, kind="ExternalInput").ap()
    wd = nc.dram_tensor("w", [D, D], F32, kind="ExternalInput").ap()
    biasd = nc.dram_tensor("bias", [D], F32, kind="ExternalInput").ap()
    onesd = nc.dram_tensor("ones_row", [1, 128], F32, kind="ExternalInput").ap()
    identd = nc.dram_tensor("ident", [128, 128], F32, kind="ExternalInput").ap()
    yd = nc.dram_tensor("y", [S_SEQ, D], F32, kind="ExternalOutput").ap()
    w8Td = nc.dram_tensor("w8T_scr", [128, 8 * D], BF16, kind="Internal").ap()
    with ExitStack() as octx:
        block = octx.enter_context(nc.Block())
        S = Sched(nc)
        cst = _emit_consts(S, nc, octx, onesd, identd)
        with ExitStack() as c0:
            sb, psb = _mk_scoped(nc, c0, "p_")
            emit_l1ffn_prep(S, sb, psb, wd, w8Td, cst)
        S.stage_barrier()
        with ExitStack() as c1:
            sb, psb = _mk_scoped(nc, c1, "m_")
            emit_l1ffn_main(S, sb, psb, xd, biasd, w8Td, yd, cst)
        S.barrier_on("vector", "ydma")
        S.finalize(octx, block)
    return nc


_CONSTS = None


def _consts():
    global _CONSTS
    if _CONSTS is None:
        _CONSTS = {
            "ones_row": np.ones((1, 128), dtype=np.float32),
            "ident": np.eye(128, dtype=np.float32),
        }
    return _CONSTS


def run_l1ffn(nc, x_full, weight, bias):
    c = _consts()
    in_maps = [{"x": np.ascontiguousarray(x_full[b]), "w": weight, "bias": bias,
                "ones_row": c["ones_row"], "ident": c["ident"]} for b in range(8)]
    res = run_bass_kernel_spmd(nc, in_maps, core_ids=list(range(8)))
    return np.stack([r["y"] for r in res.results], axis=0)





CHUNK_RT = 4
NCHUNK = NRT // CHUNK_RT
INNER = 4096


def emit_l0ffn_prep(S, sb, psb, wgd, wud, wdd, gTd, uTd, dTd, wsbd, cst):
    ident = cst["ident"]
    ones_row = cst["ones_row"]
    dTq = sb("dTq", [128, 32 * D], BF16)
    wsb = sb("wsb", [128, 3], F32)
    ones_col = sb("ones_col", [128, 1], F32)
    ps_c = psb("ps_c", [128, 512], F32)
    shared_ps_tr = psb("ps_tr_sh", [128, 2 * D], BF16)

    S.op("vector", lambda e: e.memset(ones_col, 1.0), writes=("ones_col",))

    def prep_weight(wd_ap, R, Cw, widx, tag):
        big = sb(tag + "big", [128, 8 * 1024], BF16) if tag != "down" else None
        wstage = sb(tag + "wstage", [128, 2 * D], F32)
        i32b2 = sb(tag + "i32b", [128, 2 * D], I32)
        ternb2 = sb(tag + "ternb", [128, 2 * D], BF16)
        acc = sb(tag + "acc", [128, 1], F32)
        colx = sb(tag + "colx", [128, 2], F32)
        cell = sb(tag + "cell", [1, 4], F32)
        sinvb = sb(tag + "sinvb", [128, 1], F32)
        ps_tr = shared_ps_tr
        nrt_w = R // 128
        ncc = Cw // D
        S.op("vector", lambda e: e.memset(acc, 0.0), writes=(tag + "acc",))
        wbstage = ternb2
        for t in range(nrt_w * ncc):
            j, cc = divmod(t, ncc)
            p = t % 2
            wb_ = wbstage[:, p * D:(p + 1) * D]
            S.dma("gpsimd", (lambda jj=j, ccc=cc, wb_=wb_: lambda e: e.dma_start(
                out=wb_, in_=wd_ap[jj * 128:(jj + 1) * 128, ccc * D:(ccc + 1) * D]))(),
                writes=(tag + "wbst%d" % p,))
            S.op("vector", (lambda wb_=wb_: lambda e: e.tensor_reduce(
                out=colx[:, 0:1], in_=wb_, axis=AX.X, op=ALU.add,
                apply_absolute_value=True))(),
                reads=(tag + "wbst%d" % p,), writes=(tag + "colx0",))
            S.op("vector", lambda e: e.tensor_tensor(
                out=acc, in0=acc, in1=colx[:, 0:1], op=ALU.add),
                reads=(tag + "colx0", tag + "acc"), writes=(tag + "acc",))
        S.op("tensor", lambda e: e.matmul(ps_c[0:1, 0:1], acc, ones_col,
                                          start=True, stop=True),
             reads=(tag + "acc", "ones_col"), writes=("ps_c",))
        S.pe_fence("ps_c")
        S.op("vector", lambda e: e.tensor_copy(out=cell[:, 0:1], in_=ps_c[0:1, 0:1]),
             reads=("ps_c",), writes=(tag + "cell",))
        S.op("vector", (lambda n=float(R * Cw): lambda e: e.tensor_scalar(
            out=cell[:, 1:2], in0=cell[:, 0:1], scalar1=1.0 / n, scalar2=1e-5,
            op0=ALU.mult, op1=ALU.max))(),
            reads=(tag + "cell",), writes=(tag + "cellws",))
        S.op("vector", lambda e: e.reciprocal(out=cell[:, 2:3], in_=cell[:, 1:2]),
             reads=(tag + "cellws",), writes=(tag + "cellsinv",))
        S.op("tensor", lambda e: e.matmul(ps_c[:, 0:1], ones_row, cell[0:1, 1:2],
                                          start=True, stop=True),
             reads=(tag + "cellws", "ones"), writes=("ps_c",))
        S.pe_fence("ps_c")
        S.op("vector", (lambda wi=widx: lambda e: e.tensor_copy(
            out=wsb[:, wi:wi + 1], in_=ps_c[:, 0:1]))(),
            reads=("ps_c",), writes=("wsb%d" % widx,))
        S.op("tensor", lambda e: e.matmul(ps_c[:, 0:1], ones_row, cell[0:1, 2:3],
                                          start=True, stop=True),
             reads=(tag + "cellsinv", "ones", "wsb%d" % widx), writes=("ps_c",))
        S.pe_fence("ps_c")
        S.op("vector", lambda e: e.tensor_copy(out=sinvb, in_=ps_c[:, 0:1]),
             reads=("ps_c",), writes=(tag + "sinvb",))
        if tag == "down":
            S.op("vector", (lambda wi=widx: lambda e: e.tensor_scalar(
                out=wsb[:, wi:wi + 1], in0=wsb[:, wi:wi + 1], scalar1=0.5,
                scalar2=0.0, op0=ALU.mult, op1=ALU.add))(),
                reads=("wsb%d" % widx,), writes=("wsb%d" % widx,))
        for t in range(nrt_w * ncc):
            j, cc = divmod(t, ncc)
            p = t % 2
            ws_ = wstage[:, p * D:(p + 1) * D]
            S.dma("sync" if t % 2 == 0 else "scalar",
                  (lambda jj=j, ccc=cc, ws_=ws_: lambda e: e.dma_start(
                out=ws_, in_=wd_ap[jj * 128:(jj + 1) * 128, ccc * D:(ccc + 1) * D]))(),
                writes=(tag + "wstage%d" % p,))
            i32b = i32b2[:, p * D:(p + 1) * D]
            ternb = ternb2[:, p * D:(p + 1) * D]
            S.op("vector", (lambda ws_=ws_, i32b=i32b: lambda e: e.tensor_scalar(
                out=i32b, in0=ws_, scalar1=sinvb, scalar2=0.0,
                op0=ALU.mult, op1=ALU.add))(),
                reads=(tag + "wstage%d" % p, tag + "sinvb"),
                writes=(tag + "i32w%d" % p,))
            S.op("vector", (lambda i32b=i32b: lambda e: e.tensor_scalar(
                out=i32b, in0=i32b, scalar1=1, scalar2=-1,
                op0=ALU.min, op1=ALU.max))(),
                reads=(tag + "i32w%d" % p,), writes=(tag + "i32w%d" % p,))
            S.op("vector", (lambda i32b=i32b, ternb=ternb: lambda e: e.tensor_copy(
                out=ternb, in_=i32b))(),
                reads=(tag + "i32w%d" % p,), writes=(tag + "ternb%d" % p,))
            pp = t % 2
            pstr = ps_tr[:, pp * D:(pp + 1) * D]
            for k in range(8):
                S.op("tensor", (lambda kk=k, pstr=pstr, ternb=ternb:
                                lambda e: e.transpose(
                    pstr[:, kk * 128:(kk + 1) * 128],
                    ternb[:, kk * 128:(kk + 1) * 128], ident))(),
                    reads=(tag + "ternb%d" % p, "ident"), writes=("ppstr%d" % pp,))
            S.pe_fence("ppstr%d" % pp)
            if tag == "down":
                S.op("vector", (lambda jj=j, ccc=cc, pstr=pstr: lambda e: e.tensor_copy(
                    out=dTq.rearrange("p (K r) -> p K r", K=32)
                        [:, ccc * 8:(ccc + 1) * 8, jj * 128:(jj + 1) * 128],
                    in_=pstr.rearrange("p (K c) -> p K c", K=8)))(),
                    reads=("ppstr%d" % pp,), writes=("dTq",))
            else:
                jloc = j % 8
                obq, rr0 = jloc // 4, (jloc % 4) * 128
                S.op("vector", (lambda obq=obq, rr0=rr0, pstr=pstr:
                                lambda e: e.tensor_copy(
                    out=big.rearrange("p (o k r) -> p o k r", o=2, k=8)
                        [:, obq, :, rr0:rr0 + 128],
                    in_=pstr.rearrange("p (K c) -> p K c", K=8)))(),
                    reads=("ppstr%d" % pp,), writes=(tag + "bigprep",))
                if jloc == 7:
                    hh = j // 8
                    dst = gTd if tag == "g" else uTd
                    S.dma("sync", (lambda dst=dst, hh=hh: lambda e: e.dma_start(
                        out=dst[:, hh * 8192:(hh + 1) * 8192],
                        in_=big))(),
                        reads=(tag + "bigprep",), writes=(tag + "Td", tag + "bigprep"))

    prep_weight(wgd, INNER, D, 0, "g")
    prep_weight(wud, INNER, D, 1, "u")
    prep_weight(wdd, D, INNER, 2, "down")
    S.dma("sync", lambda e: e.dma_start(out=dTd, in_=dTq),
          reads=("dTq",), writes=("dTd",))
    S.dma("sync", lambda e: e.dma_start(out=wsbd, in_=wsb),
          reads=("wsb0", "wsb1", "wsb2"), writes=("wsbd",))


def emit_l0ffn_main(S, sb, psb, xd, yd, gTd, uTd, dTd, wsbd, cst):
    ident = cst["ident"]
    big = sb("big", [128, CHUNK_RT * INNER], BF16)
    dTq = sb("dTq", [128, 32 * D], BF16)
    gsl = sb("gsl", [128, 2 * 8 * 512], BF16)
    usl = sb("usl", [128, 2 * 8 * 512], BF16)
    xqTc = sb("xqTc", [128, 2 * CHUNK_RT * D], BF16)
    i8x = sb("i8x", [128, D], mybir.dt.int8)
    i8g = sb("i8g", [128, 2 * INNER], mybir.dt.int8)
    qcol = sb("qcol", [128, 4], F32)
    q2b = sb("q2b", [128, 2 * INNER], BF16)
    guqT = sb("guqT", [128, 2 * INNER], BF16)
    xt = sb("xt", [128, 2 * D], F32)
    xqb1 = sb("xqb1", [128, D], BF16)
    gb = sb("gb", [128, 2 * 512], BF16)
    ub = sb("ub", [128, 2 * 512], BF16)
    stat = ln_stat_bufs(sb)
    qsg = sb("qsg", [128, 2 * CHUNK_RT], F32)
    qsu = sb("qsu", [128, 2 * CHUNK_RT], F32)
    colx = sb("colx", [128, 8], F32)
    qs2h = sb("qs2h", [128, 2], F32)
    wsb = sb("wsb", [128, 3], F32)
    ps_tr = psb("ps_tr", [128, 2 * D], BF16)
    ps_g = psb("ps_g", [128, 2 * 512], F32)
    ps_u = psb("ps_u", [128, 2 * 512], F32)
    ps_d = psb("ps_d", [128, 2 * 512], F32)

    S.op("vector", (lambda a=stat[5]: lambda e: e.memset(a, 1e-5))(), writes=("ln_eps",))
    S.dma("sync", lambda e: e.dma_start(out=wsb, in_=wsbd),
          reads=("wsbd",), writes=("wsb0", "wsb1", "wsb2"))
    S.dma("sync", lambda e: e.dma_start(out=dTq, in_=dTd),
          reads=("dTd",), writes=("dTq",))

    def ph1(ch):
        cp = ch % 2
        for r in range(CHUNK_RT):
            rt = ch * CHUNK_RT + r
            p = rt % 2
            xtp = xt[:, p * D:(p + 1) * D]
            kx = "xt%d" % p
            S.dma("sync", (lambda rr=rt, xtp=xtp: lambda e: e.dma_start(
                out=xtp, in_=xd[rr * 128:(rr + 1) * 128, :]))(),
                writes=(kx,))
            bn6a, mva, sda, rsa, nb, eps = stat
            lp = rt % 2
            bn6 = bn6a[:, lp * 12:(lp + 1) * 12]
            mv = mva[:, lp * 2:(lp + 1) * 2]
            sd = sda[:, lp:lp + 1]
            lx = "%d" % lp
            S.op("vector", (lambda a=bn6, b=xtp: lambda e: e.bn_stats(
                out=a[:, 0:6], in_=b[:, 0:512]))(), reads=(kx,), writes=("ln_bn6" + lx,))
            S.op("vector", (lambda a=bn6, b=xtp: lambda e: e.bn_stats(
                out=a[:, 6:12], in_=b[:, 512:1024]))(), reads=(kx,), writes=("ln_bn6b" + lx,))
            S.op("vector", (lambda mv=mv, bn6=bn6: lambda e: e.bn_aggr(
                out=mv, in_=bn6))(),
                reads=("ln_bn6" + lx, "ln_bn6b" + lx), writes=("ln_mv" + lx,))
            S.op("scalar", (lambda sd=sd, mv=mv: lambda e: e.activation(
                out=sd, in_=mv[:, 1:2], func=AF.Sqrt, bias=eps, scale=1.0))(),
                reads=("ln_mv" + lx, "ln_eps"), writes=("ln_sd" + lx,))
            S.op("vector", (lambda b=xtp: lambda e: e.tensor_reduce(
                out=colx[:, 0:1], in_=b, axis=AX.X, op=ALU.max))(),
                reads=(kx,), writes=("colx0",))
            S.op("vector", (lambda b=xtp: lambda e: e.tensor_reduce(
                out=colx[:, 1:2], in_=b, axis=AX.X, op=ALU.min))(),
                reads=(kx,), writes=("colx1",))
            S.op("vector", (lambda mv=mv: lambda e: e.tensor_tensor(
                out=colx[:, 2:3], in0=colx[:, 0:1], in1=mv[:, 0:1], op=ALU.subtract))(),
                reads=("colx0", "ln_mv" + lx), writes=("colx2",))
            S.op("vector", (lambda mv=mv: lambda e: e.tensor_tensor(
                out=colx[:, 3:4], in0=mv[:, 0:1], in1=colx[:, 1:2], op=ALU.subtract))(),
                reads=("colx1", "ln_mv" + lx), writes=("colx3",))
            S.op("vector", lambda e: e.tensor_tensor(
                out=colx[:, 4:5], in0=colx[:, 2:3], in1=colx[:, 3:4], op=ALU.max),
                reads=("colx2", "colx3"), writes=("colx4",))
            S.op("vector", (lambda sd=sd: lambda e: e.tensor_scalar(
                out=colx[:, 5:6], in0=sd, scalar1=1e-5, scalar2=0.0,
                op0=ALU.mult, op1=ALU.add))(),
                reads=("ln_sd" + lx,), writes=("colx5",))
            S.op("vector", lambda e: e.tensor_tensor(
                out=colx[:, 4:5], in0=colx[:, 4:5], in1=colx[:, 5:6], op=ALU.max),
                reads=("colx4", "colx5"), writes=("colx4",))
            S.op("vector", lambda e: e.reciprocal(out=colx[:, 6:7], in_=colx[:, 4:5]),
                 reads=("colx4",), writes=("colx6",))
            S.op("vector", lambda e: e.tensor_scalar(
                out=colx[:, 6:7], in0=colx[:, 6:7], scalar1=127.0, scalar2=0.0,
                op0=ALU.mult, op1=ALU.add),
                reads=("colx6",), writes=("colx6",))
            S.op("vector", (lambda sd=sd: lambda e: e.reciprocal(out=colx[:, 7:8], in_=sd))(),
                 reads=("ln_sd" + lx,), writes=("colx7",))
            S.op("vector", lambda e: e.scalar_tensor_tensor(
                out=colx[:, 5:6], in0=colx[:, 4:5], scalar=1.0 / 127.0,
                in1=colx[:, 7:8], op0=ALU.mult, op1=ALU.mult),
                reads=("colx4", "colx7"), writes=("colx5",))
            S.op("vector", (lambda rr=r: lambda e: e.tensor_tensor(
                out=qsg[:, cp * CHUNK_RT + rr: cp * CHUNK_RT + rr + 1], in0=colx[:, 5:6], in1=wsb[:, 0:1],
                op=ALU.mult))(),
                reads=("colx5", "wsb0"), writes=("qsg%d" % (cp * CHUNK_RT + r),))
            S.op("vector", (lambda rr=r: lambda e: e.tensor_tensor(
                out=qsu[:, cp * CHUNK_RT + rr: cp * CHUNK_RT + rr + 1], in0=colx[:, 5:6], in1=wsb[:, 1:2],
                op=ALU.mult))(),
                reads=("colx5", "wsb1"), writes=("qsu%d" % (cp * CHUNK_RT + r),))
            S.op("vector", (lambda b=xtp, mv=mv: lambda e: e.tensor_scalar(
                out=i8x, in0=b, scalar1=mv[:, 0:1], scalar2=colx[:, 6:7],
                op0=ALU.subtract, op1=ALU.mult))(),
                reads=(kx, "ln_mv" + lx, "colx6"), writes=("i32x",))
            S.op("vector", lambda e: e.tensor_copy(out=xqb1, in_=i8x),
                 reads=("i32x",), writes=("xqb1",))
            pp = rt % 2
            pstr = ps_tr[:, pp * D:(pp + 1) * D]
            for k in range(8):
                S.op("tensor", (lambda kk=k, pstr=pstr: lambda e: e.transpose(
                    pstr[:, kk * 128:(kk + 1) * 128],
                    xqb1[:, kk * 128:(kk + 1) * 128], ident))(),
                    reads=("xqb1", "ident"), writes=("pstr%d" % pp,))
            S.pe_fence("pstr%d" % pp)
            S.op("scalar", (lambda rr=r, pstr=pstr: lambda e: e.activation(
                out=xqTc[:, (cp * CHUNK_RT + rr) * D:(cp * CHUNK_RT + rr + 1) * D],
                in_=pstr, func=AF.Copy, scale=1.0))(),
                reads=("pstr%d" % pp,), writes=("xqT%d" % (cp * CHUNK_RT + r),))


    def ph2(ch):
        cp = ch % 2
        for ob in range(8):
            pb = ob % 2
            gslp = gsl[:, pb * 4096:(pb + 1) * 4096]
            uslp = usl[:, pb * 4096:(pb + 1) * 4096]
            S.dma("sync", (lambda obb=ob, gslp=gslp: lambda e: e.dma_start(
                out=gslp, in_=gTd[:, obb * 4096:(obb + 1) * 4096]))(),
                reads=("gTd",), writes=("gsl%d" % pb,))
            S.dma("gpsimd", (lambda obb=ob, uslp=uslp: lambda e: e.dma_start(
                out=uslp, in_=uTd[:, obb * 4096:(obb + 1) * 4096]))(),
                reads=("uTd",), writes=("usl%d" % pb,))
            for r in range(CHUNK_RT):
                pm = r % 2
                psg = ps_g[:, pm * 512:(pm + 1) * 512]
                psu = ps_u[:, pm * 512:(pm + 1) * 512]
                for k in range(8):
                    S.op("tensor", (lambda kk=k, rr=r, gslp=gslp, psg=psg:
                                    lambda e: e.matmul(
                        psg, xqTc[:, (cp * CHUNK_RT + rr) * D + kk * 128: (cp * CHUNK_RT + rr) * D + (kk + 1) * 128],
                        gslp[:, kk * 512:(kk + 1) * 512],
                        start=(kk == 0), stop=(kk == 7)))(),
                        reads=("xqT%d" % (cp * CHUNK_RT + r), "gsl%d" % pb), writes=("psg%d" % pm,))
                for k in range(8):
                    S.op("tensor", (lambda kk=k, rr=r, uslp=uslp, psu=psu:
                                    lambda e: e.matmul(
                        psu, xqTc[:, (cp * CHUNK_RT + rr) * D + kk * 128: (cp * CHUNK_RT + rr) * D + (kk + 1) * 128],
                        uslp[:, kk * 512:(kk + 1) * 512],
                        start=(kk == 0), stop=(kk == 7)))(),
                        reads=("xqT%d" % (cp * CHUNK_RT + r), "usl%d" % pb), writes=("psu%d" % pm,))
                S.pe_fence("psg%d" % pm)
                S.pe_fence("psu%d" % pm)
                gbp = gb[:, pm * 512:(pm + 1) * 512]
                ubp = ub[:, pm * 512:(pm + 1) * 512]
                S.op("scalar", (lambda rr=r, psg=psg, gbp=gbp: lambda e: e.activation(
                    out=gbp, in_=psg, func=AF.Silu, scale=qsg[:, cp * CHUNK_RT + rr: cp * CHUNK_RT + rr + 1]))(),
                    reads=("psg%d" % pm, "qsg%d" % (cp * CHUNK_RT + r)), writes=("gb%d" % pm,))
                S.op("vector", (lambda rr=r, psu=psu, ubp=ubp: lambda e: e.tensor_scalar(
                    out=ubp, in0=psu, scalar1=qsu[:, cp * CHUNK_RT + rr: cp * CHUNK_RT + rr + 1], scalar2=0.0,
                    op0=ALU.mult, op1=ALU.add))(),
                    reads=("psu%d" % pm, "qsu%d" % (cp * CHUNK_RT + r)), writes=("ub%d" % pm,))
                guw = ("gu%d" % r,)
                S.op("gpsimd", (lambda rr=r, obb=ob, gbp=gbp, ubp=ubp:
                                lambda e: e.tensor_tensor(
                    out=big[:, rr * INNER + obb * 512: rr * INNER + (obb + 1) * 512],
                    in0=gbp, in1=ubp, op=ALU.mult))(),
                    reads=("gb%d" % pm, "ub%d" % pm), writes=guw)


    def ph3(ch):
        def quant(r):
            rt = ch * CHUNK_RT + r
            p = rt % 2
            gu_ap = big[:, r * INNER:(r + 1) * INNER]
            i8gp = i8g[:, p * INNER:(p + 1) * INNER]
            qc0 = qcol[:, p * 2: p * 2 + 1]
            qc1 = qcol[:, p * 2 + 1: p * 2 + 2]
            qsp = qs2h[:, p:p + 1]
            S.op("vector", (lambda gu_ap=gu_ap, qc0=qc0: lambda e: e.tensor_reduce(
                out=qc0, in_=gu_ap, axis=AX.X, op=ALU.max,
                apply_absolute_value=True))(),
                reads=("gu%d" % r,), writes=("qc0_%d" % p,))
            S.op("vector", (lambda qc0=qc0: lambda e: e.tensor_scalar(
                out=qc0, in0=qc0, scalar1=1e-5, scalar2=0.0,
                op0=ALU.max, op1=ALU.add))(),
                reads=("qc0_%d" % p,), writes=("qc0_%d" % p,))
            S.op("vector", (lambda qc0=qc0, qc1=qc1: lambda e: e.reciprocal(
                out=qc1, in_=qc0))(),
                reads=("qc0_%d" % p,), writes=("qc1_%d" % p,))
            S.op("vector", (lambda qc1=qc1: lambda e: e.tensor_scalar(
                out=qc1, in0=qc1, scalar1=127.0, scalar2=0.0,
                op0=ALU.mult, op1=ALU.add))(),
                reads=("qc1_%d" % p,), writes=("qc1_%d" % p,))
            S.op("vector", (lambda gu_ap=gu_ap, i8gp=i8gp, qc1=qc1:
                            lambda e: e.tensor_scalar(
                out=i8gp, in0=gu_ap, scalar1=qc1, scalar2=0.0,
                op0=ALU.mult, op1=ALU.add))(),
                reads=("gu%d" % r, "qc1_%d" % p), writes=("i8g%d" % p,))
            S.op("vector", (lambda qc0=qc0, qsp=qsp: lambda e: e.scalar_tensor_tensor(
                out=qsp, in0=qc0, scalar=1.0 / 127.0, in1=wsb[:, 2:3],
                op0=ALU.mult, op1=ALU.mult))(),
                reads=("qc0_%d" % p, "wsb2"), writes=("qs2h%d" % p,))
            q2bp = q2b[:, p * INNER:(p + 1) * INNER]
            S.op("vector", (lambda i8gp=i8gp, q2bp=q2bp: lambda e: e.tensor_copy(
                out=q2bp, in_=i8gp))(),
                reads=("i8g%d" % p,), writes=("q2b%d" % p,))

        def trpart(r):
            rt = ch * CHUNK_RT + r
            p = rt % 2
            q2bp = q2b[:, p * INNER:(p + 1) * INNER]
            guqTp = guqT[:, p * INNER:(p + 1) * INNER]
            for kg in range(4):
                pp = kg % 2
                pstr = ps_tr[:, pp * D:(pp + 1) * D]
                for k in range(8):
                    S.op("tensor", (lambda kk=k, kgg=kg, pstr=pstr, q2bp=q2bp:
                                    lambda e: e.transpose(
                        pstr[:, kk * 128:(kk + 1) * 128],
                        q2bp[:, kgg * D + kk * 128: kgg * D + (kk + 1) * 128],
                        ident))(),
                        reads=("q2b%d" % p, "ident"), writes=("pstr%d" % pp,))
                S.pe_fence("pstr%d" % pp)
                S.op("scalar", (lambda kgg=kg, pstr=pstr, guqTp=guqTp:
                                lambda e: e.activation(
                    out=guqTp[:, kgg * D:(kgg + 1) * D], in_=pstr,
                    func=AF.Copy, scale=1.0))(),
                    reads=("pstr%d" % pp,), writes=("guqT%d" % p,))

        def mmpart(r):
            rt = ch * CHUNK_RT + r
            p = rt % 2
            guqTp = guqT[:, p * INNER:(p + 1) * INNER]
            qsp = qs2h[:, p:p + 1]
            xtp = xt[:, p * D:(p + 1) * D]
            S.dma("gpsimd", (lambda rr=rt, xtp=xtp: lambda e: e.dma_start(
                out=xtp, in_=xd[rr * 128:(rr + 1) * 128, :]))(),
                writes=("xt%d" % p,))
            for oh in range(2):
                pm = oh
                psd = ps_d[:, pm * 512:(pm + 1) * 512]
                for k in range(32):
                    S.op("tensor", (lambda kk=k, ohh=oh, psd=psd, guqTp=guqTp:
                                    lambda e: e.matmul(
                        psd, guqTp[:, kk * 128:(kk + 1) * 128],
                        dTq[:, kk * D + ohh * 512: kk * D + ohh * 512 + 512],
                        start=(kk == 0), stop=(kk == 31)))(),
                        reads=("guqT%d" % p, "dTq"), writes=("psd%d" % pm,))
                S.pe_fence("psd%d" % pm)
                S.op("vector", (lambda ohh=oh, psd=psd, xtp=xtp, qsp=qsp:
                                lambda e: e.scalar_tensor_tensor(
                    out=xtp[:, ohh * 512:(ohh + 1) * 512], in0=psd, scalar=qsp,
                    in1=xtp[:, ohh * 512:(ohh + 1) * 512],
                    op0=ALU.mult, op1=ALU.add))(),
                    reads=("psd%d" % pm, "qs2h%d" % p), writes=("xt%d" % p,))
            S.dma("scalar", (lambda rr=rt, xtp=xtp: lambda e: e.dma_start(
                out=yd[rr * 128:(rr + 1) * 128, :], in_=xtp))(),
                reads=("xt%d" % p,), writes=("ydma",))

        quant(0)
        quant(1)
        trpart(0)
        for r in range(CHUNK_RT):
            mmpart(r)
            if r + 1 < CHUNK_RT:
                trpart(r + 1)
            if r + 2 < CHUNK_RT:
                quant(r + 2)

    ph1(0)
    for ch in range(NCHUNK):
        ph2(ch)
        if ch + 1 < NCHUNK:
            ph1(ch + 1)
        ph3(ch)



def build_l0ffn():
    nc = bass.Bass("TRN2", target_bir_lowering=False, debug=False)
    xd = nc.dram_tensor("x", [S_SEQ, D], F32, kind="ExternalInput").ap()
    wgd = nc.dram_tensor("w_gate", [INNER, D], F32, kind="ExternalInput").ap()
    wud = nc.dram_tensor("w_up", [INNER, D], F32, kind="ExternalInput").ap()
    wdd = nc.dram_tensor("w_down", [D, INNER], F32, kind="ExternalInput").ap()
    onesd = nc.dram_tensor("ones_row", [1, 128], F32, kind="ExternalInput").ap()
    identd = nc.dram_tensor("ident", [128, 128], F32, kind="ExternalInput").ap()
    yd = nc.dram_tensor("y", [S_SEQ, D], F32, kind="ExternalOutput").ap()
    gTd = nc.dram_tensor("gT_scr", [128, 8 * INNER], BF16, kind="Internal").ap()
    uTd = nc.dram_tensor("uT_scr", [128, 8 * INNER], BF16, kind="Internal").ap()
    dTd = nc.dram_tensor("dT_scr", [128, 32 * D], BF16, kind="Internal").ap()
    wsbd = nc.dram_tensor("wsb_scr", [128, 3], F32, kind="Internal").ap()
    with ExitStack() as octx:
        block = octx.enter_context(nc.Block())
        S = Sched(nc)
        cst = _emit_consts(S, nc, octx, onesd, identd)
        with ExitStack() as c0:
            sb, psb = _mk_scoped(nc, c0, "p_")
            emit_l0ffn_prep(S, sb, psb, wgd, wud, wdd, gTd, uTd, dTd, wsbd, cst)
        S.stage_barrier()
        with ExitStack() as c1:
            sb, psb = _mk_scoped(nc, c1, "m_")
            emit_l0ffn_main(S, sb, psb, xd, yd, gTd, uTd, dTd, wsbd, cst)
        S.barrier_on("vector", "ydma")
        S.finalize(octx, block)
    return nc


def run_l0ffn(nc, x_full, w_gate, w_up, w_down):
    c = _consts()
    in_maps = [{"x": np.ascontiguousarray(x_full[b]), "w_gate": w_gate,
                "w_up": w_up, "w_down": w_down,
                "ones_row": c["ones_row"], "ident": c["ident"]} for b in range(8)]
    res = run_bass_kernel_spmd(nc, in_maps, core_ids=list(range(8)))
    return np.stack([r["y"] for r in res.results], axis=0)



DC_RT = 2
DNCH = NRT // DC_RT
SC = DC_RT * 128


def emit_l1attn(S, sb, psb, xd, ww, petd, yd, cst):
    ald, wdxd, bdxd, wxd, bxd = (ww["a_logit"], ww["W_dx"], ww["b_dx"],
                                 ww["W_x"], ww["b_x"])
    wgxd, bgxd, wgad, bgad, wod, bod = (ww["W_gx"], ww["b_gx"], ww["W_ga"],
                                        ww["b_ga"], ww["W_o"], ww["b_o"])
    ones_row = cst["ones_row"]
    ident = cst["ident"]
    wdx = sb("wdx", [128, 8 * D], BF16)
    wdx_t = sb("wdx_t", [16, D], BF16)
    wx = sb("wx", [128, 8 * D], BF16)
    wx_t = sb("wx_t", [16, D], BF16)
    wgx = sb("wgx", [128, 8 * D], BF16)
    wga = sb("wga", [128, 8 * D], BF16)
    wo = sb("wo", [128, 8 * D], BF16)
    peb = sb("peb", [16, S_SEQ], BF16)
    sqlrep = sb("sqlrep", [128, D], F32)
    brep2 = sb("brep2", [128, D], F32)
    rb0 = sb("rb0", [1, D], F32)
    rb1 = sb("rb1", [1, D], F32)
    ae_c = sb("ae_c", [128, 8], F32)
    bx_c = sb("bx_c", [128, 8], F32)
    bgx_c = sb("bgx_c", [128, 8], F32)
    bga_c = sb("bga_c", [128, 8], F32)
    wstage = sb("wstage_d", [128, D], F32)

    xt = sb("xt", [128, 2 * DC_RT * D], F32)
    xlnb = sb("xlnb", [128, 2 * D], BF16)
    xcm = sb("xcm", [128, 8 * SC], BF16)
    dxb = sb("dxb", [128, 2 * DC_RT * D], F32)
    xhb = sb("xhb", [128, 2 * 8 * SC], BF16)
    ab = sb("ab", [128, 8 * SC], F32)
    bb = sb("bb", [128, 8 * SC], F32)
    hb = sb("hb", [128, 8 * SC], BF16)
    h32 = sb("h32", [128, SC], F32)
    carry = sb("carry", [128, 8], F32)
    gx_t = sb("gx_t", [128, 8 * SC], F32)
    ga_t = sb("ga_t", [128, 8 * SC], F32)
    a_t = sb("a_t", [128, 2 * SC], F32)
    stat = ln_stat_bufs(sb)

    ps_tr = psb("ps_tr", [128, D], BF16)
    ps_dx = psb("ps_dx", [128, 2 * 512], F32)
    ps_xh = psb("ps_xh", [128, 2 * 512], F32)
    ps_g = psb("ps_g", [128, 2 * 512], F32)
    ps_o = psb("ps_o", [128, 512], F32)

    S.op("vector", (lambda a=stat[5]: lambda e: e.memset(a, 1e-5))(), writes=("ln_eps",))
    S.op("vector", lambda e: e.memset(carry, 0.0), writes=("carry",))
    S.dma("gpsimd", lambda e: e.dma_start(out=peb, in_=petd), writes=("peb",))

    S.dma("sync", lambda e: e.dma_start(
        out=rb0, in_=ald.rearrange("(one d) -> one d", one=1)),
        writes=("row0",))
    S.op("scalar", lambda e: e.activation(out=rb1, in_=rb0, func=AF.Sigmoid),
         reads=("row0",), writes=("aer",))
    S.op("vector", lambda e: e.tensor_tensor(out=rb1, in0=rb1, in1=rb1,
                                             op=ALU.mult),
         reads=("aer",), writes=("aer",))
    S.op("scalar", lambda e: e.activation(out=rb1, in_=rb1, func=AF.Sqrt,
                                          bias=1.0, scale=-1.0),
         reads=("aer",), writes=("sqlr",))
    for hh in range(2):
        S.op("tensor", (lambda hh=hh: lambda e: e.matmul(
            ps_dx[:, 0:512], ones_row, rb1[:, hh * 512:(hh + 1) * 512],
            start=True, stop=True))(),
            reads=("sqlr", "ones", "sqlrep"), writes=("psdx0",))
        S.op("vector", (lambda hh=hh: lambda e: e.tensor_copy(
            out=sqlrep[:, hh * 512:(hh + 1) * 512], in_=ps_dx[:, 0:512]))(),
            reads=("psdx0",), writes=("sqlrep",))
    S.dma("sync", lambda e: e.dma_start(
        out=rb0, in_=bdxd.rearrange("(one d) -> one d", one=1)),
        reads=("aer",), writes=("row1",))
    S.dma("sync", lambda e: e.dma_start(
        out=rb1, in_=bod.rearrange("(one d) -> one d", one=1)),
        reads=("sqlrep",), writes=("row2", "sqlr"))
    S.op("vector", lambda e: e.tensor_tensor(
        out=rb0, in0=rb0, in1=rb1, op=ALU.add),
        reads=("row1", "row2"), writes=("row1",))
    S.op("vector", lambda e: e.tensor_scalar(
        out=rb0, in0=rb0, scalar1=0.5,
        scalar2=0.0, op0=ALU.mult, op1=ALU.add),
        reads=("row1",), writes=("row1",))
    for hh in range(2):
        S.op("tensor", (lambda hh=hh: lambda e: e.matmul(
            ps_dx[:, 0:512], ones_row, rb0[:, hh * 512:(hh + 1) * 512],
            start=True, stop=True))(),
            reads=("row1", "ones", "brep2"), writes=("psdx0",))
        S.op("vector", (lambda hh=hh: lambda e: e.tensor_copy(
            out=brep2[:, hh * 512:(hh + 1) * 512], in_=ps_dx[:, 0:512]))(),
            reads=("psdx0",), writes=("brep2",))
    for (dst, srcd, key) in ((ae_c, ald, "ae_c"), (bx_c, bxd, "bx_c"),
                             (bgx_c, bgxd, "bgx_c"), (bga_c, bgad, "bga_c")):
        for j in range(8):
            S.dma("sync", (lambda dst=dst, srcd=srcd, jj=j: lambda e: e.dma_start(
                out=dst[:, jj:jj + 1],
                in_=srcd[jj * 128:(jj + 1) * 128].rearrange("(p one) -> p one", one=1)))(),
                writes=(key,))
    S.op("scalar", lambda e: e.activation(out=ae_c, in_=ae_c, func=AF.Sigmoid),
         reads=("ae_c",), writes=("ae_c",))
    S.op("vector", lambda e: e.tensor_tensor(out=a_t[:, 0:8], in0=ae_c, in1=ae_c,
                                             op=ALU.mult),
         reads=("ae_c",), writes=("sql_c0",))
    S.op("scalar", lambda e: e.activation(out=a_t[:, 0:8], in_=a_t[:, 0:8],
                                          func=AF.Sqrt, bias=1.0, scale=-1.0),
         reads=("sql_c0",), writes=("sql_c",))
    S.op("vector", lambda e: e.tensor_tensor(out=bx_c, in0=bx_c, in1=a_t[:, 0:8],
                                             op=ALU.mult),
         reads=("bx_c", "sql_c"), writes=("bx_c",))
    for (dst, srcd, key) in ((wdx, wdxd, "wdx"), (wgx, wgxd, "wgx"),
                             (wga, wgad, "wga"), (wo, wod, "wo")):
        for k in range(8):
            S.dma("gpsimd", (lambda dst=dst, srcd=srcd, kk=k: lambda e: e.dma_start(
                out=dst[:, kk * D:(kk + 1) * D],
                in_=srcd[kk * 128:(kk + 1) * 128, :]))(),
                writes=(key,))
    S.dma("gpsimd", lambda e: e.dma_start(out=wdx_t, in_=wdxd[1024:1040, :]),
          writes=("wdx_t",))
    for k in range(8):
        S.dma("sync", (lambda kk=k: lambda e: e.dma_start(
            out=wstage, in_=wxd[kk * 128:(kk + 1) * 128, :]))(),
            reads=("wstage_used",), writes=("wstage",))
        S.op("vector", lambda e: e.tensor_tensor(
            out=wstage, in0=wstage, in1=sqlrep, op=ALU.mult),
            reads=("wstage", "sqlrep"), writes=("wstage",))
        S.op("vector", (lambda kk=k: lambda e: e.tensor_copy(
            out=wx[:, kk * D:(kk + 1) * D], in_=wstage))(),
            reads=("wstage",), writes=("wx", "wstage_used"))
    S.dma("sync", lambda e: e.dma_start(out=wstage[0:16, :], in_=wxd[1024:1040, :]),
          reads=("wstage_used",), writes=("wstage16",))
    S.op("vector", lambda e: e.tensor_tensor(
        out=wstage[0:16, :], in0=wstage[0:16, :], in1=sqlrep[0:16, :], op=ALU.mult),
        reads=("wstage16", "sqlrep"), writes=("wstage16",))
    S.op("vector", lambda e: e.tensor_copy(out=wx_t, in_=wstage[0:16, :]),
         reads=("wstage16",), writes=("wx_t",))

    def front(ch):
        par = ch % 2
        xtc = xt[:, par * DC_RT * D:(par + 1) * DC_RT * D]
        dxc = dxb[:, par * DC_RT * D:(par + 1) * DC_RT * D]
        xhc = xhb[:, par * 8 * SC:(par + 1) * 8 * SC]
        for r in range(DC_RT):
            rt = ch * DC_RT + r
            xtp = xtc[:, r * D:(r + 1) * D]
            kx = "xt%d_%d" % (par, r)
            S.dma("sync", (lambda rr=rt, xtp=xtp: lambda e: e.dma_start(
                out=xtp, in_=xd[rr * 128:(rr + 1) * 128, :]))(),
                writes=(kx,))
            pp = rt % 2
            emit_ln(S, stat, kx, xtp,
                    xlnb[:, pp * D:(pp + 1) * D], "xlnb%d" % pp, par=pp)
            for k in range(8):
                S.op("tensor", (lambda kk=k, pp=pp: lambda e: e.transpose(
                    ps_tr[:, kk * 128:(kk + 1) * 128],
                    xlnb[:, pp * D + kk * 128: pp * D + (kk + 1) * 128], ident))(),
                    reads=("xlnb%d" % pp, "ident"), writes=("pstr",))
            S.op("vector", (lambda rr=r: lambda e: e.tensor_copy(
                out=xcm.rearrange("p (k s) -> p k s", k=8)
                    [:, :, rr * 128:(rr + 1) * 128],
                in_=ps_tr.rearrange("p (k c) -> p k c", k=8)))(),
                reads=("pstr",), writes=("xcm%d" % r, "pstr"))
        xcm_keys = tuple("xcm%d" % r for r in range(DC_RT))

        for r in range(DC_RT):
            rt = ch * DC_RT + r
            kd = "dxb%d_%d" % (par, r)
            for oh in range(2):
                pm = oh
                psd = ps_dx[:, pm * 512:(pm + 1) * 512]
                for k in range(8):
                    S.op("tensor", (lambda kk=k, rr=r, ohh=oh, psd=psd:
                                    lambda e: e.matmul(
                        psd, xcm[:, kk * SC + rr * 128: kk * SC + (rr + 1) * 128],
                        wdx[:, kk * D + ohh * 512: kk * D + ohh * 512 + 512],
                        start=(kk == 0), stop=False))(),
                        reads=xcm_keys + ("wdx",), writes=("psdx%d" % pm,))
                S.op("tensor", (lambda rr=rt, ohh=oh, psd=psd: lambda e: e.matmul(
                    psd, peb[:, rr * 128:(rr + 1) * 128],
                    wdx_t[:, ohh * 512:(ohh + 1) * 512],
                    start=False, stop=True))(),
                    reads=("peb", "wdx_t"), writes=("psdx%d" % pm,))
                S.op("vector", (lambda rr=r, ohh=oh, psd=psd, dxc=dxc: lambda e:
                                e.scalar_tensor_tensor(
                    out=dxc[:, rr * D + ohh * 512: rr * D + ohh * 512 + 512],
                    in0=psd, scalar=0.5,
                    in1=brep2[:, ohh * 512:(ohh + 1) * 512],
                    op0=ALU.mult, op1=ALU.add))(),
                    reads=("psdx%d" % pm, "brep2"), writes=(kd,))
                S.op("vector", (lambda rr=r, ohh=oh, dxc=dxc, xtc=xtc:
                                lambda e: e.tensor_tensor(
                    out=dxc[:, rr * D + ohh * 512: rr * D + ohh * 512 + 512],
                    in0=dxc[:, rr * D + ohh * 512: rr * D + ohh * 512 + 512],
                    in1=xtc[:, rr * D + ohh * 512: rr * D + ohh * 512 + 512],
                    op=ALU.add))(),
                    reads=(kd, "xt%d_%d" % (par, r)), writes=(kd,))

        for oc in range(8):
            pm = oc % 2
            psx = ps_xh[:, pm * 512:(pm + 1) * 512]
            for k in range(8):
                S.op("tensor", (lambda kk=k, occ=oc, psx=psx: lambda e: e.matmul(
                    psx[:, 0:SC], wx[:, kk * D + occ * 128: kk * D + (occ + 1) * 128],
                    xcm[:, kk * SC:(kk + 1) * SC],
                    start=(kk == 0), stop=False))(),
                    reads=xcm_keys + ("wx",), writes=("psxh%d" % pm,))
            S.op("tensor", (lambda occ=oc, rr=ch, psx=psx: lambda e: e.matmul(
                psx[:, 0:SC], wx_t[:, occ * 128:(occ + 1) * 128],
                peb[:, rr * SC:(rr + 1) * SC],
                start=False, stop=True))(),
                reads=("peb", "wx_t"), writes=("psxh%d" % pm,))
            S.op("vector", (lambda occ=oc, psx=psx, xhc=xhc, par=par:
                            lambda e: e.tensor_scalar(
                out=xhc[:, occ * SC:(occ + 1) * SC], in0=psx[:, 0:SC],
                scalar1=bx_c[:, occ:occ + 1], scalar2=0.0,
                op0=ALU.add, op1=ALU.add))(),
                reads=("psxh%d" % pm, "bx_c"), writes=("xhb%d_%d" % (par, oc),))

    def back(ch):
        par = ch % 2
        xhc = xhb[:, par * 8 * SC:(par + 1) * 8 * SC]
        dxc = dxb[:, par * DC_RT * D:(par + 1) * DC_RT * D]
        xhb_keys = tuple("xhb%d_%d" % (par, oc) for oc in range(8))

        for oc in range(8):
            for k in range(8):
                S.op("tensor", (lambda kk=k, occ=oc, xhc=xhc: lambda e: e.matmul(
                    ps_g[:, 0:SC], wgx[:, kk * D + occ * 128: kk * D + (occ + 1) * 128],
                    xhc[:, kk * SC:(kk + 1) * SC],
                    start=(kk == 0), stop=(kk == 7)))(),
                    reads=xhb_keys + ("wgx",), writes=("psg0",))
            for k in range(8):
                S.op("tensor", (lambda kk=k, occ=oc, xhc=xhc: lambda e: e.matmul(
                    ps_g[:, 512:512 + SC], wga[:, kk * D + occ * 128: kk * D + (occ + 1) * 128],
                    xhc[:, kk * SC:(kk + 1) * SC],
                    start=(kk == 0), stop=(kk == 7)))(),
                    reads=xhb_keys + ("wga",), writes=("psg1",))
            S.op("scalar", (lambda occ=oc: lambda e: e.activation(
                out=gx_t[:, occ * SC:(occ + 1) * SC], in_=ps_g[:, 0:SC],
                func=AF.Sigmoid, bias=bgx_c[:, occ:occ + 1]))(),
                reads=("psg0", "bgx_c"), writes=("gx_t%d" % oc, "psg0"))
            S.op("scalar", (lambda occ=oc: lambda e: e.activation(
                out=ga_t[:, occ * SC:(occ + 1) * SC], in_=ps_g[:, 512:512 + SC],
                func=AF.Sigmoid, bias=bga_c[:, occ:occ + 1]))(),
                reads=("psg1", "bga_c"), writes=("ga_t%d" % oc, "psg1"))
        for oc in range(8):
            sp_ = oc % 2
            gxp = gx_t[:, oc * SC:(oc + 1) * SC]
            gap = ga_t[:, oc * SC:(oc + 1) * SC]
            atp = a_t[:, sp_ * SC:(sp_ + 1) * SC]
            S.op("vector", (lambda occ=oc, gap=gap, atp=atp: lambda e: e.tensor_scalar(
                out=atp, in0=gap, scalar1=ae_c[:, occ:occ + 1], scalar2=0.0,
                op0=ALU.mult, op1=ALU.add))(),
                reads=("ga_t%d" % oc, "ae_c"), writes=("a_t%d" % sp_,))
            S.op("vector", (lambda occ=oc, atp=atp: lambda e: e.tensor_copy(
                out=ab[:, occ * SC:(occ + 1) * SC], in_=atp))(),
                reads=("a_t%d" % sp_,), writes=("ab%d" % oc,))
            S.op("vector", (lambda atp=atp, gap=gap: lambda e: e.tensor_tensor(
                out=gap, in0=atp, in1=atp, op=ALU.mult))(),
                reads=("a_t%d" % sp_, "ga_t%d" % oc), writes=("ga_t%d" % oc,))
            S.op("vector", (lambda occ=oc, gxp=gxp, xhc=xhc: lambda e: e.tensor_tensor(
                out=gxp, in0=gxp, in1=xhc[:, occ * SC:(occ + 1) * SC],
                op=ALU.mult))(),
                reads=("gx_t%d" % oc, "xhb%d_%d" % (par, oc)),
                writes=("gx_t%d" % oc,))
        for oc in range(8):
            gxp = gx_t[:, oc * SC:(oc + 1) * SC]
            gap = ga_t[:, oc * SC:(oc + 1) * SC]
            S.op("scalar", (lambda gap=gap: lambda e: e.activation(
                out=gap, in_=gap, func=AF.Sqrt, bias=1.0, scale=-1.0))(),
                reads=("ga_t%d" % oc,), writes=("ga_t%d" % oc,))
            S.op("vector", (lambda occ=oc, gxp=gxp, gap=gap: lambda e: e.tensor_tensor(
                out=bb[:, occ * SC:(occ + 1) * SC], in0=gap, in1=gxp,
                op=ALU.mult))(),
                reads=("ga_t%d" % oc, "gx_t%d" % oc), writes=("bb%d" % oc,))

        for oc in range(8):
            S.op("vector", (lambda occ=oc: lambda e: e.tensor_tensor_scan(
                out=h32, data0=ab[:, occ * SC:(occ + 1) * SC],
                data1=bb[:, occ * SC:(occ + 1) * SC],
                initial=carry[:, occ:occ + 1],
                op0=ALU.mult, op1=ALU.add))(),
                reads=("ab%d" % oc, "bb%d" % oc, "carry%d" % oc), writes=("h32",))
            S.op("vector", (lambda occ=oc: lambda e: e.tensor_copy(
                out=hb[:, occ * SC:(occ + 1) * SC], in_=h32))(),
                reads=("h32",), writes=("hb%d" % oc,))
            S.op("vector", (lambda occ=oc: lambda e: e.tensor_copy(
                out=carry[:, occ:occ + 1], in_=h32[:, SC - 1:SC]))(),
                reads=("h32",), writes=("carry%d" % oc,))
        hb_keys = tuple("hb%d" % oc for oc in range(8))

        for r in range(DC_RT):
            rt = ch * DC_RT + r
            kd = "dxb%d_%d" % (par, r)
            for oh in range(2):
                for k in range(8):
                    S.op("tensor", (lambda kk=k, rr=r, ohh=oh:
                                    lambda e: e.matmul(
                        ps_o, hb[:, kk * SC + rr * 128: kk * SC + (rr + 1) * 128],
                        wo[:, kk * D + ohh * 512: kk * D + ohh * 512 + 512],
                        start=(kk == 0), stop=(kk == 7)))(),
                        reads=hb_keys + ("wo",), writes=("pso",))
                S.op("vector", (lambda rr=r, ohh=oh, dxc=dxc:
                                lambda e: e.scalar_tensor_tensor(
                    out=dxc[:, rr * D + ohh * 512: rr * D + ohh * 512 + 512],
                    in0=ps_o, scalar=0.5,
                    in1=dxc[:, rr * D + ohh * 512: rr * D + ohh * 512 + 512],
                    op0=ALU.mult, op1=ALU.add))(),
                    reads=("pso", kd), writes=(kd, "pso"))
            S.dma("scalar", (lambda rr=rt, dxc=dxc, r=r: lambda e: e.dma_start(
                out=yd[rr * 128:(rr + 1) * 128, :],
                in_=dxc[:, r * D:(r + 1) * D]))(),
                reads=(kd,), writes=("ydma",))

    for ch in range(DNCH + 2):
        if ch >= 2:
            back(ch - 2)
        if ch < DNCH:
            front(ch)


def build_l1attn():
    nc = bass.Bass("TRN2", target_bir_lowering=False, debug=False)
    xd = nc.dram_tensor("x", [S_SEQ, D], F32, kind="ExternalInput").ap()
    ww = {}
    for nm, shp in (("a_logit", [D]), ("W_dx", [1040, D]), ("b_dx", [D]),
                    ("W_x", [1040, D]), ("b_x", [D]), ("W_gx", [D, D]),
                    ("b_gx", [D]), ("W_ga", [D, D]), ("b_ga", [D]),
                    ("W_o", [D, D]), ("b_o", [D])):
        ww[nm] = nc.dram_tensor(nm, shp, F32, kind="ExternalInput").ap()
    petd = nc.dram_tensor("peT", [16, S_SEQ], F32, kind="ExternalInput").ap()
    onesd = nc.dram_tensor("ones_row", [1, 128], F32, kind="ExternalInput").ap()
    identd = nc.dram_tensor("ident", [128, 128], F32, kind="ExternalInput").ap()
    yd = nc.dram_tensor("y", [S_SEQ, D], F32, kind="ExternalOutput").ap()
    with ExitStack() as octx:
        block = octx.enter_context(nc.Block())
        S = Sched(nc)
        cst = _emit_consts(S, nc, octx, onesd, identd)
        with ExitStack() as c1:
            sb, psb = _mk_scoped(nc, c1, "d_")
            emit_l1attn(S, sb, psb, xd, ww, petd, yd, cst)
        S.barrier_on("vector", "ydma")
        S.finalize(octx, block)
    return nc


def _pe_table():
    positions = np.arange(S_SEQ, dtype=np.float64)[:, None]
    div = np.exp(np.arange(0, 16, 2, dtype=np.float64) * (-np.log(10000.0) / 16))
    pe = np.zeros((S_SEQ, 16), dtype=np.float64)
    pe[:, 0::2] = np.sin(positions * div)
    pe[:, 1::2] = np.cos(positions * div)
    return np.ascontiguousarray((pe / 100.0).T.astype(np.float32))


def run_l1attn(nc, x_full, d):
    c = _consts()
    pet = _pe_table()
    in_maps = []
    for b in range(8):
        in_maps.append({
            "x": np.ascontiguousarray(x_full[b]),
            "a_logit": d["L1_attn_a_logit"], "W_dx": d["L1_attn_W_dx"],
            "b_dx": d["L1_attn_b_dx"], "W_x": d["L1_attn_W_x"],
            "b_x": d["L1_attn_b_x"], "W_gx": d["L1_attn_W_gx"],
            "b_gx": d["L1_attn_b_gx"], "W_ga": d["L1_attn_W_ga"],
            "b_ga": d["L1_attn_b_ga"], "W_o": d["L1_attn_W_o"],
            "b_o": d["L1_attn_b_o"], "peT": pet,
            "ones_row": c["ones_row"], "ident": c["ident"]})
    res = run_bass_kernel_spmd(nc, in_maps, core_ids=list(range(8)))
    return np.stack([r["y"] for r in res.results], axis=0)



NW = 15
NH = 16
HD = 64
VW = 72
VPAD = NH * VW


def emit_l0attn_prep_epb(S, sb, psb, pbd, epbTd, cst):
    identf = cst["identf"]
    wstage = sb("wstage", [128, 2 * D], F32)
    estage = sb("estage", [128, 4 * 512], BF16)
    ps_mm = psb("ps_mm", [128, 4 * 512], F32)
    
    for h in range(NH):
        for qt in range(4):
            pq = qt % 2
            wsl = wstage[:, pq * D:(pq + 1) * D]
            S.dma("sync", (lambda hh=h, qq=qt, wsl=wsl: lambda e: e.dma_start(
                out=wsl[:, 0:512], in_=pbd[hh, qq * 128:(qq + 1) * 128, :]))(),
                reads=("wsl_used%d" % pq,), writes=("wsl%d" % pq, "wstage"))
            for kt in range(4):
                S.op("tensor", (lambda qq=qt, kk=kt, wsl=wsl: lambda e: e.transpose(
                    ps_mm[:, kk * 512 + qq * 128: kk * 512 + (qq + 1) * 128],
                    wsl[:, kk * 128:(kk + 1) * 128], identf))(),
                    reads=("wsl%d" % pq, "identf"), writes=("ps_pb%d" % kt,))
            S.op("tensor", (lambda: lambda e: e.nop())(),
                 reads=(), writes=("wsl_used%d" % pq,))
        for kt in range(4):
            S.pe_fence("ps_pb%d" % kt)
            S.op("scalar", (lambda kk=kt: lambda e: e.activation(
                out=estage[:, kk * 512:(kk + 1) * 512],
                in_=ps_mm[:, kk * 512:(kk + 1) * 512], func=AF.Exp))(),
                reads=("ps_pb%d" % kt,), writes=("estage%d" % kt,))
            S.dma("sync",
                  (lambda hh=h, kk=kt: lambda e: e.dma_start(
                out=epbTd[:, hh * 2048 + kk * 512: hh * 2048 + (kk + 1) * 512],
                in_=estage[:, kk * 512:(kk + 1) * 512]))(),
                reads=("estage%d" % kt,), writes=("epbTd%d" % kt, "ps_pb%d" % kt))



def emit_l0attn_main(S, sb, psb, xd, ww, epbTd, yd, cst):
    wqd, bqd, wkd, bkd = ww["Wq"], ww["bq"], ww["Wk"], ww["bk"]
    wvd, bvd, wod, bod = ww["Wv"], ww["bv"], ww["Wo"], ww["bo"]
    ones_row = cst["ones_row"]
    ident = cst["ident"]
    wqs = sb("wqs", [128, 8 * D], BF16)
    wk = sb("wk", [128, 8 * D], BF16)
    wvp = sb("wvp", [128, 8 * VPAD], BF16)
    wo = sb("wo", [128, 8 * D], BF16)
    bq8_c = sb("bq8_c", [128, 8], F32)
    bk_c = sb("bk_c", [128, 8], F32)
    bvprep = sb("bvprep", [128, VPAD], F32)
    borep = sb("borep", [128, D], F32)
    rowbuf = sb("rowbuf", [1, VPAD], F32)
    wstage = sb("wstage_b", [128, 2 * D], F32)
    vstage = sb("vstage", [128, D], BF16)
    stat = ln_stat_bufs(sb)

    xt = sb("xt", [128, 2 * D], F32)
    xlnb = sb("xlnb", [128, 2 * D], BF16)
    xcm_s = sb("xcm_s", [128, 8 * 256], BF16)
    qcm = sb("qcm", [128, 8 * 1024], BF16)
    kcm = sb("kcm", [128, 8 * 768], BF16)
    vrm = sb("vrm", [128, 6 * VPAD], BF16)
    merged = sb("merged", [128, 6 * D], F32)
    epb = sb("epb", [128, 2 * 2048], BF16)
    estage = sb("estage", [128, 2 * 1024], BF16)
    owbuf = sb("owbuf", [128, 2 * 512], BF16)
    probsT = sb("probsT", [128, 3 * 4 * 512], BF16)
    rden = sb("rden", [128, 2 * 4], F32)
    mbf = vstage
    mT = sb("mT", [128, 2 * D], BF16)
    obuf = sb("obuf", [128, D], F32)
    ybuf = wstage

    ps_tr = psb("ps_tr", [128, D], BF16)
    ps_mm = psb("ps_mm", [128, 4 * 512], F32)
    ps_ow = psb("ps_ow", [128, 2 * 512], F32)
    ps_owt = psb("ps_owt", [128, D], BF16)

    S.op("vector", (lambda a=stat[5]: lambda e: e.memset(a, 1e-5))(), writes=("ln_eps",))
    for t in range(6):
        S.op("vector", (lambda tt=t: lambda e: e.memset(
            merged[:, tt * D:(tt + 1) * D], 0.0))(), writes=("mg%d" % t,))

    for k in range(8):
        S.dma("sync", (lambda kk=k: lambda e: e.dma_start(
            out=wstage[:, 0:D], in_=wqd[kk * 128:(kk + 1) * 128, :]))(),
            reads=("wstage_used",), writes=("wstage",))
        S.op("vector", (lambda kk=k: lambda e: e.tensor_scalar(
            out=wqs[:, kk * D:(kk + 1) * D], in0=wstage[:, 0:D],
            scalar1=0.125, scalar2=0.0, op0=ALU.mult, op1=ALU.add))(),
            reads=("wstage",), writes=("wqs", "wstage_used"))
        S.dma("gpsimd", (lambda kk=k: lambda e: e.dma_start(
            out=wk[:, kk * D:(kk + 1) * D], in_=wkd[kk * 128:(kk + 1) * 128, :]))(),
            writes=("wk",))
        S.dma("gpsimd", (lambda kk=k: lambda e: e.dma_start(
            out=wo[:, kk * D:(kk + 1) * D], in_=wod[kk * 128:(kk + 1) * 128, :]))(),
            writes=("wo",))
    S.op("vector", lambda e: e.memset(wvp, 0.0), writes=("wvp",))
    for k in range(8):
        S.dma("gpsimd", (lambda kk=k: lambda e: e.dma_start(
            out=vstage, in_=wvd[kk * 128:(kk + 1) * 128, :]))(),
            reads=("vstage_used",), writes=("vstage",))
        S.op("vector", (lambda kk=k: lambda e: e.tensor_copy(
            out=wvp.rearrange("p (K h v) -> p K h v", K=8, h=NH)
                [:, kk, :, 0:HD],
            in_=vstage.rearrange("p (h v) -> p h v", h=NH)))(),
            reads=("vstage", "wvp"), writes=("wvp", "vstage_used"))
    for (dst, srcd, key) in ((bq8_c, bqd, "bq8_c"), (bk_c, bkd, "bk_c")):
        for j in range(8):
            S.dma("sync", (lambda dst=dst, srcd=srcd, jj=j: lambda e: e.dma_start(
                out=dst[:, jj:jj + 1],
                in_=srcd[jj * 128:(jj + 1) * 128].rearrange("(p one) -> p one", one=1)))(),
                writes=(key,))
    S.op("vector", lambda e: e.tensor_scalar(
        out=bq8_c, in0=bq8_c, scalar1=0.125, scalar2=0.0,
        op0=ALU.mult, op1=ALU.add),
        reads=("bq8_c",), writes=("bq8_c",))
    S.op("vector", lambda e: e.memset(rowbuf, 0.0), writes=("rowv",))
    S.dma("sync", lambda e: e.dma_start(
        out=ybuf[0:1, 0:D], in_=bvd.rearrange("(one d) -> one d", one=1)),
        writes=("bvrow",))
    for h in range(NH):
        S.op("vector", (lambda hh=h: lambda e: e.tensor_copy(
            out=rowbuf[0:1, hh * VW: hh * VW + HD],
            in_=ybuf[0:1, hh * HD:(hh + 1) * HD]))(),
            reads=("rowv", "bvrow"), writes=("rowv",))
        S.op("vector", (lambda hh=h: lambda e: e.memset(
            rowbuf[:, hh * VW + HD: hh * VW + HD + 1], 1.0))(),
            reads=("rowv",), writes=("rowv",))
    for seg in range(3):
        w0 = seg * 384
        S.op("tensor", (lambda w0=w0: lambda e: e.matmul(
            ps_ow[:, 0:384], ones_row, rowbuf[:, w0:w0 + 384],
            start=True, stop=True))(),
            reads=("rowv", "ones"), writes=("ps_ow",))
        S.pe_fence("ps_ow")
        S.op("vector", (lambda w0=w0: lambda e: e.tensor_copy(
            out=bvprep[:, w0:w0 + 384], in_=ps_ow[:, 0:384]))(),
            reads=("ps_ow",), writes=("bvprep",))
    S.dma("sync", lambda e: e.dma_start(
        out=rowbuf[:, 0:D], in_=bod.rearrange("(one d) -> one d", one=1)),
        reads=("bvprep",), writes=("rowo",))
    for hh in range(2):
        S.op("tensor", (lambda hh=hh: lambda e: e.matmul(
            ps_ow[:, 0:512], ones_row, rowbuf[:, hh * 512:(hh + 1) * 512],
            start=True, stop=True))(),
            reads=("rowo", "ones"), writes=("ps_ow",))
        S.pe_fence("ps_ow")
        S.op("vector", (lambda hh=hh: lambda e: e.tensor_copy(
            out=borep[:, hh * 512:(hh + 1) * 512], in_=ps_ow[:, 0:512]))(),
            reads=("ps_ow",), writes=("borep",))
    def emit_qkv(sb_i):
        slot = sb_i % 3
        for r in range(2):
            rt = sb_i * 2 + r
            p = rt % 2
            xtp = xt[:, p * D:(p + 1) * D]
            kx = "xt%d" % p
            S.dma("sync", (lambda rr=rt, xtp=xtp: lambda e: e.dma_start(
                out=xtp, in_=xd[rr * 128:(rr + 1) * 128, :]))(),
                writes=(kx,))
            emit_ln(S, stat, kx, xtp,
                    xlnb[:, p * D:(p + 1) * D], "xlnb%d" % p, par=p)
            pstr = ps_tr
            for k in range(8):
                S.op("tensor", (lambda kk=k, pstr=pstr, p=p: lambda e: e.transpose(
                    pstr[:, kk * 128:(kk + 1) * 128],
                    xlnb[:, p * D + kk * 128: p * D + (kk + 1) * 128], ident))(),
                    reads=("xlnb%d" % p, "ident"), writes=("pstrb0",))
            S.op("vector", (lambda rr=r, pstr=pstr: lambda e: e.tensor_copy(
                out=xcm_s.rearrange("p (k s) -> p k s", k=8)
                    [:, :, rr * 128:(rr + 1) * 128],
                in_=pstr.rearrange("p (k c) -> p k c", k=8)))(),
                reads=("pstrb0",), writes=("xcm_s%d" % r,))
        xcm_keys = ("xcm_s0", "xcm_s1")
        for (wmat, wkey, bcol, bkey, dst, dkey, ost) in (
                (wqs, "wqs", bq8_c, "bq8_c", qcm, "qcm", 1024),
                (wk, "wk", bk_c, "bk_c", kcm, "kcm", 768)):
            for oc in range(8):
                bank = oc % 4
                psx = ps_mm[:, bank * 512: bank * 512 + 256]
                for k in range(8):
                    S.op("tensor", (lambda kk=k, occ=oc, psx=psx, wmat=wmat:
                                    lambda e: e.matmul(
                        psx, wmat[:, kk * D + occ * 128: kk * D + (occ + 1) * 128],
                        xcm_s[:, kk * 256:(kk + 1) * 256],
                        start=(kk == 0), stop=(kk == 7)))(),
                        reads=xcm_keys + (wkey,), writes=("ps_mm%d" % bank,))
                S.pe_fence("ps_mm%d" % bank)
                S.op("vector", (lambda occ=oc, psx=psx, dst=dst, bcol=bcol,
                                slot=slot, ost=ost: lambda e: e.tensor_scalar(
                    out=dst[:, occ * ost + slot * 256: occ * ost + (slot + 1) * 256],
                    in0=psx, scalar1=bcol[:, occ:occ + 1], scalar2=0.0,
                    op0=ALU.add, op1=ALU.add))(),
                    reads=("ps_mm%d" % bank, bkey),
                    writes=(dkey + "%d" % slot,))
                if ost == 1024 and slot == 0:
                    S.op("vector", (lambda occ=oc, psx=psx, dst=dst, bcol=bcol:
                                    lambda e: e.tensor_scalar(
                        out=dst[:, occ * 1024 + 768: occ * 1024 + 1024],
                        in0=psx, scalar1=bcol[:, occ:occ + 1], scalar2=0.0,
                        op0=ALU.add, op1=ALU.add))(),
                        reads=("ps_mm%d" % bank, bkey),
                        writes=("qcm0",))
        for r in range(2):
            vt = (sb_i * 2 + r) % 6
            for seg in range(3):
                bank = seg
                psx = ps_mm[:, bank * 512: bank * 512 + 384]
                for k in range(8):
                    S.op("tensor", (lambda kk=k, rr=r, sg=seg, psx=psx:
                                    lambda e: e.matmul(
                        psx, xcm_s[:, kk * 256 + rr * 128: kk * 256 + (rr + 1) * 128],
                        wvp[:, kk * VPAD + sg * 384: kk * VPAD + (sg + 1) * 384],
                        start=(kk == 0), stop=(kk == 7)))(),
                        reads=xcm_keys + ("wvp",), writes=("ps_mm%d" % bank,))
                S.pe_fence("ps_mm%d" % bank)
                S.op("vector", (lambda vt=vt, sg=seg, psx=psx: lambda e:
                                e.scalar_tensor_tensor(
                    out=vrm[:, vt * VPAD + sg * 384: vt * VPAD + (sg + 1) * 384],
                    in0=psx, scalar=1.0,
                    in1=bvprep[:, sg * 384:(sg + 1) * 384],
                    op0=ALU.mult, op1=ALU.add))(),
                    reads=("ps_mm%d" % bank, "bvprep"), writes=("vrm%d" % vt,))

    def finalize_block(bi):
        cnt = 1.0 if bi == 0 else (1.0 / 3.0 if bi == 14 else 0.5)
        for sub in range(2):
            st = (2 * bi + sub) % 6
            rt = 2 * bi + sub
            p = rt % 2
            S.op("vector", (lambda st=st, cnt=cnt: lambda e: e.tensor_scalar(
                out=mbf, in0=merged[:, st * D:(st + 1) * D],
                scalar1=cnt, scalar2=0.0, op0=ALU.mult, op1=ALU.add))(),
                reads=("mg%d" % st,), writes=("mbf",))
            pstr = ps_tr
            for k in range(8):
                S.op("tensor", (lambda kk=k, pstr=pstr: lambda e: e.transpose(
                    pstr[:, kk * 128:(kk + 1) * 128],
                    mbf[:, kk * 128:(kk + 1) * 128], ident))(),
                    reads=("mbf", "ident"), writes=("pstrb0",))
            S.op("vector", (lambda pstr=pstr, sub=sub: lambda e: e.tensor_copy(
                out=mT[:, sub * D:(sub + 1) * D], in_=pstr))(),
                reads=("pstrb0",), writes=("mT%d" % sub,))
            S.op("vector", (lambda st=st: lambda e: e.memset(
                merged[:, st * D:(st + 1) * D], 0.0))(),
                reads=(), writes=("mg%d" % st,))
            xtp = xt[:, p * D:(p + 1) * D]
            S.dma("gpsimd", (lambda rr=rt, xtp=xtp: lambda e: e.dma_start(
                out=xtp, in_=xd[rr * 128:(rr + 1) * 128, :]))(),
                writes=("xt%d" % p,))
            for oh in range(2):
                bank = oh
                psx = ps_mm[:, bank * 512:(bank + 1) * 512]
                for k in range(8):
                    S.op("tensor", (lambda kk=k, ohh=oh, sub=sub, psx=psx:
                                    lambda e: e.matmul(
                        psx, mT[:, sub * D + kk * 128: sub * D + (kk + 1) * 128],
                        wo[:, kk * D + ohh * 512: kk * D + ohh * 512 + 512],
                        start=(kk == 0), stop=(kk == 7)))(),
                        reads=("mT%d" % sub, "wo"), writes=("ps_mm%d" % bank,))
                S.pe_fence("ps_mm%d" % bank)
                S.op("vector", (lambda ohh=oh, psx=psx: lambda e:
                                e.scalar_tensor_tensor(
                    out=obuf[:, ohh * 512:(ohh + 1) * 512], in0=psx, scalar=1.0,
                    in1=borep[:, ohh * 512:(ohh + 1) * 512],
                    op0=ALU.mult, op1=ALU.add))(),
                    reads=("ps_mm%d" % bank, "borep"), writes=("obuf",))
            ybp = ybuf[:, p * D:(p + 1) * D]
            S.op("vector", (lambda ybp=ybp, xtp=xtp: lambda e: e.tensor_tensor(
                out=ybp, in0=obuf, in1=xtp, op=ALU.add))(),
                reads=("obuf", "xt%d" % p), writes=("yb%d" % p,))
            S.dma("scalar", (lambda rr=rt, ybp=ybp: lambda e: e.dma_start(
                out=yd[rr * 128:(rr + 1) * 128, :], in_=ybp))(),
                reads=("yb%d" % p,), writes=("ydma",))

    emit_qkv(0)
    emit_qkv(1)

    def sp(w, h, tp3, tp2):
        oc = h // 2
        off = (h % 2) * 64
        epbp = epb[:, tp2 * 2048:(tp2 + 1) * 2048]
        S.dma("scalar", (lambda hh=h, epbp=epbp: lambda e: e.dma_start(
            out=epbp, in_=epbTd[:, hh * 2048:(hh + 1) * 2048]))(),
            reads=("epbTd0", "epbTd1", "epbTd2", "epbTd3",
                   "epb_used%d" % tp2), writes=("epb%d" % tp2,))
        score_reads = tuple({"qcm%d" % ((w * 256) % 768 // 256),
                             "qcm%d" % ((w * 256 + 256) % 768 // 256)} |
                            {"kcm%d" % (((w * 256 + kt * 128) % 768) // 256)
                             for kt in range(4)})
        qs = (w * 256) % 768
        for kt in range(4):
            ks = (w * 256 + kt * 128) % 768
            S.op("tensor", (lambda kt=kt, ks=ks, qs=qs, oc=oc,
                            off=off: lambda e: e.matmul(
                ps_mm[:, kt * 512:(kt + 1) * 512],
                kcm[off:off + 64, oc * 768 + ks: oc * 768 + ks + 128],
                qcm[off:off + 64, oc * 1024 + qs: oc * 1024 + qs + 512],
                start=True, stop=True))(),
                reads=score_reads, writes=("ps_mm%d" % kt,))
        for kp in range(2):
            est = estage[:, tp2 * 1024:(tp2 + 1) * 1024]
            S.op("scalar", (lambda kp=kp, est=est: lambda e: e.activation(
                out=est, in_=ps_mm[:, kp * 1024:(kp + 1) * 1024],
                func=AF.Exp))(),
                reads=("ps_mm%d" % (2 * kp), "ps_mm%d" % (2 * kp + 1)),
                writes=("estage%d" % tp2,))
            S.op("vector" if kp == 0 else "gpsimd",
                 (lambda kp=kp, tp3=tp3, est=est, epbp=epbp:
                            lambda e: e.tensor_tensor(
                out=probsT[:, tp3 * 2048 + kp * 1024: tp3 * 2048 + (kp + 1) * 1024],
                in0=est, in1=epbp[:, kp * 1024:(kp + 1) * 1024], op=ALU.mult))(),
                reads=("estage%d" % tp2, "epb%d" % tp2),
                writes=("probsT%d_%d" % (tp3, 2 * kp),
                        "probsT%d_%d" % (tp3, 2 * kp + 1),
                        "ps_mm%d" % (2 * kp), "ps_mm%d" % (2 * kp + 1),
                        "epb_used%d" % tp2))

    def owm(w, h, tp3, tp2):
        psow = ps_ow[0:65, tp2 * 512:(tp2 + 1) * 512]
        owb = owbuf[:, tp2 * 512:(tp2 + 1) * 512]
        pstr = (ps_tr if tp2 == 0 else ps_owt)[:, 0:264]
        trk = "pstrb%d" % tp2
        rd = rden[:, tp2 * 4:(tp2 + 1) * 4]
        for kt in range(4):
            vt = ((w * 256 + kt * 128) // 128) % 6
            S.op("tensor", (lambda kt=kt, vt=vt, h=h, tp3=tp3,
                            psow=psow: lambda e: e.matmul(
                psow,
                vrm[:, vt * VPAD + h * VW: vt * VPAD + h * VW + 65],
                probsT[:, tp3 * 2048 + kt * 512: tp3 * 2048 + (kt + 1) * 512],
                start=(kt == 0), stop=(kt == 3)))(),
                reads=("probsT%d_%d" % (tp3, kt), "vrm%d" % vt),
                writes=("ps_ow%d" % tp2,))
        S.op("scalar", (lambda psow=psow, owb=owb: lambda e: e.activation(
            out=owb[0:65, :], in_=psow, func=AF.Copy, scale=1.0))(),
            reads=("ps_ow%d" % tp2,), writes=("owb%d" % tp2, "ps_ow%d" % tp2))
        for qb in range(4):
            S.op("tensor", (lambda qb=qb, owb=owb, pstr=pstr: lambda e: e.transpose(
                pstr[:, qb * 66: qb * 66 + 65],
                owb[0:65, qb * 128:(qb + 1) * 128], ident[0:65, 0:65]))(),
                reads=("owb%d" % tp2, "ident"), writes=(trk,))
        S.op("vector", (lambda pstr=pstr, rd=rd: lambda e: e.reciprocal(
            out=rd,
            in_=pstr[:, 0:264].rearrange("p (b c) -> p b c", b=4)[:, :, 64]))(),
            reads=(trk,), writes=("rden%d" % tp2,))
        if w == NW - 1:
            S.op("vector", (lambda rd=rd: lambda e: e.tensor_scalar(
                out=rd, in0=rd, scalar1=2.0, scalar2=0.0,
                op0=ALU.mult, op1=ALU.add))(),
                reads=("rden%d" % tp2,), writes=("rden%d" % tp2,))
        for qb in range(4):
            stt_ = (w * 2 + qb) % 6
            S.op("vector", (lambda qb=qb, st=stt_, h=h, pstr=pstr, rd=rd:
                            lambda e: e.scalar_tensor_tensor(
                out=merged[:, st * D + h * HD: st * D + (h + 1) * HD],
                in0=pstr[:, qb * 66: qb * 66 + 64],
                scalar=rd[:, qb:qb + 1],
                in1=merged[:, st * D + h * HD: st * D + (h + 1) * HD],
                op0=ALU.mult, op1=ALU.add))(),
                reads=(trk, "rden%d" % tp2, "mg%d" % stt_),
                writes=("mg%d" % stt_,))

    pend = []
    for w in range(NW):
        if 1 <= w and w + 1 <= 15:
            emit_qkv(w + 1)
        for h in range(NH):
            t = w * NH + h
            sp(w, h, t % 3, t % 2)
            pend.append((w, h, t % 3, t % 2))
            if len(pend) > 2:
                pw, phh, p3, p2 = pend.pop(0)
                owm(pw, phh, p3, p2)
                if phh == NH - 1:
                    finalize_block(pw)
    for (pw, phh, p3, p2) in pend:
        owm(pw, phh, p3, p2)
        if phh == NH - 1:
            finalize_block(pw)
    finalize_block(15)


def build_l0attn():
    nc = bass.Bass("TRN2", target_bir_lowering=False, debug=False)
    xd = nc.dram_tensor("x", [S_SEQ, D], F32, kind="ExternalInput").ap()
    ww = {}
    for nm, shp in (("Wq", [D, D]), ("bq", [D]), ("Wk", [D, D]), ("bk", [D]),
                    ("Wv", [D, D]), ("bv", [D]), ("Wo", [D, D]), ("bo", [D])):
        ww[nm] = nc.dram_tensor(nm, shp, F32, kind="ExternalInput").ap()
    pbd = nc.dram_tensor("pos_bias", [NH, 512, 512], F32, kind="ExternalInput").ap()
    onesd = nc.dram_tensor("ones_row", [1, 128], F32, kind="ExternalInput").ap()
    identd = nc.dram_tensor("ident", [128, 128], F32, kind="ExternalInput").ap()
    yd = nc.dram_tensor("y", [S_SEQ, D], F32, kind="ExternalOutput").ap()
    epbTd = nc.dram_tensor("epbT_scr", [128, NH * 2048], BF16, kind="Internal").ap()
    with ExitStack() as octx:
        block = octx.enter_context(nc.Block())
        S = Sched(nc)
        cst = _emit_consts(S, nc, octx, onesd, identd)
        with ExitStack() as c0:
            sb, psb = _mk_scoped(nc, c0, "p_")
            emit_l0attn_prep_epb(S, sb, psb, pbd, epbTd, cst)
        S.stage_barrier()
        with ExitStack() as c1:
            sb, psb = _mk_scoped(nc, c1, "m_")
            emit_l0attn_main(S, sb, psb, xd, ww, epbTd, yd, cst)
        S.barrier_on("vector", "ydma")
        S.finalize(octx, block)
    return nc


def run_l0attn(nc, x_full, d):
    c = _consts()
    in_maps = []
    for b in range(8):
        in_maps.append({
            "x": np.ascontiguousarray(x_full[b]),
            "Wq": d["L0_attn_Wq"], "bq": d["L0_attn_bq"],
            "Wk": d["L0_attn_Wk"], "bk": d["L0_attn_bk"],
            "Wv": d["L0_attn_Wv"], "bv": d["L0_attn_bv"],
            "Wo": d["L0_attn_Wo"], "bo": d["L0_attn_bo"],
            "pos_bias": d["L0_attn_pos_bias"],
            "ones_row": c["ones_row"], "ident": c["ident"]})
    res = run_bass_kernel_spmd(nc, in_maps, core_ids=list(range(8)))
    return np.stack([r["y"] for r in res.results], axis=0)


_NCS = None


_FUSED = None


def build_fused():
    nc = bass.Bass("TRN2", target_bir_lowering=False, debug=False)
    x0d = nc.dram_tensor("x0", [S_SEQ, D], F32, kind="ExternalInput").ap()
    win = {}
    for nm, shp in (
            ("L0_attn_Wq", [D, D]), ("L0_attn_bq", [D]),
            ("L0_attn_Wk", [D, D]), ("L0_attn_bk", [D]),
            ("L0_attn_Wv", [D, D]), ("L0_attn_bv", [D]),
            ("L0_attn_Wo", [D, D]), ("L0_attn_bo", [D]),
            ("L0_attn_pos_bias", [NH, 512, 512]),
            ("L0_ffn_w_gate", [INNER, D]), ("L0_ffn_w_up", [INNER, D]),
            ("L0_ffn_w_down", [D, INNER]),
            ("L1_attn_a_logit", [D]), ("L1_attn_W_dx", [1040, D]),
            ("L1_attn_b_dx", [D]), ("L1_attn_W_x", [1040, D]),
            ("L1_attn_b_x", [D]), ("L1_attn_W_gx", [D, D]),
            ("L1_attn_b_gx", [D]), ("L1_attn_W_ga", [D, D]),
            ("L1_attn_b_ga", [D]), ("L1_attn_W_o", [D, D]),
            ("L1_attn_b_o", [D]), ("L1_ffn_weight", [D, D]),
            ("L1_ffn_bias", [D])):
        win[nm] = nc.dram_tensor(nm, shp, F32, kind="ExternalInput").ap()
    petd = nc.dram_tensor("peT", [16, S_SEQ], F32, kind="ExternalInput").ap()
    onesd = nc.dram_tensor("ones_row", [1, 128], F32, kind="ExternalInput").ap()
    identd = nc.dram_tensor("ident", [128, 128], F32, kind="ExternalInput").ap()
    yd = nc.dram_tensor("y", [S_SEQ, D], F32, kind="ExternalOutput").ap()
    xs = [nc.dram_tensor("xs%d" % i, [S_SEQ, D], F32, kind="Internal").ap()
          for i in range(5)]
    gTd = nc.dram_tensor("gT_scr", [128, 8 * INNER], BF16, kind="Internal").ap()
    uTd = nc.dram_tensor("uT_scr", [128, 8 * INNER], BF16, kind="Internal").ap()
    dTd = nc.dram_tensor("dT_scr", [128, 32 * D], BF16, kind="Internal").ap()
    wsbd = nc.dram_tensor("wsb_scr", [128, 3], F32, kind="Internal").ap()
    w8Td = nc.dram_tensor("w8T_scr", [128, 8 * D], BF16, kind="Internal").ap()
    epbTd = nc.dram_tensor("epbT_scr", [128, NH * 2048], BF16, kind="Internal").ap()

    wwB = {"Wq": win["L0_attn_Wq"], "bq": win["L0_attn_bq"],
           "Wk": win["L0_attn_Wk"], "bk": win["L0_attn_bk"],
           "Wv": win["L0_attn_Wv"], "bv": win["L0_attn_bv"],
           "Wo": win["L0_attn_Wo"], "bo": win["L0_attn_bo"]}
    wwD = {"a_logit": win["L1_attn_a_logit"], "W_dx": win["L1_attn_W_dx"],
           "b_dx": win["L1_attn_b_dx"], "W_x": win["L1_attn_W_x"],
           "b_x": win["L1_attn_b_x"], "W_gx": win["L1_attn_W_gx"],
           "b_gx": win["L1_attn_b_gx"], "W_ga": win["L1_attn_W_ga"],
           "b_ga": win["L1_attn_b_ga"], "W_o": win["L1_attn_W_o"],
           "b_o": win["L1_attn_b_o"]}

    with ExitStack() as octx:
        block = octx.enter_context(nc.Block())
        S = Sched(nc)
        cst = _emit_consts(S, nc, octx, onesd, identd)
        with ExitStack() as c0:
            sb, psb = _mk_scoped(nc, c0, "pA_")
            emit_l0ffn_prep(S, sb, psb, win["L0_ffn_w_gate"], win["L0_ffn_w_up"],
                            win["L0_ffn_w_down"], gTd, uTd, dTd, wsbd, cst)
            sb, psb = _mk_scoped(nc, c0, "pC_")
            emit_l1ffn_prep(S, sb, psb, win["L1_ffn_weight"], w8Td, cst)
            sb, psb = _mk_scoped(nc, c0, "pB_")
            emit_l0attn_prep_epb(S, sb, psb, win["L0_attn_pos_bias"], epbTd, cst)
        S.stage_barrier()
        with ExitStack() as c1:
            sb, psb = _mk_scoped(nc, c1, "a1_")
            emit_l0ffn_main(S, sb, psb, x0d, xs[0], gTd, uTd, dTd, wsbd, cst)
        S.stage_barrier()
        with ExitStack() as c2:
            sb, psb = _mk_scoped(nc, c2, "b_")
            emit_l0attn_main(S, sb, psb, xs[0], wwB, epbTd, xs[1], cst)
        S.stage_barrier()
        with ExitStack() as c3:
            sb, psb = _mk_scoped(nc, c3, "a2_")
            emit_l0ffn_main(S, sb, psb, xs[1], xs[2], gTd, uTd, dTd, wsbd, cst)
        S.stage_barrier()
        with ExitStack() as c4:
            sb, psb = _mk_scoped(nc, c4, "c1_")
            emit_l1ffn_main(S, sb, psb, xs[2], win["L1_ffn_bias"], w8Td, xs[3], cst)
        S.stage_barrier()
        with ExitStack() as c5:
            sb, psb = _mk_scoped(nc, c5, "d_")
            emit_l1attn(S, sb, psb, xs[3], wwD, petd, xs[4], cst)
        S.stage_barrier()
        with ExitStack() as c6:
            sb, psb = _mk_scoped(nc, c6, "c2_")
            emit_l1ffn_main(S, sb, psb, xs[4], win["L1_ffn_bias"], w8Td, yd, cst)
        S.barrier_on("vector", "ydma")
        S.finalize(octx, block)
    return nc


def kernel(**inputs):
    global _FUSED
    d = inputs
    if _FUSED is None:
        _FUSED = build_fused()
    nc = _FUSED
    c = _consts()
    pet = _pe_table()
    names = ["L0_attn_Wq", "L0_attn_bq", "L0_attn_Wk", "L0_attn_bk",
             "L0_attn_Wv", "L0_attn_bv", "L0_attn_Wo", "L0_attn_bo",
             "L0_attn_pos_bias", "L0_ffn_w_gate", "L0_ffn_w_up",
             "L0_ffn_w_down", "L1_attn_a_logit", "L1_attn_W_dx",
             "L1_attn_b_dx", "L1_attn_W_x", "L1_attn_b_x", "L1_attn_W_gx",
             "L1_attn_b_gx", "L1_attn_W_ga", "L1_attn_b_ga", "L1_attn_W_o",
             "L1_attn_b_o", "L1_ffn_weight", "L1_ffn_bias"]
    wmap = {nm: np.ascontiguousarray(np.asarray(d[nm], dtype=np.float32))
            for nm in names}
    x_full = np.asarray(d["L0_attn_hidden_states"], dtype=np.float32)
    in_maps = []
    for b in range(8):
        m = {"x0": np.ascontiguousarray(x_full[b]), "peT": pet,
             "ones_row": c["ones_row"], "ident": c["ident"]}
        m.update(wmap)
        in_maps.append(m)
    res = run_bass_kernel_spmd(nc, in_maps, core_ids=list(range(8)))
    return np.stack([r["y"] for r in res.results], axis=0)


# version 42 (final)
# speedup vs baseline: 1.1118x; 1.1118x over previous
import numpy as np
from contextlib import ExitStack

import concourse.bass as bass
import concourse.mybir as mybir
from concourse.bass_utils import run_bass_kernel_spmd

F32 = mybir.dt.float32
BF16 = mybir.dt.bfloat16
I32 = mybir.dt.int32
F8 = mybir.dt.float8e4
AF = mybir.ActivationFunctionType
ALU = mybir.AluOpType
AX = mybir.AxisListType

S_SEQ = 4096
D = 1024
NRT = S_SEQ // 128

ENGS = ("tensor", "vector", "scalar", "sync", "gpsimd")


class Sched:

    def __init__(self, nc, serial=False):
        self.nc = nc
        self.serial = serial
        self.comp_sem = {e: "c_" + e for e in ENGS}
        self.dma_sem = {e: "d_" + e for e in ENGS}
        self.cnt = {}
        self.q = {e: [] for e in ENGS}
        self.res = {}
        self.barrier_toks = []

    def _bump(self, sem, inc):
        self.cnt[sem] = self.cnt.get(sem, 0) + inc
        return self.cnt[sem]

    def _emit(self, eng, fn, reads, writes, sem, inc, is_dma):
        toks = []

        def need(tok):
            if tok is None:
                return
            if tok not in toks:
                toks.append(tok)

        for k in reads:
            d = self.res.get(k)
            if d:
                need(d[0])
        for k in writes:
            d = self.res.get(k)
            if d:
                need(d[0])
                for r in d[1]:
                    need(r)
        v = self._bump(sem, inc)
        qidx = len(self.q[eng])
        tok = (eng, qidx, sem, v, is_dma)
        self.q[eng].append([toks, fn, sem, inc])
        for k in writes:
            self.res[k] = [tok, []]
        for k in reads:
            d = self.res.setdefault(k, [None, []])
            d[1].append(tok)

    def op(self, eng, fn, reads=(), writes=()):
        if self.serial:
            reads = tuple(reads) + ("__SER__",)
            writes = tuple(writes) + ("__SER__",)
        self._emit(eng, fn, reads, writes, self.comp_sem[eng], 1, False)

    def dma(self, queue, fn, reads=(), writes=()):
        if self.serial:
            reads = tuple(reads) + ("__SER__",)
            writes = tuple(writes) + ("__SER__",)
        key = writes[0] if writes else "r"
        sem = "dq_%s_%s" % (queue, key)
        self._emit(queue, fn, reads, writes, sem, 16, True)

    def pe_fence(self, key):
        pass

    def barrier_on(self, eng, key):
        self.op(eng, lambda e: e.nop(), reads=(key,))

    def stage_barrier(self):
        for e in ENGS:
            self._emit(e, lambda e2: e2.drain(), ("__BARRIER__",),
                       ("__bar_%s" % e,), self.comp_sem[e], 1, False)
        toks = [self.res["__bar_%s" % e][0] for e in ENGS]
        dma_toks = [("dma", -1, s, v, True) for s, v in self.cnt.items()
                    if s.startswith("dq_") and v > 0]
        for e in ENGS:
            waits = [t for t in toks if t[0] != e] + dma_toks
            v = self._bump(self.comp_sem[e], 1)
            self.q[e].append([waits, lambda e2: e2.nop(), self.comp_sem[e], 1])
            self.res["__bar2_%s" % e] = [(e, len(self.q[e]) - 1,
                                          self.comp_sem[e], v, False), []]

    use_drains = False

    def finalize(self, ctx, block):
        marked = {e: set() for e in ENGS}
        if self.use_drains:
            for eng in ENGS:
                for entry in self.q[eng]:
                    for tok in entry[0]:
                        peng, qidx, sem, v, is_dma = tok
                        if (not is_dma) and peng != eng and qidx >= 0:
                            marked[peng].add(qidx)
        dnum = {}
        for e in ENGS:
            cnt = 0
            for qidx in range(len(self.q[e])):
                if qidx in marked[e]:
                    cnt += 1
                dnum[(e, qidx)] = cnt
        dr_sem = {e: "dr_" + e for e in ENGS}
        sems = {}
        names = set(self.cnt)
        for e in ENGS:
            if marked[e]:
                names.add(dr_sem[e])
        for name in sorted(names):
            sems[name] = ctx.enter_context(self.nc.semaphore(name))

        def make_body(engname):
            def body(eng):
                waited = {}
                for qidx, (toks, fn, sem, inc) in enumerate(self.q[engname]):
                    for tok in toks:
                        peng, pqidx, psem, pv, is_dma = tok
                        if (not is_dma) and peng != engname and pqidx >= 0 \
                                and pqidx in marked[peng]:
                            s, v = dr_sem[peng], dnum[(peng, pqidx)]
                        elif (not is_dma) and peng == engname:
                            if engname == "tensor":
                                continue
                            s, v = psem, pv
                        else:
                            s, v = psem, pv
                        if waited.get(s, -1) < v:
                            eng.wait_ge(sems[s], v)
                            waited[s] = v
                    fn(eng).then_inc(sems[sem], inc)
                    if qidx in marked[engname]:
                        eng.drain().then_inc(sems[dr_sem[engname]], 1)
            return body

        for e in ENGS:
            if self.q[e]:
                getattr(block, e)(make_body(e))



def emit_ln(S, stat, xt_key, xt_ap, out_ap, out_key, out_extra=(), par=0):
    bn6a, mva, sda, rsa, nb, eps = stat
    bn6 = bn6a[:, par * 12:(par + 1) * 12]
    mv = mva[:, par * 2:(par + 1) * 2]
    sd = sda[:, par:par + 1]
    rs = rsa[:, par:par + 1]
    sx = "%d" % par
    S.op("vector", (lambda a=bn6, b=xt_ap: lambda e: e.bn_stats(
        out=a[:, 0:6], in_=b[:, 0:512]))(),
        reads=(xt_key,), writes=("ln_bn6" + sx,))
    S.op("vector", (lambda a=bn6, b=xt_ap: lambda e: e.bn_stats(
        out=a[:, 6:12], in_=b[:, 512:1024]))(),
        reads=(xt_key,), writes=("ln_bn6b" + sx,))
    S.op("vector", (lambda a=mv, b=bn6: lambda e: e.bn_aggr(out=a, in_=b))(),
         reads=("ln_bn6" + sx, "ln_bn6b" + sx), writes=("ln_mv" + sx,))
    S.op("scalar", (lambda a=sd, b=mv: lambda e: e.activation(
        out=a, in_=b[:, 1:2], func=AF.Sqrt, bias=eps, scale=1.0))(),
        reads=("ln_mv" + sx,), writes=("ln_sd" + sx,))
    S.op("vector", (lambda a=rs, b=sd: lambda e: e.reciprocal(out=a, in_=b))(),
         reads=("ln_sd" + sx,), writes=("ln_rs" + sx,))
    S.op("vector", (lambda a=out_ap, b=xt_ap, c=mv, d=rs: lambda e: e.tensor_scalar(
        out=a, in0=b, scalar1=c[:, 0:1], scalar2=d, op0=ALU.subtract, op1=ALU.mult))(),
        reads=(xt_key, "ln_mv" + sx, "ln_rs" + sx), writes=(out_key,) + tuple(out_extra))


def emit_transpose_tile(S, in_ap, in_keys, ps_ap, ps_key, out_ap, out_key, ident,
                        nblk=8, evac_eng="vector"):
    for k in range(nblk):
        S.op("tensor",
             (lambda kk: lambda e: e.transpose(
                 ps_ap[:, kk * 128:(kk + 1) * 128],
                 in_ap[:, kk * 128:(kk + 1) * 128], ident))(k),
             reads=tuple(in_keys) + ("ident",), writes=(ps_key,))
    S.pe_fence(ps_key)
    S.op(evac_eng, (lambda a=out_ap, b=ps_ap: lambda e: e.tensor_copy(out=a, in_=b))(),
         reads=(ps_key,), writes=(out_key,))


def ln_stat_bufs(sb):
    return (sb("ln_bn6", [128, 24], F32), sb("ln_mv", [128, 4], F32),
            sb("ln_sd", [128, 2], F32), sb("ln_rs", [128, 2], F32),
            sb("ln_nb", [128, 1], F32), sb("ln_eps", [128, 1], F32))



def emit_l1ffn_prep(S, sb, psb, wd, w8Td, cst):
    ident = cst["ident"]
    wstage = sb("wstage", [128, 2 * D], F32)
    wclip = sb("wclip", [128, D], F32)
    w8 = sb("w8", [128, D], F8)
    w8b = sb("w8b", [128, D], BF16)
    w8T = sb("w8T", [128, 8 * D], BF16)
    ps_tr = psb("ps_tr", [128, D], BF16)
    for j in range(8):
        p = j % 2
        ws = wstage[:, p * D:(p + 1) * D]
        S.dma("sync", (lambda jj, ws=ws: lambda e: e.dma_start(
            out=ws, in_=wd[jj * 128:(jj + 1) * 128, :]))(j),
            writes=("wstage%d" % p,))
        S.op("vector", (lambda ws=ws: lambda e: e.tensor_scalar(
            out=wclip, in0=ws, scalar1=240.0, scalar2=-240.0,
            op0=ALU.min, op1=ALU.max))(),
            reads=("wstage%d" % p,), writes=("wclip",))
        S.op("vector", lambda e: e.tensor_copy(out=w8, in_=wclip),
             reads=("wclip",), writes=("w8",))
        S.op("vector", lambda e: e.tensor_copy(out=w8b, in_=w8),
             reads=("w8",), writes=("w8b",))
        for k in range(8):
            S.op("tensor", (lambda kk: lambda e: e.transpose(
                ps_tr[:, kk * 128:(kk + 1) * 128],
                w8b[:, kk * 128:(kk + 1) * 128], ident))(k),
                reads=("w8b", "ident"), writes=("ps_tr0",))
        S.pe_fence("ps_tr0")
        S.op("vector", (lambda jj: lambda e: e.tensor_copy(
            out=w8T.rearrange("p (k c) -> p k c", k=8)[:, :, jj * 128:(jj + 1) * 128],
            in_=ps_tr[:, 0:D].rearrange("p (k c) -> p k c", k=8)))(j),
            reads=("ps_tr0",), writes=("w8T",))
    S.dma("sync", lambda e: e.dma_start(out=w8Td, in_=w8T),
          reads=("w8T",), writes=("w8Td",))


def emit_l1ffn_main(S, sb, psb, xd, biasd, w8Td, yd, cst):
    ident = cst["ident"]
    ones_row = cst["ones_row"]
    w8T = sb("w8T", [128, 8 * D], BF16)
    brep05 = sb("brep05", [128, D], F32)
    bias_row = sb("bias_row", [1, D], F32)
    xt = sb("xt", [128, 6 * D], F32)
    xln = sb("xln", [128, 6 * D], F32)
    xq8 = sb("xq8", [128, 6 * D], F8)
    xqb = sb("xqb", [128, 6 * D], BF16)
    xqT = sb("xqT", [128, 6 * D], BF16)
    yt = sb("yt", [128, 6 * D], F32)
    ubuf = sb("ubuf", [128, 6 * D], F32)
    stat2 = (sb("ln2_bn6", [128, 3 * 24], F32), sb("ln2_mv", [128, 3 * 4], F32),
             sb("ln2_sd", [128, 3 * 2], F32), sb("ln2_rs", [128, 3 * 2], F32),
             sb("ln2_eps", [128, 1], F32))
    ps_tr = psb("ps_tr", [128, 2 * D], BF16)
    ps_mm = psb("ps_mm", [128, 2 * 512], F32)
    ps_bc = psb("ps_bc", [128, 512], F32)

    S.op("vector", (lambda a=stat2[4]: lambda e: e.memset(a, 1e-5))(), writes=("ln_eps",))
    S.dma("sync", lambda e: e.dma_start(out=w8T, in_=w8Td),
          reads=("w8Td",), writes=("w8T",))
    S.dma("sync", lambda e: e.dma_start(
        out=bias_row, in_=biasd.rearrange("(one d) -> one d", one=1)),
        writes=("bias_row",))
    for half in range(2):
        S.op("tensor", (lambda hh: lambda e: e.matmul(
            ps_bc, ones_row, bias_row[:, hh * 512:(hh + 1) * 512],
            start=True, stop=True))(half),
            reads=("ones", "bias_row", "brep05"), writes=("ps_bc",))
        S.pe_fence("ps_bc")
        S.op("vector", (lambda hh: lambda e: e.tensor_scalar(
            out=brep05[:, hh * 512:(hh + 1) * 512], in0=ps_bc,
            scalar1=0.5, scalar2=0.0, op0=ALU.mult, op1=ALU.add))(half),
            reads=("ps_bc",), writes=("brep05",))

    NP2 = NRT // 2

    def quantc(pr):
        p = pr % 3
        bn6, mv, sd, rs, eps = (stat2[0][:, p * 24:(p + 1) * 24],
                                stat2[1][:, p * 4:(p + 1) * 4],
                                stat2[2][:, p * 2:(p + 1) * 2],
                                stat2[3][:, p * 2:(p + 1) * 2], stat2[4])
        xtp = xt[:, p * 2 * D:(p + 1) * 2 * D]
        xlnp = xln[:, p * 2 * D:(p + 1) * 2 * D]
        xq8p = xq8[:, p * 2 * D:(p + 1) * 2 * D]
        xqbp = xqb[:, p * 2 * D:(p + 1) * 2 * D]
        kx = "xt%d" % p
        for s_ in range(2):
            S.dma("sync", (lambda rr=2 * pr + s_, s_=s_, xtp=xtp: lambda e: e.dma_start(
                out=xtp[:, s_ * D:(s_ + 1) * D],
                in_=xd[rr * 128:(rr + 1) * 128, :]))(),
                writes=(kx,) if s_ == 1 else (kx + "a",))
        for s_ in range(2):
            for g in range(2):
                S.op("vector", (lambda a=bn6, b=xtp, s_=s_, g=g: lambda e: e.bn_stats(
                    out=a[:, s_ * 12 + g * 6: s_ * 12 + (g + 1) * 6],
                    in_=b[:, s_ * D + g * 512: s_ * D + (g + 1) * 512]))(),
                    reads=(kx, kx + "a"), writes=("ln_bn6_%d_%d" % (p, s_),))
            S.op("vector", (lambda a=mv, b=bn6, s_=s_: lambda e: e.bn_aggr(
                out=a[:, s_ * 2:(s_ + 1) * 2], in_=b[:, s_ * 12:(s_ + 1) * 12]))(),
                reads=("ln_bn6_%d_%d" % (p, s_),), writes=("ln_mv%d" % p,))
        S.op("scalar", (lambda a=sd, b=mv: lambda e: e.activation(
            out=a, in_=b.rearrange("p (s two) -> p s two", two=2)[:, :, 1],
            func=AF.Sqrt, bias=eps, scale=1.0))(),
            reads=("ln_mv%d" % p, "ln_eps"), writes=("ln_sd%d" % p,))
        S.op("vector", (lambda a=rs, b=sd: lambda e: e.reciprocal(out=a, in_=b))(),
             reads=("ln_sd%d" % p,), writes=("ln_rs%d" % p,))
        for s_ in range(2):
            S.op("vector", (lambda a=xlnp, b=xtp, c=mv, d=rs, s_=s_:
                            lambda e: e.tensor_scalar(
                out=a[:, s_ * D:(s_ + 1) * D], in0=b[:, s_ * D:(s_ + 1) * D],
                scalar1=c[:, s_ * 2: s_ * 2 + 1], scalar2=d[:, s_:s_ + 1],
                op0=ALU.subtract, op1=ALU.mult))(),
                reads=(kx, kx + "a", "ln_mv%d" % p, "ln_rs%d" % p),
                writes=("xln%d" % p,))
        S.op("vector", (lambda xq8p=xq8p, xlnp=xlnp: lambda e: e.tensor_copy(
            out=xq8p, in_=xlnp))(),
            reads=("xln%d" % p,), writes=("xq8%d" % p,))
        S.op("vector", (lambda xqbp=xqbp, xq8p=xq8p: lambda e: e.tensor_copy(
            out=xqbp, in_=xq8p))(),
            reads=("xq8%d" % p,), writes=("xqb%d" % p,))

    def consumec(pr):
        p = pr % 3
        xtp = xt[:, p * 2 * D:(p + 1) * 2 * D]
        xqbp = xqb[:, p * 2 * D:(p + 1) * 2 * D]
        xqTp = xqT[:, p * 2 * D:(p + 1) * 2 * D]
        ytp = yt[:, p * 2 * D:(p + 1) * 2 * D]
        ubp = ubuf[:, p * 2 * D:(p + 1) * 2 * D]
        for s_ in range(2):
            pstr = ps_tr[:, s_ * D:(s_ + 1) * D]
            for k in range(8):
                S.op("tensor", (lambda kk=k, s_=s_, pstr=pstr, xqbp=xqbp:
                                lambda e: e.transpose(
                    pstr[:, kk * 128:(kk + 1) * 128],
                    xqbp[:, s_ * D + kk * 128: s_ * D + (kk + 1) * 128], ident))(),
                    reads=("xqb%d" % p, "ident"), writes=("ps_trm%d" % s_,))
            S.op("scalar", (lambda s_=s_, pstr=pstr, xqTp=xqTp: lambda e: e.activation(
                out=xqTp[:, s_ * D:(s_ + 1) * D], in_=pstr,
                func=AF.Copy, scale=1.0))(),
                reads=("ps_trm%d" % s_,), writes=("xqT%d" % p,))
        for s_ in range(2):
            for oh in range(2):
                pm = (2 * s_ + oh) % 2
                psmm = ps_mm[:, pm * 512:(pm + 1) * 512]
                for k in range(8):
                    S.op("tensor", (lambda kk=k, ohh=oh, s_=s_, xqTp=xqTp, psmm=psmm:
                                    lambda e: e.matmul(
                        psmm, xqTp[:, s_ * D + kk * 128: s_ * D + (kk + 1) * 128],
                        w8T[:, kk * D + ohh * 512: kk * D + ohh * 512 + 512],
                        start=(kk == 0), stop=(kk == 7)))(),
                        reads=("xqT%d" % p, "w8T"), writes=("psmm%d" % pm,))
                S.op("vector", (lambda ohh=oh, s_=s_, psmm=psmm, ubp=ubp:
                                lambda e: e.scalar_tensor_tensor(
                    out=ubp[:, s_ * D + ohh * 512: s_ * D + (ohh + 1) * 512],
                    in0=psmm, scalar=0.5,
                    in1=brep05[:, ohh * 512:(ohh + 1) * 512],
                    op0=ALU.mult, op1=ALU.add))(),
                    reads=("psmm%d" % pm, "brep05"), writes=("ubuf%d" % p,))
        S.op("vector", (lambda ytp=ytp, xtp=xtp, ubp=ubp: lambda e: e.tensor_tensor(
            out=ytp, in0=ubp, in1=xtp, op=ALU.add))(),
            reads=("ubuf%d" % p, "xt%d" % p, "xt%da" % p), writes=("yt%d" % p,))
        for s_ in range(2):
            S.dma("scalar", (lambda rr=2 * pr + s_, s_=s_, ytp=ytp:
                             lambda e: e.dma_start(
                out=yd[rr * 128:(rr + 1) * 128, :],
                in_=ytp[:, s_ * D:(s_ + 1) * D]))(),
                reads=("yt%d" % p,), writes=("ydma",))

    quantc(0)
    quantc(1)
    for pr in range(NP2):
        consumec(pr)
        if pr + 2 < NP2:
            quantc(pr + 2)
def _mk_scoped(nc, ctx, prefix):
    sb = lambda n, s, d: ctx.enter_context(nc.sbuf_tensor(prefix + n, s, d)).ap()
    psb = lambda n, s, d: ctx.enter_context(nc.psum_tensor(prefix + n, s, d)).ap()
    return sb, psb


def _emit_consts(S, nc, octx, onesd, identd):
    cst = {}
    cst["ones_row"] = octx.enter_context(nc.sbuf_tensor("cst_ones", [1, 128], F32)).ap()
    cst["ident"] = octx.enter_context(nc.sbuf_tensor("cst_ident", [128, 128], BF16)).ap()
    cst["identf"] = octx.enter_context(nc.sbuf_tensor("cst_identf", [128, 128], F32)).ap()
    S.dma("sync", (lambda a=cst["ones_row"]: lambda e: e.dma_start(out=a, in_=onesd))(),
          writes=("ones",))
    S.dma("gpsimd", (lambda a=cst["ident"]: lambda e: e.dma_start(out=a, in_=identd))(),
          writes=("ident",))
    S.dma("sync", (lambda a=cst["identf"]: lambda e: e.dma_start(out=a, in_=identd))(),
          writes=("identf",))
    return cst


def build_l1ffn():
    nc = bass.Bass("TRN2", target_bir_lowering=False, debug=False)
    xd = nc.dram_tensor("x", [S_SEQ, D], F32, kind="ExternalInput").ap()
    wd = nc.dram_tensor("w", [D, D], F32, kind="ExternalInput").ap()
    biasd = nc.dram_tensor("bias", [D], F32, kind="ExternalInput").ap()
    onesd = nc.dram_tensor("ones_row", [1, 128], F32, kind="ExternalInput").ap()
    identd = nc.dram_tensor("ident", [128, 128], F32, kind="ExternalInput").ap()
    yd = nc.dram_tensor("y", [S_SEQ, D], F32, kind="ExternalOutput").ap()
    w8Td = nc.dram_tensor("w8T_scr", [128, 8 * D], BF16, kind="Internal").ap()
    with ExitStack() as octx:
        block = octx.enter_context(nc.Block())
        S = Sched(nc)
        cst = _emit_consts(S, nc, octx, onesd, identd)
        with ExitStack() as c0:
            sb, psb = _mk_scoped(nc, c0, "p_")
            emit_l1ffn_prep(S, sb, psb, wd, w8Td, cst)
        S.stage_barrier()
        with ExitStack() as c1:
            sb, psb = _mk_scoped(nc, c1, "m_")
            emit_l1ffn_main(S, sb, psb, xd, biasd, w8Td, yd, cst)
        S.barrier_on("vector", "ydma")
        S.finalize(octx, block)
    return nc


_CONSTS = None


def _consts():
    global _CONSTS
    if _CONSTS is None:
        _CONSTS = {
            "ones_row": np.ones((1, 128), dtype=np.float32),
            "ident": np.eye(128, dtype=np.float32),
        }
    return _CONSTS


def run_l1ffn(nc, x_full, weight, bias):
    c = _consts()
    in_maps = [{"x": np.ascontiguousarray(x_full[b]), "w": weight, "bias": bias,
                "ones_row": c["ones_row"], "ident": c["ident"]} for b in range(8)]
    res = run_bass_kernel_spmd(nc, in_maps, core_ids=list(range(8)))
    return np.stack([r["y"] for r in res.results], axis=0)





CHUNK_RT = 4
NCHUNK = NRT // CHUNK_RT
INNER = 4096


def emit_l0ffn_prep(S, sb, psb, wgd, wud, wdd, gTd, uTd, dTd, wsbd, cst):
    ident = cst["ident"]
    ones_row = cst["ones_row"]
    dTq = sb("dTq", [128, 32 * D], BF16)
    wsb = sb("wsb", [128, 3], F32)
    ones_col = sb("ones_col", [128, 1], F32)
    ps_c = psb("ps_c", [128, 512], F32)
    shared_ps_tr = psb("ps_tr_sh", [128, 2 * D], BF16)

    S.op("vector", lambda e: e.memset(ones_col, 1.0), writes=("ones_col",))

    def prep_weight(wd_ap, R, Cw, widx, tag):
        big = sb(tag + "big", [128, 8 * 1024], BF16) if tag != "down" else None
        wstage = sb(tag + "wstage", [128, 2 * D], F32)
        i32b2 = sb(tag + "i32b", [128, 2 * D], I32)
        ternb2 = sb(tag + "ternb", [128, 2 * D], BF16)
        acc = sb(tag + "acc", [128, 1], F32)
        colx = sb(tag + "colx", [128, 2], F32)
        cell = sb(tag + "cell", [1, 4], F32)
        sinvb = sb(tag + "sinvb", [128, 1], F32)
        ps_tr = shared_ps_tr
        nrt_w = R // 128
        ncc = Cw // D
        S.op("vector", lambda e: e.memset(acc, 0.0), writes=(tag + "acc",))
        wbstage = ternb2
        for t in range(nrt_w * ncc):
            j, cc = divmod(t, ncc)
            p = t % 2
            wb_ = wbstage[:, p * D:(p + 1) * D]
            S.dma("gpsimd", (lambda jj=j, ccc=cc, wb_=wb_: lambda e: e.dma_start(
                out=wb_, in_=wd_ap[jj * 128:(jj + 1) * 128, ccc * D:(ccc + 1) * D]))(),
                writes=(tag + "wbst%d" % p,))
            S.op("vector", (lambda wb_=wb_: lambda e: e.tensor_reduce(
                out=colx[:, 0:1], in_=wb_, axis=AX.X, op=ALU.add,
                apply_absolute_value=True))(),
                reads=(tag + "wbst%d" % p,), writes=(tag + "colx0",))
            S.op("vector", lambda e: e.tensor_tensor(
                out=acc, in0=acc, in1=colx[:, 0:1], op=ALU.add),
                reads=(tag + "colx0", tag + "acc"), writes=(tag + "acc",))
        S.op("tensor", lambda e: e.matmul(ps_c[0:1, 0:1], acc, ones_col,
                                          start=True, stop=True),
             reads=(tag + "acc", "ones_col"), writes=("ps_c",))
        S.pe_fence("ps_c")
        S.op("vector", lambda e: e.tensor_copy(out=cell[:, 0:1], in_=ps_c[0:1, 0:1]),
             reads=("ps_c",), writes=(tag + "cell",))
        S.op("vector", (lambda n=float(R * Cw): lambda e: e.tensor_scalar(
            out=cell[:, 1:2], in0=cell[:, 0:1], scalar1=1.0 / n, scalar2=1e-5,
            op0=ALU.mult, op1=ALU.max))(),
            reads=(tag + "cell",), writes=(tag + "cellws",))
        S.op("vector", lambda e: e.reciprocal(out=cell[:, 2:3], in_=cell[:, 1:2]),
             reads=(tag + "cellws",), writes=(tag + "cellsinv",))
        S.op("tensor", lambda e: e.matmul(ps_c[:, 0:1], ones_row, cell[0:1, 1:2],
                                          start=True, stop=True),
             reads=(tag + "cellws", "ones"), writes=("ps_c",))
        S.pe_fence("ps_c")
        S.op("vector", (lambda wi=widx: lambda e: e.tensor_copy(
            out=wsb[:, wi:wi + 1], in_=ps_c[:, 0:1]))(),
            reads=("ps_c",), writes=("wsb%d" % widx,))
        S.op("tensor", lambda e: e.matmul(ps_c[:, 0:1], ones_row, cell[0:1, 2:3],
                                          start=True, stop=True),
             reads=(tag + "cellsinv", "ones", "wsb%d" % widx), writes=("ps_c",))
        S.pe_fence("ps_c")
        S.op("vector", lambda e: e.tensor_copy(out=sinvb, in_=ps_c[:, 0:1]),
             reads=("ps_c",), writes=(tag + "sinvb",))
        if tag == "down":
            S.op("vector", (lambda wi=widx: lambda e: e.tensor_scalar(
                out=wsb[:, wi:wi + 1], in0=wsb[:, wi:wi + 1], scalar1=0.5,
                scalar2=0.0, op0=ALU.mult, op1=ALU.add))(),
                reads=("wsb%d" % widx,), writes=("wsb%d" % widx,))
        for t in range(nrt_w * ncc):
            j, cc = divmod(t, ncc)
            p = t % 2
            ws_ = wstage[:, p * D:(p + 1) * D]
            S.dma("sync" if t % 2 == 0 else "scalar",
                  (lambda jj=j, ccc=cc, ws_=ws_: lambda e: e.dma_start(
                out=ws_, in_=wd_ap[jj * 128:(jj + 1) * 128, ccc * D:(ccc + 1) * D]))(),
                writes=(tag + "wstage%d" % p,))
            i32b = i32b2[:, p * D:(p + 1) * D]
            ternb = ternb2[:, p * D:(p + 1) * D]
            S.op("vector", (lambda ws_=ws_, i32b=i32b: lambda e: e.tensor_scalar(
                out=i32b, in0=ws_, scalar1=sinvb, scalar2=0.0,
                op0=ALU.mult, op1=ALU.add))(),
                reads=(tag + "wstage%d" % p, tag + "sinvb"),
                writes=(tag + "i32w%d" % p,))
            S.op("vector", (lambda i32b=i32b: lambda e: e.tensor_scalar(
                out=i32b, in0=i32b, scalar1=1, scalar2=-1,
                op0=ALU.min, op1=ALU.max))(),
                reads=(tag + "i32w%d" % p,), writes=(tag + "i32w%d" % p,))
            S.op("vector", (lambda i32b=i32b, ternb=ternb: lambda e: e.tensor_copy(
                out=ternb, in_=i32b))(),
                reads=(tag + "i32w%d" % p,), writes=(tag + "ternb%d" % p,))
            pp = t % 2
            pstr = ps_tr[:, pp * D:(pp + 1) * D]
            for k in range(8):
                S.op("tensor", (lambda kk=k, pstr=pstr, ternb=ternb:
                                lambda e: e.transpose(
                    pstr[:, kk * 128:(kk + 1) * 128],
                    ternb[:, kk * 128:(kk + 1) * 128], ident))(),
                    reads=(tag + "ternb%d" % p, "ident"), writes=("ppstr%d" % pp,))
            S.pe_fence("ppstr%d" % pp)
            if tag == "down":
                S.op("vector", (lambda jj=j, ccc=cc, pstr=pstr: lambda e: e.tensor_copy(
                    out=dTq.rearrange("p (K r) -> p K r", K=32)
                        [:, ccc * 8:(ccc + 1) * 8, jj * 128:(jj + 1) * 128],
                    in_=pstr.rearrange("p (K c) -> p K c", K=8)))(),
                    reads=("ppstr%d" % pp,), writes=("dTq",))
            else:
                jloc = j % 8
                obq, rr0 = jloc // 4, (jloc % 4) * 128
                S.op("vector", (lambda obq=obq, rr0=rr0, pstr=pstr:
                                lambda e: e.tensor_copy(
                    out=big.rearrange("p (o k r) -> p o k r", o=2, k=8)
                        [:, obq, :, rr0:rr0 + 128],
                    in_=pstr.rearrange("p (K c) -> p K c", K=8)))(),
                    reads=("ppstr%d" % pp,), writes=(tag + "bigprep",))
                if jloc == 7:
                    hh = j // 8
                    dst = gTd if tag == "g" else uTd
                    S.dma("sync", (lambda dst=dst, hh=hh: lambda e: e.dma_start(
                        out=dst[:, hh * 8192:(hh + 1) * 8192],
                        in_=big))(),
                        reads=(tag + "bigprep",), writes=(tag + "Td", tag + "bigprep"))

    prep_weight(wgd, INNER, D, 0, "g")
    prep_weight(wud, INNER, D, 1, "u")
    prep_weight(wdd, D, INNER, 2, "down")
    S.dma("sync", lambda e: e.dma_start(out=dTd, in_=dTq),
          reads=("dTq",), writes=("dTd",))
    S.dma("sync", lambda e: e.dma_start(out=wsbd, in_=wsb),
          reads=("wsb0", "wsb1", "wsb2"), writes=("wsbd",))


def emit_l0ffn_main(S, sb, psb, xd, yd, gTd, uTd, dTd, wsbd, cst):
    ident = cst["ident"]
    big = sb("big", [128, CHUNK_RT * INNER], BF16)
    dTq = sb("dTq", [128, 32 * D], BF16)
    gsl = sb("gsl", [128, 2 * 8 * 512], BF16)
    usl = sb("usl", [128, 2 * 8 * 512], BF16)
    xqTc = sb("xqTc", [128, 2 * CHUNK_RT * D], BF16)
    i8x = sb("i8x", [128, D], mybir.dt.int8)
    i8g = sb("i8g", [128, 2 * INNER], mybir.dt.int8)
    qcol = sb("qcol", [128, 4], F32)
    q2b = sb("q2b", [128, 2 * INNER], BF16)
    guqT = sb("guqT", [128, 2 * INNER], BF16)
    xt = sb("xt", [128, 2 * D], F32)
    xqb1 = sb("xqb1", [128, D], BF16)
    gb = sb("gb", [128, 2 * 512], BF16)
    ub = sb("ub", [128, 2 * 512], BF16)
    stat = ln_stat_bufs(sb)
    qsg = sb("qsg", [128, 2 * CHUNK_RT], F32)
    qsu = sb("qsu", [128, 2 * CHUNK_RT], F32)
    colx = sb("colx", [128, 8], F32)
    qs2h = sb("qs2h", [128, 2], F32)
    wsb = sb("wsb", [128, 3], F32)
    ps_tr = psb("ps_tr", [128, 2 * D], BF16)
    ps_g = psb("ps_g", [128, 2 * 512], F32)
    ps_u = psb("ps_u", [128, 2 * 512], F32)
    ps_d = psb("ps_d", [128, 2 * 512], F32)

    S.op("vector", (lambda a=stat[5]: lambda e: e.memset(a, 1e-5))(), writes=("ln_eps",))
    S.dma("sync", lambda e: e.dma_start(out=wsb, in_=wsbd),
          reads=("wsbd",), writes=("wsb0", "wsb1", "wsb2"))
    S.dma("sync", lambda e: e.dma_start(out=dTq, in_=dTd),
          reads=("dTd",), writes=("dTq",))

    def ph1(ch):
        cp = ch % 2
        for r in range(CHUNK_RT):
            rt = ch * CHUNK_RT + r
            p = rt % 2
            xtp = xt[:, p * D:(p + 1) * D]
            kx = "xt%d" % p
            S.dma("sync", (lambda rr=rt, xtp=xtp: lambda e: e.dma_start(
                out=xtp, in_=xd[rr * 128:(rr + 1) * 128, :]))(),
                writes=(kx,))
            bn6a, mva, sda, rsa, nb, eps = stat
            lp = rt % 2
            bn6 = bn6a[:, lp * 12:(lp + 1) * 12]
            mv = mva[:, lp * 2:(lp + 1) * 2]
            sd = sda[:, lp:lp + 1]
            lx = "%d" % lp
            S.op("vector", (lambda a=bn6, b=xtp: lambda e: e.bn_stats(
                out=a[:, 0:6], in_=b[:, 0:512]))(), reads=(kx,), writes=("ln_bn6" + lx,))
            S.op("vector", (lambda a=bn6, b=xtp: lambda e: e.bn_stats(
                out=a[:, 6:12], in_=b[:, 512:1024]))(), reads=(kx,), writes=("ln_bn6b" + lx,))
            S.op("vector", (lambda mv=mv, bn6=bn6: lambda e: e.bn_aggr(
                out=mv, in_=bn6))(),
                reads=("ln_bn6" + lx, "ln_bn6b" + lx), writes=("ln_mv" + lx,))
            S.op("scalar", (lambda sd=sd, mv=mv: lambda e: e.activation(
                out=sd, in_=mv[:, 1:2], func=AF.Sqrt, bias=eps, scale=1.0))(),
                reads=("ln_mv" + lx, "ln_eps"), writes=("ln_sd" + lx,))
            S.op("vector", (lambda b=xtp: lambda e: e.tensor_reduce(
                out=colx[:, 0:1], in_=b, axis=AX.X, op=ALU.max))(),
                reads=(kx,), writes=("colx0",))
            S.op("vector", (lambda b=xtp: lambda e: e.tensor_reduce(
                out=colx[:, 1:2], in_=b, axis=AX.X, op=ALU.min))(),
                reads=(kx,), writes=("colx1",))
            S.op("vector", (lambda mv=mv: lambda e: e.tensor_tensor(
                out=colx[:, 2:3], in0=colx[:, 0:1], in1=mv[:, 0:1], op=ALU.subtract))(),
                reads=("colx0", "ln_mv" + lx), writes=("colx2",))
            S.op("vector", (lambda mv=mv: lambda e: e.tensor_tensor(
                out=colx[:, 3:4], in0=mv[:, 0:1], in1=colx[:, 1:2], op=ALU.subtract))(),
                reads=("colx1", "ln_mv" + lx), writes=("colx3",))
            S.op("vector", lambda e: e.tensor_tensor(
                out=colx[:, 4:5], in0=colx[:, 2:3], in1=colx[:, 3:4], op=ALU.max),
                reads=("colx2", "colx3"), writes=("colx4",))
            S.op("vector", (lambda sd=sd: lambda e: e.tensor_scalar(
                out=colx[:, 5:6], in0=sd, scalar1=1e-5, scalar2=0.0,
                op0=ALU.mult, op1=ALU.add))(),
                reads=("ln_sd" + lx,), writes=("colx5",))
            S.op("vector", lambda e: e.tensor_tensor(
                out=colx[:, 4:5], in0=colx[:, 4:5], in1=colx[:, 5:6], op=ALU.max),
                reads=("colx4", "colx5"), writes=("colx4",))
            S.op("vector", lambda e: e.reciprocal(out=colx[:, 6:7], in_=colx[:, 4:5]),
                 reads=("colx4",), writes=("colx6",))
            S.op("vector", lambda e: e.tensor_scalar(
                out=colx[:, 6:7], in0=colx[:, 6:7], scalar1=127.0, scalar2=0.0,
                op0=ALU.mult, op1=ALU.add),
                reads=("colx6",), writes=("colx6",))
            S.op("vector", (lambda sd=sd: lambda e: e.reciprocal(out=colx[:, 7:8], in_=sd))(),
                 reads=("ln_sd" + lx,), writes=("colx7",))
            S.op("vector", lambda e: e.scalar_tensor_tensor(
                out=colx[:, 5:6], in0=colx[:, 4:5], scalar=1.0 / 127.0,
                in1=colx[:, 7:8], op0=ALU.mult, op1=ALU.mult),
                reads=("colx4", "colx7"), writes=("colx5",))
            S.op("vector", (lambda rr=r: lambda e: e.tensor_tensor(
                out=qsg[:, cp * CHUNK_RT + rr: cp * CHUNK_RT + rr + 1], in0=colx[:, 5:6], in1=wsb[:, 0:1],
                op=ALU.mult))(),
                reads=("colx5", "wsb0"), writes=("qsg%d" % (cp * CHUNK_RT + r),))
            S.op("vector", (lambda rr=r: lambda e: e.tensor_tensor(
                out=qsu[:, cp * CHUNK_RT + rr: cp * CHUNK_RT + rr + 1], in0=colx[:, 5:6], in1=wsb[:, 1:2],
                op=ALU.mult))(),
                reads=("colx5", "wsb1"), writes=("qsu%d" % (cp * CHUNK_RT + r),))
            S.op("vector", (lambda b=xtp, mv=mv: lambda e: e.tensor_scalar(
                out=i8x, in0=b, scalar1=mv[:, 0:1], scalar2=colx[:, 6:7],
                op0=ALU.subtract, op1=ALU.mult))(),
                reads=(kx, "ln_mv" + lx, "colx6"), writes=("i32x",))
            S.op("vector", lambda e: e.tensor_copy(out=xqb1, in_=i8x),
                 reads=("i32x",), writes=("xqb1",))
            pp = rt % 2
            pstr = ps_tr[:, pp * D:(pp + 1) * D]
            for k in range(8):
                S.op("tensor", (lambda kk=k, pstr=pstr: lambda e: e.transpose(
                    pstr[:, kk * 128:(kk + 1) * 128],
                    xqb1[:, kk * 128:(kk + 1) * 128], ident))(),
                    reads=("xqb1", "ident"), writes=("pstr%d" % pp,))
            S.pe_fence("pstr%d" % pp)
            S.op("scalar", (lambda rr=r, pstr=pstr: lambda e: e.activation(
                out=xqTc[:, (cp * CHUNK_RT + rr) * D:(cp * CHUNK_RT + rr + 1) * D],
                in_=pstr, func=AF.Copy, scale=1.0))(),
                reads=("pstr%d" % pp,), writes=("xqT%d" % (cp * CHUNK_RT + r),))


    def ph2(ch):
        cp = ch % 2
        for ob in range(8):
            pb = ob % 2
            gslp = gsl[:, pb * 4096:(pb + 1) * 4096]
            uslp = usl[:, pb * 4096:(pb + 1) * 4096]
            S.dma("sync", (lambda obb=ob, gslp=gslp: lambda e: e.dma_start(
                out=gslp, in_=gTd[:, obb * 4096:(obb + 1) * 4096]))(),
                reads=("gTd",), writes=("gsl%d" % pb,))
            S.dma("gpsimd", (lambda obb=ob, uslp=uslp: lambda e: e.dma_start(
                out=uslp, in_=uTd[:, obb * 4096:(obb + 1) * 4096]))(),
                reads=("uTd",), writes=("usl%d" % pb,))
            for r in range(CHUNK_RT):
                pm = r % 2
                psg = ps_g[:, pm * 512:(pm + 1) * 512]
                psu = ps_u[:, pm * 512:(pm + 1) * 512]
                for k in range(8):
                    S.op("tensor", (lambda kk=k, rr=r, gslp=gslp, psg=psg:
                                    lambda e: e.matmul(
                        psg, xqTc[:, (cp * CHUNK_RT + rr) * D + kk * 128: (cp * CHUNK_RT + rr) * D + (kk + 1) * 128],
                        gslp[:, kk * 512:(kk + 1) * 512],
                        start=(kk == 0), stop=(kk == 7)))(),
                        reads=("xqT%d" % (cp * CHUNK_RT + r), "gsl%d" % pb), writes=("psg%d" % pm,))
                for k in range(8):
                    S.op("tensor", (lambda kk=k, rr=r, uslp=uslp, psu=psu:
                                    lambda e: e.matmul(
                        psu, xqTc[:, (cp * CHUNK_RT + rr) * D + kk * 128: (cp * CHUNK_RT + rr) * D + (kk + 1) * 128],
                        uslp[:, kk * 512:(kk + 1) * 512],
                        start=(kk == 0), stop=(kk == 7)))(),
                        reads=("xqT%d" % (cp * CHUNK_RT + r), "usl%d" % pb), writes=("psu%d" % pm,))
                S.pe_fence("psg%d" % pm)
                S.pe_fence("psu%d" % pm)
                gbp = gb[:, pm * 512:(pm + 1) * 512]
                ubp = ub[:, pm * 512:(pm + 1) * 512]
                S.op("scalar", (lambda rr=r, psg=psg, gbp=gbp: lambda e: e.activation(
                    out=gbp, in_=psg, func=AF.Silu, scale=qsg[:, cp * CHUNK_RT + rr: cp * CHUNK_RT + rr + 1]))(),
                    reads=("psg%d" % pm, "qsg%d" % (cp * CHUNK_RT + r)), writes=("gb%d" % pm,))
                S.op("vector", (lambda rr=r, psu=psu, ubp=ubp: lambda e: e.tensor_scalar(
                    out=ubp, in0=psu, scalar1=qsu[:, cp * CHUNK_RT + rr: cp * CHUNK_RT + rr + 1], scalar2=0.0,
                    op0=ALU.mult, op1=ALU.add))(),
                    reads=("psu%d" % pm, "qsu%d" % (cp * CHUNK_RT + r)), writes=("ub%d" % pm,))
                guw = ("gu%d" % r,)
                S.op("vector", (lambda rr=r, obb=ob, gbp=gbp, ubp=ubp:
                                lambda e: e.tensor_tensor(
                    out=big[:, rr * INNER + obb * 512: rr * INNER + (obb + 1) * 512],
                    in0=gbp, in1=ubp, op=ALU.mult))(),
                    reads=("gb%d" % pm, "ub%d" % pm), writes=guw)


    def ph3(ch):
        def quant(r):
            rt = ch * CHUNK_RT + r
            p = rt % 2
            gu_ap = big[:, r * INNER:(r + 1) * INNER]
            i8gp = i8g[:, p * INNER:(p + 1) * INNER]
            qc0 = qcol[:, p * 2: p * 2 + 1]
            qc1 = qcol[:, p * 2 + 1: p * 2 + 2]
            qsp = qs2h[:, p:p + 1]
            S.op("vector", (lambda gu_ap=gu_ap, qc0=qc0: lambda e: e.tensor_reduce(
                out=qc0, in_=gu_ap, axis=AX.X, op=ALU.max,
                apply_absolute_value=True))(),
                reads=("gu%d" % r,), writes=("qc0_%d" % p,))
            S.op("vector", (lambda qc0=qc0: lambda e: e.tensor_scalar(
                out=qc0, in0=qc0, scalar1=1e-5, scalar2=0.0,
                op0=ALU.max, op1=ALU.add))(),
                reads=("qc0_%d" % p,), writes=("qc0_%d" % p,))
            S.op("vector", (lambda qc0=qc0, qc1=qc1: lambda e: e.reciprocal(
                out=qc1, in_=qc0))(),
                reads=("qc0_%d" % p,), writes=("qc1_%d" % p,))
            S.op("vector", (lambda qc1=qc1: lambda e: e.tensor_scalar(
                out=qc1, in0=qc1, scalar1=127.0, scalar2=0.0,
                op0=ALU.mult, op1=ALU.add))(),
                reads=("qc1_%d" % p,), writes=("qc1_%d" % p,))
            S.op("vector", (lambda gu_ap=gu_ap, i8gp=i8gp, qc1=qc1:
                            lambda e: e.tensor_scalar(
                out=i8gp, in0=gu_ap, scalar1=qc1, scalar2=0.0,
                op0=ALU.mult, op1=ALU.add))(),
                reads=("gu%d" % r, "qc1_%d" % p), writes=("i8g%d" % p,))
            S.op("vector", (lambda qc0=qc0, qsp=qsp: lambda e: e.scalar_tensor_tensor(
                out=qsp, in0=qc0, scalar=1.0 / 127.0, in1=wsb[:, 2:3],
                op0=ALU.mult, op1=ALU.mult))(),
                reads=("qc0_%d" % p, "wsb2"), writes=("qs2h%d" % p,))
            q2bp = q2b[:, p * INNER:(p + 1) * INNER]
            S.op("vector", (lambda i8gp=i8gp, q2bp=q2bp: lambda e: e.tensor_copy(
                out=q2bp, in_=i8gp))(),
                reads=("i8g%d" % p,), writes=("q2b%d" % p,))

        def trpart(r):
            rt = ch * CHUNK_RT + r
            p = rt % 2
            q2bp = q2b[:, p * INNER:(p + 1) * INNER]
            guqTp = guqT[:, p * INNER:(p + 1) * INNER]
            for kg in range(4):
                pp = kg % 2
                pstr = ps_tr[:, pp * D:(pp + 1) * D]
                for k in range(8):
                    S.op("tensor", (lambda kk=k, kgg=kg, pstr=pstr, q2bp=q2bp:
                                    lambda e: e.transpose(
                        pstr[:, kk * 128:(kk + 1) * 128],
                        q2bp[:, kgg * D + kk * 128: kgg * D + (kk + 1) * 128],
                        ident))(),
                        reads=("q2b%d" % p, "ident"), writes=("pstr%d" % pp,))
                S.pe_fence("pstr%d" % pp)
                S.op("scalar", (lambda kgg=kg, pstr=pstr, guqTp=guqTp:
                                lambda e: e.activation(
                    out=guqTp[:, kgg * D:(kgg + 1) * D], in_=pstr,
                    func=AF.Copy, scale=1.0))(),
                    reads=("pstr%d" % pp,), writes=("guqT%d" % p,))

        def mmpart(r):
            rt = ch * CHUNK_RT + r
            p = rt % 2
            guqTp = guqT[:, p * INNER:(p + 1) * INNER]
            qsp = qs2h[:, p:p + 1]
            xtp = xt[:, p * D:(p + 1) * D]
            S.dma("gpsimd", (lambda rr=rt, xtp=xtp: lambda e: e.dma_start(
                out=xtp, in_=xd[rr * 128:(rr + 1) * 128, :]))(),
                writes=("xt%d" % p,))
            for oh in range(2):
                pm = oh
                psd = ps_d[:, pm * 512:(pm + 1) * 512]
                for k in range(32):
                    S.op("tensor", (lambda kk=k, ohh=oh, psd=psd, guqTp=guqTp:
                                    lambda e: e.matmul(
                        psd, guqTp[:, kk * 128:(kk + 1) * 128],
                        dTq[:, kk * D + ohh * 512: kk * D + ohh * 512 + 512],
                        start=(kk == 0), stop=(kk == 31)))(),
                        reads=("guqT%d" % p, "dTq"), writes=("psd%d" % pm,))
                S.pe_fence("psd%d" % pm)
                S.op("vector", (lambda ohh=oh, psd=psd, xtp=xtp, qsp=qsp:
                                lambda e: e.scalar_tensor_tensor(
                    out=xtp[:, ohh * 512:(ohh + 1) * 512], in0=psd, scalar=qsp,
                    in1=xtp[:, ohh * 512:(ohh + 1) * 512],
                    op0=ALU.mult, op1=ALU.add))(),
                    reads=("psd%d" % pm, "qs2h%d" % p), writes=("xt%d" % p,))
            S.dma("scalar", (lambda rr=rt, xtp=xtp: lambda e: e.dma_start(
                out=yd[rr * 128:(rr + 1) * 128, :], in_=xtp))(),
                reads=("xt%d" % p,), writes=("ydma",))

        quant(0)
        quant(1)
        trpart(0)
        for r in range(CHUNK_RT):
            mmpart(r)
            if r + 1 < CHUNK_RT:
                trpart(r + 1)
            if r + 2 < CHUNK_RT:
                quant(r + 2)

    ph1(0)
    for ch in range(NCHUNK):
        ph2(ch)
        if ch + 1 < NCHUNK:
            ph1(ch + 1)
        ph3(ch)



def build_l0ffn():
    nc = bass.Bass("TRN2", target_bir_lowering=False, debug=False)
    xd = nc.dram_tensor("x", [S_SEQ, D], F32, kind="ExternalInput").ap()
    wgd = nc.dram_tensor("w_gate", [INNER, D], F32, kind="ExternalInput").ap()
    wud = nc.dram_tensor("w_up", [INNER, D], F32, kind="ExternalInput").ap()
    wdd = nc.dram_tensor("w_down", [D, INNER], F32, kind="ExternalInput").ap()
    onesd = nc.dram_tensor("ones_row", [1, 128], F32, kind="ExternalInput").ap()
    identd = nc.dram_tensor("ident", [128, 128], F32, kind="ExternalInput").ap()
    yd = nc.dram_tensor("y", [S_SEQ, D], F32, kind="ExternalOutput").ap()
    gTd = nc.dram_tensor("gT_scr", [128, 8 * INNER], BF16, kind="Internal").ap()
    uTd = nc.dram_tensor("uT_scr", [128, 8 * INNER], BF16, kind="Internal").ap()
    dTd = nc.dram_tensor("dT_scr", [128, 32 * D], BF16, kind="Internal").ap()
    wsbd = nc.dram_tensor("wsb_scr", [128, 3], F32, kind="Internal").ap()
    with ExitStack() as octx:
        block = octx.enter_context(nc.Block())
        S = Sched(nc)
        cst = _emit_consts(S, nc, octx, onesd, identd)
        with ExitStack() as c0:
            sb, psb = _mk_scoped(nc, c0, "p_")
            emit_l0ffn_prep(S, sb, psb, wgd, wud, wdd, gTd, uTd, dTd, wsbd, cst)
        S.stage_barrier()
        with ExitStack() as c1:
            sb, psb = _mk_scoped(nc, c1, "m_")
            emit_l0ffn_main(S, sb, psb, xd, yd, gTd, uTd, dTd, wsbd, cst)
        S.barrier_on("vector", "ydma")
        S.finalize(octx, block)
    return nc


def run_l0ffn(nc, x_full, w_gate, w_up, w_down):
    c = _consts()
    in_maps = [{"x": np.ascontiguousarray(x_full[b]), "w_gate": w_gate,
                "w_up": w_up, "w_down": w_down,
                "ones_row": c["ones_row"], "ident": c["ident"]} for b in range(8)]
    res = run_bass_kernel_spmd(nc, in_maps, core_ids=list(range(8)))
    return np.stack([r["y"] for r in res.results], axis=0)



DC_RT = 2
DNCH = NRT // DC_RT
SC = DC_RT * 128


def emit_l1attn(S, sb, psb, xd, ww, petd, yd, cst):
    ald, wdxd, bdxd, wxd, bxd = (ww["a_logit"], ww["W_dx"], ww["b_dx"],
                                 ww["W_x"], ww["b_x"])
    wgxd, bgxd, wgad, bgad, wod, bod = (ww["W_gx"], ww["b_gx"], ww["W_ga"],
                                        ww["b_ga"], ww["W_o"], ww["b_o"])
    ones_row = cst["ones_row"]
    ident = cst["ident"]
    wdx = sb("wdx", [128, 8 * D], BF16)
    wdx_t = sb("wdx_t", [16, D], BF16)
    wx = sb("wx", [128, 8 * D], BF16)
    wx_t = sb("wx_t", [16, D], BF16)
    wgx = sb("wgx", [128, 8 * D], BF16)
    wga = sb("wga", [128, 8 * D], BF16)
    wo = sb("wo", [128, 8 * D], BF16)
    peb = sb("peb", [16, S_SEQ], BF16)
    sqlrep = sb("sqlrep", [128, D], F32)
    brep2 = sb("brep2", [128, D], F32)
    rb0 = sb("rb0", [1, D], F32)
    rb1 = sb("rb1", [1, D], F32)
    ae_c = sb("ae_c", [128, 8], F32)
    bx_c = sb("bx_c", [128, 8], F32)
    bgx_c = sb("bgx_c", [128, 8], F32)
    bga_c = sb("bga_c", [128, 8], F32)
    wstage = sb("wstage_d", [128, D], F32)

    xt = sb("xt", [128, 2 * DC_RT * D], F32)
    xlnb = sb("xlnb", [128, 2 * D], BF16)
    xcm = sb("xcm", [128, 8 * SC], BF16)
    dxb = sb("dxb", [128, 2 * DC_RT * D], F32)
    xhb = sb("xhb", [128, 2 * 8 * SC], BF16)
    ab = sb("ab", [128, 8 * SC], F32)
    bb = sb("bb", [128, 8 * SC], F32)
    hb = sb("hb", [128, 8 * SC], BF16)
    h32 = sb("h32", [128, SC], F32)
    carry = sb("carry", [128, 8], F32)
    gx_t = sb("gx_t", [128, 8 * SC], F32)
    ga_t = sb("ga_t", [128, 8 * SC], F32)
    a_t = sb("a_t", [128, 2 * SC], F32)
    stat = ln_stat_bufs(sb)

    ps_tr = psb("ps_tr", [128, D], BF16)
    ps_dx = psb("ps_dx", [128, 2 * 512], F32)
    ps_xh = psb("ps_xh", [128, 2 * 512], F32)
    ps_g = psb("ps_g", [128, 2 * 512], F32)
    ps_o = psb("ps_o", [128, 512], F32)

    S.op("vector", (lambda a=stat[5]: lambda e: e.memset(a, 1e-5))(), writes=("ln_eps",))
    S.op("vector", lambda e: e.memset(carry, 0.0), writes=("carry",))
    S.dma("gpsimd", lambda e: e.dma_start(out=peb, in_=petd), writes=("peb",))

    S.dma("sync", lambda e: e.dma_start(
        out=rb0, in_=ald.rearrange("(one d) -> one d", one=1)),
        writes=("row0",))
    S.op("scalar", lambda e: e.activation(out=rb1, in_=rb0, func=AF.Sigmoid),
         reads=("row0",), writes=("aer",))
    S.op("vector", lambda e: e.tensor_tensor(out=rb1, in0=rb1, in1=rb1,
                                             op=ALU.mult),
         reads=("aer",), writes=("aer",))
    S.op("scalar", lambda e: e.activation(out=rb1, in_=rb1, func=AF.Sqrt,
                                          bias=1.0, scale=-1.0),
         reads=("aer",), writes=("sqlr",))
    for hh in range(2):
        S.op("tensor", (lambda hh=hh: lambda e: e.matmul(
            ps_dx[:, 0:512], ones_row, rb1[:, hh * 512:(hh + 1) * 512],
            start=True, stop=True))(),
            reads=("sqlr", "ones", "sqlrep"), writes=("psdx0",))
        S.op("vector", (lambda hh=hh: lambda e: e.tensor_copy(
            out=sqlrep[:, hh * 512:(hh + 1) * 512], in_=ps_dx[:, 0:512]))(),
            reads=("psdx0",), writes=("sqlrep",))
    S.dma("sync", lambda e: e.dma_start(
        out=rb0, in_=bdxd.rearrange("(one d) -> one d", one=1)),
        reads=("aer",), writes=("row1",))
    S.dma("sync", lambda e: e.dma_start(
        out=rb1, in_=bod.rearrange("(one d) -> one d", one=1)),
        reads=("sqlrep",), writes=("row2", "sqlr"))
    S.op("vector", lambda e: e.tensor_tensor(
        out=rb0, in0=rb0, in1=rb1, op=ALU.add),
        reads=("row1", "row2"), writes=("row1",))
    S.op("vector", lambda e: e.tensor_scalar(
        out=rb0, in0=rb0, scalar1=0.5,
        scalar2=0.0, op0=ALU.mult, op1=ALU.add),
        reads=("row1",), writes=("row1",))
    for hh in range(2):
        S.op("tensor", (lambda hh=hh: lambda e: e.matmul(
            ps_dx[:, 0:512], ones_row, rb0[:, hh * 512:(hh + 1) * 512],
            start=True, stop=True))(),
            reads=("row1", "ones", "brep2"), writes=("psdx0",))
        S.op("vector", (lambda hh=hh: lambda e: e.tensor_copy(
            out=brep2[:, hh * 512:(hh + 1) * 512], in_=ps_dx[:, 0:512]))(),
            reads=("psdx0",), writes=("brep2",))
    for (dst, srcd, key) in ((ae_c, ald, "ae_c"), (bx_c, bxd, "bx_c"),
                             (bgx_c, bgxd, "bgx_c"), (bga_c, bgad, "bga_c")):
        for j in range(8):
            S.dma("sync", (lambda dst=dst, srcd=srcd, jj=j: lambda e: e.dma_start(
                out=dst[:, jj:jj + 1],
                in_=srcd[jj * 128:(jj + 1) * 128].rearrange("(p one) -> p one", one=1)))(),
                writes=(key,))
    S.op("scalar", lambda e: e.activation(out=ae_c, in_=ae_c, func=AF.Sigmoid),
         reads=("ae_c",), writes=("ae_c",))
    S.op("vector", lambda e: e.tensor_tensor(out=a_t[:, 0:8], in0=ae_c, in1=ae_c,
                                             op=ALU.mult),
         reads=("ae_c",), writes=("sql_c0",))
    S.op("scalar", lambda e: e.activation(out=a_t[:, 0:8], in_=a_t[:, 0:8],
                                          func=AF.Sqrt, bias=1.0, scale=-1.0),
         reads=("sql_c0",), writes=("sql_c",))
    S.op("vector", lambda e: e.tensor_tensor(out=bx_c, in0=bx_c, in1=a_t[:, 0:8],
                                             op=ALU.mult),
         reads=("bx_c", "sql_c"), writes=("bx_c",))
    for (dst, srcd, key) in ((wdx, wdxd, "wdx"), (wgx, wgxd, "wgx"),
                             (wga, wgad, "wga"), (wo, wod, "wo")):
        for k in range(8):
            S.dma("gpsimd", (lambda dst=dst, srcd=srcd, kk=k: lambda e: e.dma_start(
                out=dst[:, kk * D:(kk + 1) * D],
                in_=srcd[kk * 128:(kk + 1) * 128, :]))(),
                writes=(key,))
    S.dma("gpsimd", lambda e: e.dma_start(out=wdx_t, in_=wdxd[1024:1040, :]),
          writes=("wdx_t",))
    for k in range(8):
        S.dma("sync", (lambda kk=k: lambda e: e.dma_start(
            out=wstage, in_=wxd[kk * 128:(kk + 1) * 128, :]))(),
            reads=("wstage_used",), writes=("wstage",))
        S.op("vector", lambda e: e.tensor_tensor(
            out=wstage, in0=wstage, in1=sqlrep, op=ALU.mult),
            reads=("wstage", "sqlrep"), writes=("wstage",))
        S.op("vector", (lambda kk=k: lambda e: e.tensor_copy(
            out=wx[:, kk * D:(kk + 1) * D], in_=wstage))(),
            reads=("wstage",), writes=("wx", "wstage_used"))
    S.dma("sync", lambda e: e.dma_start(out=wstage[0:16, :], in_=wxd[1024:1040, :]),
          reads=("wstage_used",), writes=("wstage16",))
    S.op("vector", lambda e: e.tensor_tensor(
        out=wstage[0:16, :], in0=wstage[0:16, :], in1=sqlrep[0:16, :], op=ALU.mult),
        reads=("wstage16", "sqlrep"), writes=("wstage16",))
    S.op("vector", lambda e: e.tensor_copy(out=wx_t, in_=wstage[0:16, :]),
         reads=("wstage16",), writes=("wx_t",))

    def front(ch):
        par = ch % 2
        xtc = xt[:, par * DC_RT * D:(par + 1) * DC_RT * D]
        dxc = dxb[:, par * DC_RT * D:(par + 1) * DC_RT * D]
        xhc = xhb[:, par * 8 * SC:(par + 1) * 8 * SC]
        for r in range(DC_RT):
            rt = ch * DC_RT + r
            xtp = xtc[:, r * D:(r + 1) * D]
            kx = "xt%d_%d" % (par, r)
            S.dma("sync", (lambda rr=rt, xtp=xtp: lambda e: e.dma_start(
                out=xtp, in_=xd[rr * 128:(rr + 1) * 128, :]))(),
                writes=(kx,))
            pp = rt % 2
            emit_ln(S, stat, kx, xtp,
                    xlnb[:, pp * D:(pp + 1) * D], "xlnb%d" % pp, par=pp)
            for k in range(8):
                S.op("tensor", (lambda kk=k, pp=pp: lambda e: e.transpose(
                    ps_tr[:, kk * 128:(kk + 1) * 128],
                    xlnb[:, pp * D + kk * 128: pp * D + (kk + 1) * 128], ident))(),
                    reads=("xlnb%d" % pp, "ident"), writes=("pstr",))
            S.op("vector", (lambda rr=r: lambda e: e.tensor_copy(
                out=xcm.rearrange("p (k s) -> p k s", k=8)
                    [:, :, rr * 128:(rr + 1) * 128],
                in_=ps_tr.rearrange("p (k c) -> p k c", k=8)))(),
                reads=("pstr",), writes=("xcm%d" % r, "pstr"))
        xcm_keys = tuple("xcm%d" % r for r in range(DC_RT))

        for r in range(DC_RT):
            rt = ch * DC_RT + r
            kd = "dxb%d_%d" % (par, r)
            for oh in range(2):
                pm = oh
                psd = ps_dx[:, pm * 512:(pm + 1) * 512]
                for k in range(8):
                    S.op("tensor", (lambda kk=k, rr=r, ohh=oh, psd=psd:
                                    lambda e: e.matmul(
                        psd, xcm[:, kk * SC + rr * 128: kk * SC + (rr + 1) * 128],
                        wdx[:, kk * D + ohh * 512: kk * D + ohh * 512 + 512],
                        start=(kk == 0), stop=False))(),
                        reads=xcm_keys + ("wdx",), writes=("psdx%d" % pm,))
                S.op("tensor", (lambda rr=rt, ohh=oh, psd=psd: lambda e: e.matmul(
                    psd, peb[:, rr * 128:(rr + 1) * 128],
                    wdx_t[:, ohh * 512:(ohh + 1) * 512],
                    start=False, stop=True))(),
                    reads=("peb", "wdx_t"), writes=("psdx%d" % pm,))
                S.op("vector", (lambda rr=r, ohh=oh, psd=psd, dxc=dxc: lambda e:
                                e.scalar_tensor_tensor(
                    out=dxc[:, rr * D + ohh * 512: rr * D + ohh * 512 + 512],
                    in0=psd, scalar=0.5,
                    in1=brep2[:, ohh * 512:(ohh + 1) * 512],
                    op0=ALU.mult, op1=ALU.add))(),
                    reads=("psdx%d" % pm, "brep2"), writes=(kd,))
                S.op("vector", (lambda rr=r, ohh=oh, dxc=dxc, xtc=xtc:
                                lambda e: e.tensor_tensor(
                    out=dxc[:, rr * D + ohh * 512: rr * D + ohh * 512 + 512],
                    in0=dxc[:, rr * D + ohh * 512: rr * D + ohh * 512 + 512],
                    in1=xtc[:, rr * D + ohh * 512: rr * D + ohh * 512 + 512],
                    op=ALU.add))(),
                    reads=(kd, "xt%d_%d" % (par, r)), writes=(kd,))

        for oc in range(8):
            pm = oc % 2
            psx = ps_xh[:, pm * 512:(pm + 1) * 512]
            for k in range(8):
                S.op("tensor", (lambda kk=k, occ=oc, psx=psx: lambda e: e.matmul(
                    psx[:, 0:SC], wx[:, kk * D + occ * 128: kk * D + (occ + 1) * 128],
                    xcm[:, kk * SC:(kk + 1) * SC],
                    start=(kk == 0), stop=False))(),
                    reads=xcm_keys + ("wx",), writes=("psxh%d" % pm,))
            S.op("tensor", (lambda occ=oc, rr=ch, psx=psx: lambda e: e.matmul(
                psx[:, 0:SC], wx_t[:, occ * 128:(occ + 1) * 128],
                peb[:, rr * SC:(rr + 1) * SC],
                start=False, stop=True))(),
                reads=("peb", "wx_t"), writes=("psxh%d" % pm,))
            S.op("vector", (lambda occ=oc, psx=psx, xhc=xhc, par=par:
                            lambda e: e.tensor_scalar(
                out=xhc[:, occ * SC:(occ + 1) * SC], in0=psx[:, 0:SC],
                scalar1=bx_c[:, occ:occ + 1], scalar2=0.0,
                op0=ALU.add, op1=ALU.add))(),
                reads=("psxh%d" % pm, "bx_c"), writes=("xhb%d_%d" % (par, oc),))

    def back(ch):
        par = ch % 2
        xhc = xhb[:, par * 8 * SC:(par + 1) * 8 * SC]
        dxc = dxb[:, par * DC_RT * D:(par + 1) * DC_RT * D]
        xhb_keys = tuple("xhb%d_%d" % (par, oc) for oc in range(8))

        for oc in range(8):
            for k in range(8):
                S.op("tensor", (lambda kk=k, occ=oc, xhc=xhc: lambda e: e.matmul(
                    ps_g[:, 0:SC], wgx[:, kk * D + occ * 128: kk * D + (occ + 1) * 128],
                    xhc[:, kk * SC:(kk + 1) * SC],
                    start=(kk == 0), stop=(kk == 7)))(),
                    reads=xhb_keys + ("wgx",), writes=("psg0",))
            for k in range(8):
                S.op("tensor", (lambda kk=k, occ=oc, xhc=xhc: lambda e: e.matmul(
                    ps_g[:, 512:512 + SC], wga[:, kk * D + occ * 128: kk * D + (occ + 1) * 128],
                    xhc[:, kk * SC:(kk + 1) * SC],
                    start=(kk == 0), stop=(kk == 7)))(),
                    reads=xhb_keys + ("wga",), writes=("psg1",))
            S.op("scalar", (lambda occ=oc: lambda e: e.activation(
                out=gx_t[:, occ * SC:(occ + 1) * SC], in_=ps_g[:, 0:SC],
                func=AF.Sigmoid, bias=bgx_c[:, occ:occ + 1]))(),
                reads=("psg0", "bgx_c"), writes=("gx_t%d" % oc, "psg0"))
            S.op("scalar", (lambda occ=oc: lambda e: e.activation(
                out=ga_t[:, occ * SC:(occ + 1) * SC], in_=ps_g[:, 512:512 + SC],
                func=AF.Sigmoid, bias=bga_c[:, occ:occ + 1]))(),
                reads=("psg1", "bga_c"), writes=("ga_t%d" % oc, "psg1"))
        for oc in range(8):
            sp_ = oc % 2
            gxp = gx_t[:, oc * SC:(oc + 1) * SC]
            gap = ga_t[:, oc * SC:(oc + 1) * SC]
            atp = a_t[:, sp_ * SC:(sp_ + 1) * SC]
            S.op("vector", (lambda occ=oc, gap=gap, atp=atp: lambda e: e.tensor_scalar(
                out=atp, in0=gap, scalar1=ae_c[:, occ:occ + 1], scalar2=0.0,
                op0=ALU.mult, op1=ALU.add))(),
                reads=("ga_t%d" % oc, "ae_c"), writes=("a_t%d" % sp_,))
            S.op("vector", (lambda occ=oc, atp=atp: lambda e: e.tensor_copy(
                out=ab[:, occ * SC:(occ + 1) * SC], in_=atp))(),
                reads=("a_t%d" % sp_,), writes=("ab%d" % oc,))
            S.op("vector", (lambda atp=atp, gap=gap: lambda e: e.tensor_tensor(
                out=gap, in0=atp, in1=atp, op=ALU.mult))(),
                reads=("a_t%d" % sp_, "ga_t%d" % oc), writes=("ga_t%d" % oc,))
            S.op("vector", (lambda occ=oc, gxp=gxp, xhc=xhc: lambda e: e.tensor_tensor(
                out=gxp, in0=gxp, in1=xhc[:, occ * SC:(occ + 1) * SC],
                op=ALU.mult))(),
                reads=("gx_t%d" % oc, "xhb%d_%d" % (par, oc)),
                writes=("gx_t%d" % oc,))
        for oc in range(8):
            gxp = gx_t[:, oc * SC:(oc + 1) * SC]
            gap = ga_t[:, oc * SC:(oc + 1) * SC]
            S.op("scalar", (lambda gap=gap: lambda e: e.activation(
                out=gap, in_=gap, func=AF.Sqrt, bias=1.0, scale=-1.0))(),
                reads=("ga_t%d" % oc,), writes=("ga_t%d" % oc,))
            S.op("vector", (lambda occ=oc, gxp=gxp, gap=gap: lambda e: e.tensor_tensor(
                out=bb[:, occ * SC:(occ + 1) * SC], in0=gap, in1=gxp,
                op=ALU.mult))(),
                reads=("ga_t%d" % oc, "gx_t%d" % oc), writes=("bb%d" % oc,))

        for oc in range(8):
            S.op("vector", (lambda occ=oc: lambda e: e.tensor_tensor_scan(
                out=h32, data0=ab[:, occ * SC:(occ + 1) * SC],
                data1=bb[:, occ * SC:(occ + 1) * SC],
                initial=carry[:, occ:occ + 1],
                op0=ALU.mult, op1=ALU.add))(),
                reads=("ab%d" % oc, "bb%d" % oc, "carry%d" % oc), writes=("h32",))
            S.op("vector", (lambda occ=oc: lambda e: e.tensor_copy(
                out=hb[:, occ * SC:(occ + 1) * SC], in_=h32))(),
                reads=("h32",), writes=("hb%d" % oc,))
            S.op("vector", (lambda occ=oc: lambda e: e.tensor_copy(
                out=carry[:, occ:occ + 1], in_=h32[:, SC - 1:SC]))(),
                reads=("h32",), writes=("carry%d" % oc,))
        hb_keys = tuple("hb%d" % oc for oc in range(8))

        for r in range(DC_RT):
            rt = ch * DC_RT + r
            kd = "dxb%d_%d" % (par, r)
            for oh in range(2):
                for k in range(8):
                    S.op("tensor", (lambda kk=k, rr=r, ohh=oh:
                                    lambda e: e.matmul(
                        ps_o, hb[:, kk * SC + rr * 128: kk * SC + (rr + 1) * 128],
                        wo[:, kk * D + ohh * 512: kk * D + ohh * 512 + 512],
                        start=(kk == 0), stop=(kk == 7)))(),
                        reads=hb_keys + ("wo",), writes=("pso",))
                S.op("vector", (lambda rr=r, ohh=oh, dxc=dxc:
                                lambda e: e.scalar_tensor_tensor(
                    out=dxc[:, rr * D + ohh * 512: rr * D + ohh * 512 + 512],
                    in0=ps_o, scalar=0.5,
                    in1=dxc[:, rr * D + ohh * 512: rr * D + ohh * 512 + 512],
                    op0=ALU.mult, op1=ALU.add))(),
                    reads=("pso", kd), writes=(kd, "pso"))
            S.dma("scalar", (lambda rr=rt, dxc=dxc, r=r: lambda e: e.dma_start(
                out=yd[rr * 128:(rr + 1) * 128, :],
                in_=dxc[:, r * D:(r + 1) * D]))(),
                reads=(kd,), writes=("ydma",))

    for ch in range(DNCH + 2):
        if ch >= 2:
            back(ch - 2)
        if ch < DNCH:
            front(ch)


def build_l1attn():
    nc = bass.Bass("TRN2", target_bir_lowering=False, debug=False)
    xd = nc.dram_tensor("x", [S_SEQ, D], F32, kind="ExternalInput").ap()
    ww = {}
    for nm, shp in (("a_logit", [D]), ("W_dx", [1040, D]), ("b_dx", [D]),
                    ("W_x", [1040, D]), ("b_x", [D]), ("W_gx", [D, D]),
                    ("b_gx", [D]), ("W_ga", [D, D]), ("b_ga", [D]),
                    ("W_o", [D, D]), ("b_o", [D])):
        ww[nm] = nc.dram_tensor(nm, shp, F32, kind="ExternalInput").ap()
    petd = nc.dram_tensor("peT", [16, S_SEQ], F32, kind="ExternalInput").ap()
    onesd = nc.dram_tensor("ones_row", [1, 128], F32, kind="ExternalInput").ap()
    identd = nc.dram_tensor("ident", [128, 128], F32, kind="ExternalInput").ap()
    yd = nc.dram_tensor("y", [S_SEQ, D], F32, kind="ExternalOutput").ap()
    with ExitStack() as octx:
        block = octx.enter_context(nc.Block())
        S = Sched(nc)
        cst = _emit_consts(S, nc, octx, onesd, identd)
        with ExitStack() as c1:
            sb, psb = _mk_scoped(nc, c1, "d_")
            emit_l1attn(S, sb, psb, xd, ww, petd, yd, cst)
        S.barrier_on("vector", "ydma")
        S.finalize(octx, block)
    return nc


def _pe_table():
    positions = np.arange(S_SEQ, dtype=np.float64)[:, None]
    div = np.exp(np.arange(0, 16, 2, dtype=np.float64) * (-np.log(10000.0) / 16))
    pe = np.zeros((S_SEQ, 16), dtype=np.float64)
    pe[:, 0::2] = np.sin(positions * div)
    pe[:, 1::2] = np.cos(positions * div)
    return np.ascontiguousarray((pe / 100.0).T.astype(np.float32))


def run_l1attn(nc, x_full, d):
    c = _consts()
    pet = _pe_table()
    in_maps = []
    for b in range(8):
        in_maps.append({
            "x": np.ascontiguousarray(x_full[b]),
            "a_logit": d["L1_attn_a_logit"], "W_dx": d["L1_attn_W_dx"],
            "b_dx": d["L1_attn_b_dx"], "W_x": d["L1_attn_W_x"],
            "b_x": d["L1_attn_b_x"], "W_gx": d["L1_attn_W_gx"],
            "b_gx": d["L1_attn_b_gx"], "W_ga": d["L1_attn_W_ga"],
            "b_ga": d["L1_attn_b_ga"], "W_o": d["L1_attn_W_o"],
            "b_o": d["L1_attn_b_o"], "peT": pet,
            "ones_row": c["ones_row"], "ident": c["ident"]})
    res = run_bass_kernel_spmd(nc, in_maps, core_ids=list(range(8)))
    return np.stack([r["y"] for r in res.results], axis=0)



NW = 15
NH = 16
HD = 64
VW = 72
VPAD = NH * VW


def emit_l0attn_prep_epb(S, sb, psb, pbd, epbTd, cst):
    identf = cst["identf"]
    wstage = sb("wstage", [128, 2 * D], F32)
    estage = sb("estage", [128, 4 * 512], BF16)
    ps_mm = psb("ps_mm", [128, 4 * 512], F32)
    
    for h in range(NH):
        for qt in range(4):
            pq = qt % 2
            wsl = wstage[:, pq * D:(pq + 1) * D]
            S.dma("sync", (lambda hh=h, qq=qt, wsl=wsl: lambda e: e.dma_start(
                out=wsl[:, 0:512], in_=pbd[hh, qq * 128:(qq + 1) * 128, :]))(),
                reads=("wsl_used%d" % pq,), writes=("wsl%d" % pq, "wstage"))
            for kt in range(4):
                S.op("tensor", (lambda qq=qt, kk=kt, wsl=wsl: lambda e: e.transpose(
                    ps_mm[:, kk * 512 + qq * 128: kk * 512 + (qq + 1) * 128],
                    wsl[:, kk * 128:(kk + 1) * 128], identf))(),
                    reads=("wsl%d" % pq, "identf"), writes=("ps_pb%d" % kt,))
            S.op("tensor", (lambda: lambda e: e.nop())(),
                 reads=(), writes=("wsl_used%d" % pq,))
        for kt in range(4):
            S.pe_fence("ps_pb%d" % kt)
            S.op("scalar", (lambda kk=kt: lambda e: e.activation(
                out=estage[:, kk * 512:(kk + 1) * 512],
                in_=ps_mm[:, kk * 512:(kk + 1) * 512], func=AF.Exp))(),
                reads=("ps_pb%d" % kt,), writes=("estage%d" % kt,))
            S.dma("sync",
                  (lambda hh=h, kk=kt: lambda e: e.dma_start(
                out=epbTd[:, hh * 2048 + kk * 512: hh * 2048 + (kk + 1) * 512],
                in_=estage[:, kk * 512:(kk + 1) * 512]))(),
                reads=("estage%d" % kt,), writes=("epbTd%d" % kt, "ps_pb%d" % kt))



def emit_l0attn_main(S, sb, psb, xd, ww, epbTd, yd, cst):
    wqd, bqd, wkd, bkd = ww["Wq"], ww["bq"], ww["Wk"], ww["bk"]
    wvd, bvd, wod, bod = ww["Wv"], ww["bv"], ww["Wo"], ww["bo"]
    ones_row = cst["ones_row"]
    ident = cst["ident"]
    wqs = sb("wqs", [128, 8 * D], BF16)
    wk = sb("wk", [128, 8 * D], BF16)
    wvp = sb("wvp", [128, 8 * VPAD], BF16)
    wo = sb("wo", [128, 8 * D], BF16)
    bq8_c = sb("bq8_c", [128, 8], F32)
    bk_c = sb("bk_c", [128, 8], F32)
    bvprep = sb("bvprep", [128, VPAD], F32)
    borep = sb("borep", [128, D], F32)
    rowbuf = sb("rowbuf", [1, VPAD], F32)
    wstage = sb("wstage_b", [128, 2 * D], F32)
    vstage = sb("vstage", [128, D], BF16)
    stat = ln_stat_bufs(sb)

    xt = sb("xt", [128, 2 * D], F32)
    xlnb = sb("xlnb", [128, 2 * D], BF16)
    xcm_s = sb("xcm_s", [128, 8 * 256], BF16)
    qcm = sb("qcm", [128, 8 * 1024], BF16)
    kcm = sb("kcm", [128, 8 * 768], BF16)
    vrm = sb("vrm", [128, 6 * VPAD], BF16)
    merged = sb("merged", [128, 6 * D], F32)
    epb = sb("epb", [128, 2 * 2048], BF16)
    estage = sb("estage", [128, 2 * 1024], BF16)
    owbuf = sb("owbuf", [128, 2 * 512], BF16)
    probsT = sb("probsT", [128, 3 * 4 * 512], BF16)
    rden = sb("rden", [128, 2 * 4], F32)
    mbf = vstage
    mT = sb("mT", [128, 2 * D], BF16)
    obuf = sb("obuf", [128, D], F32)
    ybuf = wstage

    ps_tr = psb("ps_tr", [128, D], BF16)
    ps_mm = psb("ps_mm", [128, 4 * 512], F32)
    ps_ow = psb("ps_ow", [128, 2 * 512], F32)
    ps_owt = psb("ps_owt", [128, D], BF16)

    S.op("vector", (lambda a=stat[5]: lambda e: e.memset(a, 1e-5))(), writes=("ln_eps",))
    for t in range(6):
        S.op("vector", (lambda tt=t: lambda e: e.memset(
            merged[:, tt * D:(tt + 1) * D], 0.0))(), writes=("mg%d" % t,))

    for k in range(8):
        S.dma("sync", (lambda kk=k: lambda e: e.dma_start(
            out=wstage[:, 0:D], in_=wqd[kk * 128:(kk + 1) * 128, :]))(),
            reads=("wstage_used",), writes=("wstage",))
        S.op("vector", (lambda kk=k: lambda e: e.tensor_scalar(
            out=wqs[:, kk * D:(kk + 1) * D], in0=wstage[:, 0:D],
            scalar1=0.125, scalar2=0.0, op0=ALU.mult, op1=ALU.add))(),
            reads=("wstage",), writes=("wqs", "wstage_used"))
        S.dma("gpsimd", (lambda kk=k: lambda e: e.dma_start(
            out=wk[:, kk * D:(kk + 1) * D], in_=wkd[kk * 128:(kk + 1) * 128, :]))(),
            writes=("wk",))
        S.dma("gpsimd", (lambda kk=k: lambda e: e.dma_start(
            out=wo[:, kk * D:(kk + 1) * D], in_=wod[kk * 128:(kk + 1) * 128, :]))(),
            writes=("wo",))
    S.op("vector", lambda e: e.memset(wvp, 0.0), writes=("wvp",))
    for k in range(8):
        S.dma("gpsimd", (lambda kk=k: lambda e: e.dma_start(
            out=vstage, in_=wvd[kk * 128:(kk + 1) * 128, :]))(),
            reads=("vstage_used",), writes=("vstage",))
        S.op("vector", (lambda kk=k: lambda e: e.tensor_copy(
            out=wvp.rearrange("p (K h v) -> p K h v", K=8, h=NH)
                [:, kk, :, 0:HD],
            in_=vstage.rearrange("p (h v) -> p h v", h=NH)))(),
            reads=("vstage", "wvp"), writes=("wvp", "vstage_used"))
    for (dst, srcd, key) in ((bq8_c, bqd, "bq8_c"), (bk_c, bkd, "bk_c")):
        for j in range(8):
            S.dma("sync", (lambda dst=dst, srcd=srcd, jj=j: lambda e: e.dma_start(
                out=dst[:, jj:jj + 1],
                in_=srcd[jj * 128:(jj + 1) * 128].rearrange("(p one) -> p one", one=1)))(),
                writes=(key,))
    S.op("vector", lambda e: e.tensor_scalar(
        out=bq8_c, in0=bq8_c, scalar1=0.125, scalar2=0.0,
        op0=ALU.mult, op1=ALU.add),
        reads=("bq8_c",), writes=("bq8_c",))
    S.op("vector", lambda e: e.memset(rowbuf, 0.0), writes=("rowv",))
    S.dma("sync", lambda e: e.dma_start(
        out=ybuf[0:1, 0:D], in_=bvd.rearrange("(one d) -> one d", one=1)),
        writes=("bvrow",))
    for h in range(NH):
        S.op("vector", (lambda hh=h: lambda e: e.tensor_copy(
            out=rowbuf[0:1, hh * VW: hh * VW + HD],
            in_=ybuf[0:1, hh * HD:(hh + 1) * HD]))(),
            reads=("rowv", "bvrow"), writes=("rowv",))
        S.op("vector", (lambda hh=h: lambda e: e.memset(
            rowbuf[:, hh * VW + HD: hh * VW + HD + 1], 1.0))(),
            reads=("rowv",), writes=("rowv",))
    for seg in range(3):
        w0 = seg * 384
        S.op("tensor", (lambda w0=w0: lambda e: e.matmul(
            ps_ow[:, 0:384], ones_row, rowbuf[:, w0:w0 + 384],
            start=True, stop=True))(),
            reads=("rowv", "ones"), writes=("ps_ow",))
        S.pe_fence("ps_ow")
        S.op("vector", (lambda w0=w0: lambda e: e.tensor_copy(
            out=bvprep[:, w0:w0 + 384], in_=ps_ow[:, 0:384]))(),
            reads=("ps_ow",), writes=("bvprep",))
    S.dma("sync", lambda e: e.dma_start(
        out=rowbuf[:, 0:D], in_=bod.rearrange("(one d) -> one d", one=1)),
        reads=("bvprep",), writes=("rowo",))
    for hh in range(2):
        S.op("tensor", (lambda hh=hh: lambda e: e.matmul(
            ps_ow[:, 0:512], ones_row, rowbuf[:, hh * 512:(hh + 1) * 512],
            start=True, stop=True))(),
            reads=("rowo", "ones"), writes=("ps_ow",))
        S.pe_fence("ps_ow")
        S.op("vector", (lambda hh=hh: lambda e: e.tensor_copy(
            out=borep[:, hh * 512:(hh + 1) * 512], in_=ps_ow[:, 0:512]))(),
            reads=("ps_ow",), writes=("borep",))
    def emit_qkv(sb_i):
        slot = sb_i % 3
        for r in range(2):
            rt = sb_i * 2 + r
            p = rt % 2
            xtp = xt[:, p * D:(p + 1) * D]
            kx = "xt%d" % p
            S.dma("sync", (lambda rr=rt, xtp=xtp: lambda e: e.dma_start(
                out=xtp, in_=xd[rr * 128:(rr + 1) * 128, :]))(),
                writes=(kx,))
            emit_ln(S, stat, kx, xtp,
                    xlnb[:, p * D:(p + 1) * D], "xlnb%d" % p, par=p)
            pstr = ps_tr
            for k in range(8):
                S.op("tensor", (lambda kk=k, pstr=pstr, p=p: lambda e: e.transpose(
                    pstr[:, kk * 128:(kk + 1) * 128],
                    xlnb[:, p * D + kk * 128: p * D + (kk + 1) * 128], ident))(),
                    reads=("xlnb%d" % p, "ident"), writes=("pstrb0",))
            S.op("vector", (lambda rr=r, pstr=pstr: lambda e: e.tensor_copy(
                out=xcm_s.rearrange("p (k s) -> p k s", k=8)
                    [:, :, rr * 128:(rr + 1) * 128],
                in_=pstr.rearrange("p (k c) -> p k c", k=8)))(),
                reads=("pstrb0",), writes=("xcm_s%d" % r,))
        xcm_keys = ("xcm_s0", "xcm_s1")
        for (wmat, wkey, bcol, bkey, dst, dkey, ost) in (
                (wqs, "wqs", bq8_c, "bq8_c", qcm, "qcm", 1024),
                (wk, "wk", bk_c, "bk_c", kcm, "kcm", 768)):
            for oc in range(8):
                bank = oc % 4
                psx = ps_mm[:, bank * 512: bank * 512 + 256]
                for k in range(8):
                    S.op("tensor", (lambda kk=k, occ=oc, psx=psx, wmat=wmat:
                                    lambda e: e.matmul(
                        psx, wmat[:, kk * D + occ * 128: kk * D + (occ + 1) * 128],
                        xcm_s[:, kk * 256:(kk + 1) * 256],
                        start=(kk == 0), stop=(kk == 7)))(),
                        reads=xcm_keys + (wkey,), writes=("ps_mm%d" % bank,))
                S.pe_fence("ps_mm%d" % bank)
                S.op("vector", (lambda occ=oc, psx=psx, dst=dst, bcol=bcol,
                                slot=slot, ost=ost: lambda e: e.tensor_scalar(
                    out=dst[:, occ * ost + slot * 256: occ * ost + (slot + 1) * 256],
                    in0=psx, scalar1=bcol[:, occ:occ + 1], scalar2=0.0,
                    op0=ALU.add, op1=ALU.add))(),
                    reads=("ps_mm%d" % bank, bkey),
                    writes=(dkey + "%d" % slot,))
                if ost == 1024 and slot == 0:
                    S.op("vector", (lambda occ=oc, psx=psx, dst=dst, bcol=bcol:
                                    lambda e: e.tensor_scalar(
                        out=dst[:, occ * 1024 + 768: occ * 1024 + 1024],
                        in0=psx, scalar1=bcol[:, occ:occ + 1], scalar2=0.0,
                        op0=ALU.add, op1=ALU.add))(),
                        reads=("ps_mm%d" % bank, bkey),
                        writes=("qcm0",))
        for r in range(2):
            vt = (sb_i * 2 + r) % 6
            for seg in range(3):
                bank = seg
                psx = ps_mm[:, bank * 512: bank * 512 + 384]
                for k in range(8):
                    S.op("tensor", (lambda kk=k, rr=r, sg=seg, psx=psx:
                                    lambda e: e.matmul(
                        psx, xcm_s[:, kk * 256 + rr * 128: kk * 256 + (rr + 1) * 128],
                        wvp[:, kk * VPAD + sg * 384: kk * VPAD + (sg + 1) * 384],
                        start=(kk == 0), stop=(kk == 7)))(),
                        reads=xcm_keys + ("wvp",), writes=("ps_mm%d" % bank,))
                S.pe_fence("ps_mm%d" % bank)
                S.op("vector", (lambda vt=vt, sg=seg, psx=psx: lambda e:
                                e.scalar_tensor_tensor(
                    out=vrm[:, vt * VPAD + sg * 384: vt * VPAD + (sg + 1) * 384],
                    in0=psx, scalar=1.0,
                    in1=bvprep[:, sg * 384:(sg + 1) * 384],
                    op0=ALU.mult, op1=ALU.add))(),
                    reads=("ps_mm%d" % bank, "bvprep"), writes=("vrm%d" % vt,))

    def finalize_block(bi):
        cnt = 1.0 if bi == 0 else (1.0 / 3.0 if bi == 14 else 0.5)
        for sub in range(2):
            st = (2 * bi + sub) % 6
            rt = 2 * bi + sub
            p = rt % 2
            S.op("vector", (lambda st=st, cnt=cnt: lambda e: e.tensor_scalar(
                out=mbf, in0=merged[:, st * D:(st + 1) * D],
                scalar1=cnt, scalar2=0.0, op0=ALU.mult, op1=ALU.add))(),
                reads=("mg%d" % st,), writes=("mbf",))
            pstr = ps_tr
            for k in range(8):
                S.op("tensor", (lambda kk=k, pstr=pstr: lambda e: e.transpose(
                    pstr[:, kk * 128:(kk + 1) * 128],
                    mbf[:, kk * 128:(kk + 1) * 128], ident))(),
                    reads=("mbf", "ident"), writes=("pstrb0",))
            S.op("vector", (lambda pstr=pstr, sub=sub: lambda e: e.tensor_copy(
                out=mT[:, sub * D:(sub + 1) * D], in_=pstr))(),
                reads=("pstrb0",), writes=("mT%d" % sub,))
            S.op("vector", (lambda st=st: lambda e: e.memset(
                merged[:, st * D:(st + 1) * D], 0.0))(),
                reads=(), writes=("mg%d" % st,))
            xtp = xt[:, p * D:(p + 1) * D]
            S.dma("gpsimd", (lambda rr=rt, xtp=xtp: lambda e: e.dma_start(
                out=xtp, in_=xd[rr * 128:(rr + 1) * 128, :]))(),
                writes=("xt%d" % p,))
            for oh in range(2):
                bank = oh
                psx = ps_mm[:, bank * 512:(bank + 1) * 512]
                for k in range(8):
                    S.op("tensor", (lambda kk=k, ohh=oh, sub=sub, psx=psx:
                                    lambda e: e.matmul(
                        psx, mT[:, sub * D + kk * 128: sub * D + (kk + 1) * 128],
                        wo[:, kk * D + ohh * 512: kk * D + ohh * 512 + 512],
                        start=(kk == 0), stop=(kk == 7)))(),
                        reads=("mT%d" % sub, "wo"), writes=("ps_mm%d" % bank,))
                S.pe_fence("ps_mm%d" % bank)
                S.op("vector", (lambda ohh=oh, psx=psx: lambda e:
                                e.scalar_tensor_tensor(
                    out=obuf[:, ohh * 512:(ohh + 1) * 512], in0=psx, scalar=1.0,
                    in1=borep[:, ohh * 512:(ohh + 1) * 512],
                    op0=ALU.mult, op1=ALU.add))(),
                    reads=("ps_mm%d" % bank, "borep"), writes=("obuf",))
            ybp = ybuf[:, p * D:(p + 1) * D]
            S.op("vector", (lambda ybp=ybp, xtp=xtp: lambda e: e.tensor_tensor(
                out=ybp, in0=obuf, in1=xtp, op=ALU.add))(),
                reads=("obuf", "xt%d" % p), writes=("yb%d" % p,))
            S.dma("scalar", (lambda rr=rt, ybp=ybp: lambda e: e.dma_start(
                out=yd[rr * 128:(rr + 1) * 128, :], in_=ybp))(),
                reads=("yb%d" % p,), writes=("ydma",))

    emit_qkv(0)
    emit_qkv(1)

    def sp(w, h, tp3, tp2):
        oc = h // 2
        off = (h % 2) * 64
        epbp = epb[:, tp2 * 2048:(tp2 + 1) * 2048]
        S.dma("scalar", (lambda hh=h, epbp=epbp: lambda e: e.dma_start(
            out=epbp, in_=epbTd[:, hh * 2048:(hh + 1) * 2048]))(),
            reads=("epbTd0", "epbTd1", "epbTd2", "epbTd3",
                   "epb_used%d" % tp2), writes=("epb%d" % tp2,))
        score_reads = tuple({"qcm%d" % ((w * 256) % 768 // 256),
                             "qcm%d" % ((w * 256 + 256) % 768 // 256)} |
                            {"kcm%d" % (((w * 256 + kt * 128) % 768) // 256)
                             for kt in range(4)})
        qs = (w * 256) % 768
        for kt in range(4):
            ks = (w * 256 + kt * 128) % 768
            S.op("tensor", (lambda kt=kt, ks=ks, qs=qs, oc=oc,
                            off=off: lambda e: e.matmul(
                ps_mm[:, kt * 512:(kt + 1) * 512],
                kcm[off:off + 64, oc * 768 + ks: oc * 768 + ks + 128],
                qcm[off:off + 64, oc * 1024 + qs: oc * 1024 + qs + 512],
                start=True, stop=True))(),
                reads=score_reads, writes=("ps_mm%d" % kt,))
        for kp in range(2):
            est = estage[:, tp2 * 1024:(tp2 + 1) * 1024]
            S.op("scalar", (lambda kp=kp, est=est: lambda e: e.activation(
                out=est, in_=ps_mm[:, kp * 1024:(kp + 1) * 1024],
                func=AF.Exp))(),
                reads=("ps_mm%d" % (2 * kp), "ps_mm%d" % (2 * kp + 1)),
                writes=("estage%d" % tp2,))
            S.op("vector" if kp == 0 else "gpsimd",
                 (lambda kp=kp, tp3=tp3, est=est, epbp=epbp:
                            lambda e: e.tensor_tensor(
                out=probsT[:, tp3 * 2048 + kp * 1024: tp3 * 2048 + (kp + 1) * 1024],
                in0=est, in1=epbp[:, kp * 1024:(kp + 1) * 1024], op=ALU.mult))(),
                reads=("estage%d" % tp2, "epb%d" % tp2),
                writes=("probsT%d_%d" % (tp3, 2 * kp),
                        "probsT%d_%d" % (tp3, 2 * kp + 1),
                        "ps_mm%d" % (2 * kp), "ps_mm%d" % (2 * kp + 1),
                        "epb_used%d" % tp2))

    def owm(w, h, tp3, tp2):
        psow = ps_ow[0:65, tp2 * 512:(tp2 + 1) * 512]
        owb = owbuf[:, tp2 * 512:(tp2 + 1) * 512]
        pstr = (ps_tr if tp2 == 0 else ps_owt)[:, 0:264]
        trk = "pstrb%d" % tp2
        rd = rden[:, tp2 * 4:(tp2 + 1) * 4]
        for kt in range(4):
            vt = ((w * 256 + kt * 128) // 128) % 6
            S.op("tensor", (lambda kt=kt, vt=vt, h=h, tp3=tp3,
                            psow=psow: lambda e: e.matmul(
                psow,
                vrm[:, vt * VPAD + h * VW: vt * VPAD + h * VW + 65],
                probsT[:, tp3 * 2048 + kt * 512: tp3 * 2048 + (kt + 1) * 512],
                start=(kt == 0), stop=(kt == 3)))(),
                reads=("probsT%d_%d" % (tp3, kt), "vrm%d" % vt),
                writes=("ps_ow%d" % tp2,))
        S.op("scalar", (lambda psow=psow, owb=owb: lambda e: e.activation(
            out=owb[0:65, :], in_=psow, func=AF.Copy, scale=1.0))(),
            reads=("ps_ow%d" % tp2,), writes=("owb%d" % tp2, "ps_ow%d" % tp2))
        for qb in range(4):
            S.op("tensor", (lambda qb=qb, owb=owb, pstr=pstr: lambda e: e.transpose(
                pstr[:, qb * 66: qb * 66 + 65],
                owb[0:65, qb * 128:(qb + 1) * 128], ident[0:65, 0:65]))(),
                reads=("owb%d" % tp2, "ident"), writes=(trk,))
        S.op("vector", (lambda pstr=pstr, rd=rd: lambda e: e.reciprocal(
            out=rd,
            in_=pstr[:, 0:264].rearrange("p (b c) -> p b c", b=4)[:, :, 64]))(),
            reads=(trk,), writes=("rden%d" % tp2,))
        if w == NW - 1:
            S.op("vector", (lambda rd=rd: lambda e: e.tensor_scalar(
                out=rd, in0=rd, scalar1=2.0, scalar2=0.0,
                op0=ALU.mult, op1=ALU.add))(),
                reads=("rden%d" % tp2,), writes=("rden%d" % tp2,))
        for qb in range(4):
            stt_ = (w * 2 + qb) % 6
            S.op("vector", (lambda qb=qb, st=stt_, h=h, pstr=pstr, rd=rd:
                            lambda e: e.scalar_tensor_tensor(
                out=merged[:, st * D + h * HD: st * D + (h + 1) * HD],
                in0=pstr[:, qb * 66: qb * 66 + 64],
                scalar=rd[:, qb:qb + 1],
                in1=merged[:, st * D + h * HD: st * D + (h + 1) * HD],
                op0=ALU.mult, op1=ALU.add))(),
                reads=(trk, "rden%d" % tp2, "mg%d" % stt_),
                writes=("mg%d" % stt_,))

    pend = []
    for w in range(NW):
        if 1 <= w and w + 1 <= 15:
            emit_qkv(w + 1)
        for h in range(NH):
            t = w * NH + h
            sp(w, h, t % 3, t % 2)
            pend.append((w, h, t % 3, t % 2))
            if len(pend) > 2:
                pw, phh, p3, p2 = pend.pop(0)
                owm(pw, phh, p3, p2)
                if phh == NH - 1:
                    finalize_block(pw)
    for (pw, phh, p3, p2) in pend:
        owm(pw, phh, p3, p2)
        if phh == NH - 1:
            finalize_block(pw)
    finalize_block(15)


def build_l0attn():
    nc = bass.Bass("TRN2", target_bir_lowering=False, debug=False)
    xd = nc.dram_tensor("x", [S_SEQ, D], F32, kind="ExternalInput").ap()
    ww = {}
    for nm, shp in (("Wq", [D, D]), ("bq", [D]), ("Wk", [D, D]), ("bk", [D]),
                    ("Wv", [D, D]), ("bv", [D]), ("Wo", [D, D]), ("bo", [D])):
        ww[nm] = nc.dram_tensor(nm, shp, F32, kind="ExternalInput").ap()
    pbd = nc.dram_tensor("pos_bias", [NH, 512, 512], F32, kind="ExternalInput").ap()
    onesd = nc.dram_tensor("ones_row", [1, 128], F32, kind="ExternalInput").ap()
    identd = nc.dram_tensor("ident", [128, 128], F32, kind="ExternalInput").ap()
    yd = nc.dram_tensor("y", [S_SEQ, D], F32, kind="ExternalOutput").ap()
    epbTd = nc.dram_tensor("epbT_scr", [128, NH * 2048], BF16, kind="Internal").ap()
    with ExitStack() as octx:
        block = octx.enter_context(nc.Block())
        S = Sched(nc)
        cst = _emit_consts(S, nc, octx, onesd, identd)
        with ExitStack() as c0:
            sb, psb = _mk_scoped(nc, c0, "p_")
            emit_l0attn_prep_epb(S, sb, psb, pbd, epbTd, cst)
        S.stage_barrier()
        with ExitStack() as c1:
            sb, psb = _mk_scoped(nc, c1, "m_")
            emit_l0attn_main(S, sb, psb, xd, ww, epbTd, yd, cst)
        S.barrier_on("vector", "ydma")
        S.finalize(octx, block)
    return nc


def run_l0attn(nc, x_full, d):
    c = _consts()
    in_maps = []
    for b in range(8):
        in_maps.append({
            "x": np.ascontiguousarray(x_full[b]),
            "Wq": d["L0_attn_Wq"], "bq": d["L0_attn_bq"],
            "Wk": d["L0_attn_Wk"], "bk": d["L0_attn_bk"],
            "Wv": d["L0_attn_Wv"], "bv": d["L0_attn_bv"],
            "Wo": d["L0_attn_Wo"], "bo": d["L0_attn_bo"],
            "pos_bias": d["L0_attn_pos_bias"],
            "ones_row": c["ones_row"], "ident": c["ident"]})
    res = run_bass_kernel_spmd(nc, in_maps, core_ids=list(range(8)))
    return np.stack([r["y"] for r in res.results], axis=0)


_NCS = None


_FUSED = None


def build_fused():
    nc = bass.Bass("TRN2", target_bir_lowering=False, debug=False)
    x0d = nc.dram_tensor("x0", [S_SEQ, D], F32, kind="ExternalInput").ap()
    win = {}
    for nm, shp in (
            ("L0_attn_Wq", [D, D]), ("L0_attn_bq", [D]),
            ("L0_attn_Wk", [D, D]), ("L0_attn_bk", [D]),
            ("L0_attn_Wv", [D, D]), ("L0_attn_bv", [D]),
            ("L0_attn_Wo", [D, D]), ("L0_attn_bo", [D]),
            ("L0_attn_pos_bias", [NH, 512, 512]),
            ("L0_ffn_w_gate", [INNER, D]), ("L0_ffn_w_up", [INNER, D]),
            ("L0_ffn_w_down", [D, INNER]),
            ("L1_attn_a_logit", [D]), ("L1_attn_W_dx", [1040, D]),
            ("L1_attn_b_dx", [D]), ("L1_attn_W_x", [1040, D]),
            ("L1_attn_b_x", [D]), ("L1_attn_W_gx", [D, D]),
            ("L1_attn_b_gx", [D]), ("L1_attn_W_ga", [D, D]),
            ("L1_attn_b_ga", [D]), ("L1_attn_W_o", [D, D]),
            ("L1_attn_b_o", [D]), ("L1_ffn_weight", [D, D]),
            ("L1_ffn_bias", [D])):
        win[nm] = nc.dram_tensor(nm, shp, F32, kind="ExternalInput").ap()
    petd = nc.dram_tensor("peT", [16, S_SEQ], F32, kind="ExternalInput").ap()
    onesd = nc.dram_tensor("ones_row", [1, 128], F32, kind="ExternalInput").ap()
    identd = nc.dram_tensor("ident", [128, 128], F32, kind="ExternalInput").ap()
    yd = nc.dram_tensor("y", [S_SEQ, D], F32, kind="ExternalOutput").ap()
    xs = [nc.dram_tensor("xs%d" % i, [S_SEQ, D], F32, kind="Internal").ap()
          for i in range(5)]
    gTd = nc.dram_tensor("gT_scr", [128, 8 * INNER], BF16, kind="Internal").ap()
    uTd = nc.dram_tensor("uT_scr", [128, 8 * INNER], BF16, kind="Internal").ap()
    dTd = nc.dram_tensor("dT_scr", [128, 32 * D], BF16, kind="Internal").ap()
    wsbd = nc.dram_tensor("wsb_scr", [128, 3], F32, kind="Internal").ap()
    w8Td = nc.dram_tensor("w8T_scr", [128, 8 * D], BF16, kind="Internal").ap()
    epbTd = nc.dram_tensor("epbT_scr", [128, NH * 2048], BF16, kind="Internal").ap()

    wwB = {"Wq": win["L0_attn_Wq"], "bq": win["L0_attn_bq"],
           "Wk": win["L0_attn_Wk"], "bk": win["L0_attn_bk"],
           "Wv": win["L0_attn_Wv"], "bv": win["L0_attn_bv"],
           "Wo": win["L0_attn_Wo"], "bo": win["L0_attn_bo"]}
    wwD = {"a_logit": win["L1_attn_a_logit"], "W_dx": win["L1_attn_W_dx"],
           "b_dx": win["L1_attn_b_dx"], "W_x": win["L1_attn_W_x"],
           "b_x": win["L1_attn_b_x"], "W_gx": win["L1_attn_W_gx"],
           "b_gx": win["L1_attn_b_gx"], "W_ga": win["L1_attn_W_ga"],
           "b_ga": win["L1_attn_b_ga"], "W_o": win["L1_attn_W_o"],
           "b_o": win["L1_attn_b_o"]}

    with ExitStack() as octx:
        block = octx.enter_context(nc.Block())
        S = Sched(nc)
        cst = _emit_consts(S, nc, octx, onesd, identd)
        with ExitStack() as c0:
            sb, psb = _mk_scoped(nc, c0, "pA_")
            emit_l0ffn_prep(S, sb, psb, win["L0_ffn_w_gate"], win["L0_ffn_w_up"],
                            win["L0_ffn_w_down"], gTd, uTd, dTd, wsbd, cst)
            sb, psb = _mk_scoped(nc, c0, "pC_")
            emit_l1ffn_prep(S, sb, psb, win["L1_ffn_weight"], w8Td, cst)
            sb, psb = _mk_scoped(nc, c0, "pB_")
            emit_l0attn_prep_epb(S, sb, psb, win["L0_attn_pos_bias"], epbTd, cst)
        S.stage_barrier()
        with ExitStack() as c1:
            sb, psb = _mk_scoped(nc, c1, "a1_")
            emit_l0ffn_main(S, sb, psb, x0d, xs[0], gTd, uTd, dTd, wsbd, cst)
        S.stage_barrier()
        with ExitStack() as c2:
            sb, psb = _mk_scoped(nc, c2, "b_")
            emit_l0attn_main(S, sb, psb, xs[0], wwB, epbTd, xs[1], cst)
        S.stage_barrier()
        with ExitStack() as c3:
            sb, psb = _mk_scoped(nc, c3, "a2_")
            emit_l0ffn_main(S, sb, psb, xs[1], xs[2], gTd, uTd, dTd, wsbd, cst)
        S.stage_barrier()
        with ExitStack() as c4:
            sb, psb = _mk_scoped(nc, c4, "c1_")
            emit_l1ffn_main(S, sb, psb, xs[2], win["L1_ffn_bias"], w8Td, xs[3], cst)
        S.stage_barrier()
        with ExitStack() as c5:
            sb, psb = _mk_scoped(nc, c5, "d_")
            emit_l1attn(S, sb, psb, xs[3], wwD, petd, xs[4], cst)
        S.stage_barrier()
        with ExitStack() as c6:
            sb, psb = _mk_scoped(nc, c6, "c2_")
            emit_l1ffn_main(S, sb, psb, xs[4], win["L1_ffn_bias"], w8Td, yd, cst)
        S.barrier_on("vector", "ydma")
        S.finalize(octx, block)
    return nc


def kernel(**inputs):
    global _FUSED
    d = inputs
    if _FUSED is None:
        _FUSED = build_fused()
    nc = _FUSED
    c = _consts()
    pet = _pe_table()
    names = ["L0_attn_Wq", "L0_attn_bq", "L0_attn_Wk", "L0_attn_bk",
             "L0_attn_Wv", "L0_attn_bv", "L0_attn_Wo", "L0_attn_bo",
             "L0_attn_pos_bias", "L0_ffn_w_gate", "L0_ffn_w_up",
             "L0_ffn_w_down", "L1_attn_a_logit", "L1_attn_W_dx",
             "L1_attn_b_dx", "L1_attn_W_x", "L1_attn_b_x", "L1_attn_W_gx",
             "L1_attn_b_gx", "L1_attn_W_ga", "L1_attn_b_ga", "L1_attn_W_o",
             "L1_attn_b_o", "L1_ffn_weight", "L1_ffn_bias"]
    wmap = {nm: np.ascontiguousarray(np.asarray(d[nm], dtype=np.float32))
            for nm in names}
    x_full = np.asarray(d["L0_attn_hidden_states"], dtype=np.float32)
    in_maps = []
    for b in range(8):
        m = {"x0": np.ascontiguousarray(x_full[b]), "peT": pet,
             "ones_row": c["ones_row"], "ident": c["ident"]}
        m.update(wmap)
        in_maps.append(m)
    res = run_bass_kernel_spmd(nc, in_maps, core_ids=list(range(8)))
    return np.stack([r["y"] for r in res.results], axis=0)
